# version 01
# baseline (speedup 1.0000x reference)
import math
import contextlib
import numpy as np
import concourse.bass as bass
import concourse.mybir as mybir
from concourse.bass_utils import run_bass_kernel_spmd

F32 = mybir.dt.float32
BF16 = mybir.dt.bfloat16
I32 = mybir.dt.int32
U8 = mybir.dt.uint8
AF = mybir.ActivationFunctionType
ALU = mybir.AluOpType
AX = mybir.AxisListType

NCORES = 8
D = 4096
S = 4096
B = 2
T = B * S
NH = 32
HD = 128
HPC = NH // NCORES
TPC = T // NCORES
CLIP = 8.0
LN_EPS = 1e-5
SC0 = 1.0 / math.sqrt(HD)
NEGBIG = -1.0e9

QL, KVL, ROPE, NOPE, QH, VH = 1536, 512, 64, 128, 192, 128
SC1 = 1.0 / math.sqrt(QH)
INTER = 11008
GRP = 128
NG = INTER // GRP
NGC = 11
NPC = NGC * GRP
NE = 3277
EI_NEG = -4.0


def _alibi_slopes():
    m = np.arange(1, NH + 1, dtype=np.float32) * (8.0 / NH)
    return (1.0 / np.power(2.0, m)).astype(np.float32)


class Sem:
    def __init__(self, handle):
        self.h = handle
        self.n = 0

    def inc(self, instr, k=1):
        self.n += k
        instr.then_inc(self.h, k)


def make_sems(ctx, nc, names):
    return {n: Sem(ctx.enter_context(nc.semaphore(n))) for n in names}


def edge(eng, sem, instr):
    sem.n += 1
    instr.then_inc(sem.h, 1)
    eng.wait_ge(sem.h, sem.n)


def _nchunks(total, w=512):
    out = []
    o = 0
    while o < total:
        cw = min(w, total - o)
        out.append((o, cw))
        o += cw
    return out



RACE_DETECT = [False]


def _bass():
    return bass.Bass("TRN2", target_bir_lowering=False, debug=False,
                     detect_race_conditions=RACE_DETECT[0])


def build_A(TC=16, MATS=("g", "u", "d"), KTO=None):
    nc = _bass()
    EC = 12
    TW = 512

    xfm = nc.dram_tensor("xfm", [D, T], F32, kind="ExternalInput").ap()
    wq = nc.dram_tensor("wq", [D, EC * 128], F32, kind="ExternalInput").ap()
    qwg = nc.dram_tensor("qwg", [D // 8, NPC], I32, kind="ExternalInput").ap()
    scg = nc.dram_tensor("scg", [32, NPC], F32, kind="ExternalInput").ap()
    qzg = nc.dram_tensor("qzg", [32, NPC // 8], I32, kind="ExternalInput").ap()
    gig = nc.dram_tensor("gig", [32, D], I32, kind="ExternalInput").ap()
    qwu = nc.dram_tensor("qwu", [D // 8, NPC], I32, kind="ExternalInput").ap()
    scu = nc.dram_tensor("scu", [32, NPC], F32, kind="ExternalInput").ap()
    qzu = nc.dram_tensor("qzu", [32, NPC // 8], I32, kind="ExternalInput").ap()
    qwd = nc.dram_tensor("qwd", [NPC // 8, D], I32, kind="ExternalInput").ap()
    scd = nc.dram_tensor("scd", [NGC, D], F32, kind="ExternalInput").ap()
    qzd = nc.dram_tensor("qzd", [NGC, D // 8], I32, kind="ExternalInput").ap()
    gid = nc.dram_tensor("gid", [NGC, NPC], I32, kind="ExternalInput").ap()
    gidv = nc.dram_tensor("gidv", [NGC, 1], F32, kind="ExternalInput").ap()
    shifts = nc.dram_tensor("shifts", [128, 1], I32, kind="ExternalInput").ap()

    qkvc = nc.dram_tensor("qkvc", [EC * 128, T], F32, kind="ExternalOutput").ap()
    stats = nc.dram_tensor("stats", [4, T], F32, kind="ExternalOutput").ap()
    dqg = nc.dram_tensor("dqg", [D, NPC], BF16, kind="ExternalOutput").ap()
    dqu = nc.dram_tensor("dqu", [D, NPC], BF16, kind="ExternalOutput").ap()
    dqd = nc.dram_tensor("dqd", [NPC, D], BF16, kind="ExternalOutput").ap()

    DQT = []
    KT = {"g": 32, "u": 32, "d": NGC}
    if KTO:
        KT.update(KTO)
    NCOLS = {"g": NPC, "u": NPC, "d": D}
    for mat in MATS:
        for kt in range(KT[mat]):
            for (c0, cw) in _nchunks(NCOLS[mat]):
                DQT.append((mat, kt, c0, cw))
    DQK = []
    KT_END = []
    cnt = 0
    for mat in MATS:
        nchk = len(_nchunks(NCOLS[mat]))
        for kt in range(KT[mat]):
            DQK.append((mat, kt))
            cnt += nchk
            KT_END.append(cnt)
    TILES_BEFORE = {}
    MATIDX = {}
    cnt = 0
    for mi, mat in enumerate(MATS):
        TILES_BEFORE[mat] = cnt
        MATIDX[mat] = mi
        cnt += KT[mat] * len(_nchunks(NCOLS[mat]))

    ctx = contextlib.ExitStack()
    with ctx:
        wsb = ctx.enter_context(nc.sbuf_tensor("wsb", [128, 32 * EC * 128], BF16))
        xsb = [ctx.enter_context(nc.sbuf_tensor(f"xsb{i}", [128, 16 * TW], BF16)) for i in range(3)]
        qsb = [ctx.enter_context(nc.sbuf_tensor(f"qsb{i}", [128, TW], F32)) for i in range(4)]
        sqsb = [ctx.enter_context(nc.sbuf_tensor(f"sqsb{i}", [128, TW], F32)) for i in range(2)]
        sel = ctx.enter_context(nc.sbuf_tensor("sel", [128, 16], F32))
        stsb = [ctx.enter_context(nc.sbuf_tensor(f"stsb{i}", [4, TW], F32)) for i in range(2)]
        qbc = [ctx.enter_context(nc.sbuf_tensor(f"qbc{i}", [128, 512], I32)) for i in range(2)]
        w4i = ctx.enter_context(nc.sbuf_tensor("w4i", [128, 512], I32))
        w4f = ctx.enter_context(nc.sbuf_tensor("w4f", [128, 512], F32))
        wt1 = ctx.enter_context(nc.sbuf_tensor("wt1", [128, 512], F32))
        wdq = [ctx.enter_context(nc.sbuf_tensor(f"wdq{i}", [128, 512], BF16)) for i in range(2)]
        shsb = ctx.enter_context(nc.sbuf_tensor("shsb", [128, 1], I32))
        scsb = ctx.enter_context(nc.sbuf_tensor("scsb", [32, D], BF16))
        zssb = ctx.enter_context(nc.sbuf_tensor("zssb", [32, D], BF16))
        qzsb = ctx.enter_context(nc.sbuf_tensor("qzsb", [32, 512], I32))
        z4sb = ctx.enter_context(nc.sbuf_tensor("z4sb", [32, 512], I32))
        gisb = [ctx.enter_context(nc.sbuf_tensor(f"gisb{i}", [32, 128], I32)) for i in range(2)]
        gtf = [ctx.enter_context(nc.sbuf_tensor(f"gtf{i}", [32, 128], BF16)) for i in range(2)]
        iog = ctx.enter_context(nc.sbuf_tensor("iog", [32, 1], F32))
        givsb = ctx.enter_context(nc.sbuf_tensor("givsb", [NGC, 1], F32))

        ps_qkv = [ctx.enter_context(nc.psum_tensor(f"psq{i}", [128, TW], F32)) for i in range(2)]
        ps_st = ctx.enter_context(nc.psum_tensor("psst", [4, TW], F32))
        ps_s = [ctx.enter_context(nc.psum_tensor(f"pss{i}", [128, 512], F32)) for i in range(2)]
        ps_z = [ctx.enter_context(nc.psum_tensor(f"psz{i}", [128, 512], F32)) for i in range(2)]

        s = make_sems(ctx, nc, [
            "ldw", "mmq", "evq", "sq", "smm", "evst",
            "mlds", "mldz", "zsp", "dgt",
            "dmm", "dvw", "dst", "ini", "dlds", "vch",
            "ldx0", "ldx1", "ldx2", "dld0", "dld1", "gld0", "gld1",
            "stq0", "stq1", "stq2", "stq3", "sts0", "sts1", "std0", "std1"])
        LDX = [s["ldx0"], s["ldx1"], s["ldx2"]]
        DLD = [s["dld0"], s["dld1"]]
        GLD = [s["gld0"], s["gld1"]]
        STQ = [s["stq0"], s["stq1"], s["stq2"], s["stq3"]]
        STS = [s["sts0"], s["sts1"]]
        STD = [s["std0"], s["std1"]]

        LDXC = []
        DLDC = []
        GLDC = []
        STQC = []
        STSC = []
        STDC = []

        blk = ctx.enter_context(nc.Block())

        @blk.gpsimd
        def _(g):
            s["ini"].inc(g.iota(iog[:], pattern=[[1, 1]], base=0,
                                channel_multiplier=1,
                                allow_small_or_imprecise_dtypes=True))
            for i in range(4):
                src = wq.rearrange("(c p) e -> p c e", p=128)[:, 8 * i:8 * (i + 1), :]
                s["ldw"].inc(g.dma_start(
                    out=wsb[:, (8 * i) * EC * 128:(8 * i + 8) * EC * 128],
                    in_=src), 16)
            s["mlds"].inc(g.dma_start(out=scsb[0:32, 0:NPC], in_=scg[:, :]), 16)
            for j in range(2 * TC):
                t, h = j // 2, j % 2
                if j >= 3:
                    g.wait_ge(s["mmq"].h, EC * ((j - 3) // 2 + 1))
                src = xfm.rearrange("(c p) t -> p c t", p=128)[
                    :, 16 * h:16 * (h + 1), t * TW:(t + 1) * TW]
                LDX[j % 3].inc(g.dma_start(out=xsb[j % 3][:], in_=src), 16)
                LDXC.append(LDX[j % 3].n)
            for di, (mat, kt, c0, cw) in enumerate(DQT):
                if kt == 0 and c0 == 0 and MATIDX[mat] > 0:
                    nb = TILES_BEFORE[mat]
                    g.wait_ge(s["dmm"].h, 2 * nb)
                    g.wait_ge(s["zsp"].h, MATIDX[mat])
                    ng = 32 if mat != "d" else NGC
                    sc_src = {"u": scu, "d": scd}[mat]
                    s["mlds"].inc(g.dma_start(out=scsb[0:ng, 0:NCOLS[mat]],
                                              in_=sc_src[:, :]), 16)
                qw = {"g": qwg, "u": qwu, "d": qwd}[mat]
                if di >= 2:
                    g.wait_ge(s["dvw"].h, di - 1)
                src = qw[16 * kt:16 * (kt + 1), c0:c0 + cw]
                src = src.rearrange("m (o n) -> m o n", o=1).broadcast_to([16, 8, cw])
                DLD[di % 2].inc(g.dma_start(out=qbc[di % 2][:, 0:cw], in_=src), 16)
                DLDC.append(DLD[di % 2].n)

        @blk.sync
        def _(sy):
            qz_src = {"g": qzg, "u": qzu, "d": qzd}
            sy.dma_start(out=shsb[:], in_=shifts[:, :]).then_inc(s["dlds"].h, 16)
            sy.dma_start(out=givsb[:], in_=gidv[:, :]).then_inc(s["dlds"].h, 16)
            s["dlds"].n += 32
            m0 = MATS[0]
            ng0 = 32 if m0 != "d" else NGC
            s["mldz"].inc(sy.dma_start(out=qzsb[0:ng0, 0:NCOLS[m0] // 8],
                                       in_=qz_src[m0][:, :]), 16)
            for t in range(TC):
                for e in range(EC):
                    i0 = t * EC + e
                    sy.wait_ge(s["evq"].h, i0 + 1)
                    STQ[i0 % 4].inc(
                        sy.dma_start(out=qkvc[e * 128:(e + 1) * 128, t * TW:(t + 1) * TW],
                                     in_=qsb[i0 % 4][:]), 16)
                    STQC.append(STQ[i0 % 4].n)
                sy.wait_ge(s["evst"].h, t + 1)
                STS[t % 2].inc(
                    sy.dma_start(out=stats[:, t * TW:(t + 1) * TW],
                                 in_=stsb[t % 2][:]), 16)
                STSC.append(STS[t % 2].n)
            nkt0 = 0
            for di, (mat, kt, c0, cw) in enumerate(DQT):
                if kt == 0 and c0 == 0 and MATIDX[mat] > 0:
                    sy.wait_ge(s["zsp"].h, MATIDX[mat])
                    ng = 32 if mat != "d" else NGC
                    s["mldz"].inc(sy.dma_start(out=qzsb[0:ng, 0:NCOLS[mat] // 8],
                                               in_=qz_src[mat][:, :]), 16)
                if c0 == 0:
                    if nkt0 >= 2:
                        sy.wait_ge(s["dgt"].h, nkt0 - 1)
                    if mat != "d":
                        src = gig[0:32, kt * 128:(kt + 1) * 128]
                        dst = gisb[nkt0 % 2][0:32, :]
                    else:
                        src = gid[0:NGC, kt * 128:(kt + 1) * 128]
                        dst = gisb[nkt0 % 2][0:NGC, :]
                    GLD[nkt0 % 2].inc(sy.dma_start(out=dst, in_=src), 16)
                    GLDC.append(GLD[nkt0 % 2].n)
                    nkt0 += 1
                dq = {"g": dqg, "u": dqu, "d": dqd}[mat]
                sy.wait_ge(s["dst"].h, di + 1)
                STD[di % 2].inc(
                    sy.dma_start(out=dq[128 * kt:128 * (kt + 1), c0:c0 + cw],
                                 in_=wdq[di % 2][:, 0:cw]), 16)
                STDC.append(STD[di % 2].n)
            for ss in (STQ + STS + STD):
                sy.wait_ge(ss.h, ss.n)

        @blk.scalar
        def _(sc):
            for t in range(TC):
                for e in range(8):
                    i0 = t * EC + e
                    sc.wait_ge(s["evq"].h, i0 + 1)
                    if t * 8 + e >= 2:
                        sc.wait_ge(s["smm"].h, t * 8 + e - 1)
                    s["sq"].inc(sc.activation(sqsb[e % 2][:], qsb[i0 % 4][:], AF.Square))

        @blk.vector
        def _(v):
            edge(v, s["vch"], v.memset(sel[:], 0.0))
            for j in range(4):
                v.memset(sel[:, 4 * j + j:4 * j + j + 1], 1.0)
            edge(v, s["vch"], v.engine_nop())
            for t in range(TC):
                for e in range(EC):
                    i0 = t * EC + e
                    v.wait_ge(s["mmq"].h, i0 + 1)
                    if i0 >= 4:
                        v.wait_ge(STQ[i0 % 4].h, STQC[i0 - 4])
                        tprev = (i0 - 4) // EC
                        eprev = (i0 - 4) % EC
                        if eprev < 8:
                            v.wait_ge(s["smm"].h, 8 * tprev + eprev + 1)
                    s["evq"].inc(v.tensor_scalar(qsb[i0 % 4][:], ps_qkv[e % 2][:],
                                                 -CLIP, CLIP, ALU.max, ALU.min))
                v.wait_ge(s["smm"].h, 8 * (t + 1))
                if t >= 2:
                    v.wait_ge(STS[t % 2].h, STSC[t - 2])
                s["evst"].inc(v.tensor_copy(stsb[t % 2][:], ps_st[:]))
            nkt = 0
            for di, (mat, kt, c0, cw) in enumerate(DQT):
                ng = 32 if mat != "d" else NGC
                ncols = NCOLS[mat]
                if c0 == 0 and kt == 0:
                    mi = MATIDX[mat]
                    v.wait_ge(s["mlds"].h, 16 * (mi + 1))
                    v.wait_ge(s["mldz"].h, 16 * (mi + 1))
                    for (zc0, zcw) in _nchunks(ncols):
                        for r in range(8):
                            ii = v.tensor_scalar(
                                z4sb[0:ng, 0:zcw].rearrange("g (n o) -> g n o", o=8)[:, :, r],
                                qzsb[0:ng, zc0 // 8:(zc0 + zcw) // 8], 4 * r, 15,
                                ALU.logical_shift_right, ALU.bitwise_and)
                        edge(v, s["vch"], ii)
                        edge(v, s["vch"], v.tensor_copy(zssb[0:ng, zc0:zc0 + zcw],
                                                        z4sb[0:ng, 0:zcw]))
                        edge(v, s["vch"], v.tensor_scalar(zssb[0:ng, zc0:zc0 + zcw],
                                        zssb[0:ng, zc0:zc0 + zcw], 1.0, None, ALU.add))
                        last = v.tensor_tensor(zssb[0:ng, zc0:zc0 + zcw],
                                               zssb[0:ng, zc0:zc0 + zcw],
                                               scsb[0:ng, zc0:zc0 + zcw], ALU.mult)
                    s["zsp"].inc(last)
                if c0 == 0:
                    nkt += 1
                    ki = nkt - 1
                    if nkt >= 3:
                        v.wait_ge(s["dmm"].h, 2 * KT_END[nkt - 3])
                    v.wait_ge(GLD[ki % 2].h, GLDC[ki])
                    if mat != "d":
                        v.wait_ge(s["ini"].h, 1)
                        s["dgt"].inc(v.tensor_scalar(gtf[ki % 2][0:32, :],
                                                     gisb[ki % 2][0:32, :],
                                                     iog[0:32, 0:1], None, ALU.is_equal))
                    else:
                        s["dgt"].inc(v.tensor_scalar(gtf[ki % 2][0:NGC, :],
                                                     gisb[ki % 2][0:NGC, :],
                                                     givsb[0:NGC, 0:1], None, ALU.is_equal))
                if di == 0:
                    v.wait_ge(s["dlds"].h, 32)
                v.wait_ge(DLD[di % 2].h, DLDC[di])
                ii = v.tensor_tensor(w4i[:, 0:cw], qbc[di % 2][:, 0:cw],
                                     shsb[:, 0:1].broadcast_to([128, cw]),
                                     ALU.logical_shift_right)
                s["dvw"].inc(ii)
                v.wait_ge(s["dvw"].h, s["dvw"].n)
                edge(v, s["vch"], v.tensor_scalar(w4i[:, 0:cw], w4i[:, 0:cw],
                                                  15, None, ALU.bitwise_and))
                edge(v, s["vch"], v.tensor_copy(w4f[:, 0:cw], w4i[:, 0:cw]))
                v.wait_ge(s["dmm"].h, 2 * (di + 1))
                if di >= 2:
                    v.wait_ge(STD[di % 2].h, STDC[di - 2])
                edge(v, s["vch"], v.tensor_tensor(wt1[:, 0:cw], w4f[:, 0:cw],
                                                  ps_s[di % 2][:, 0:cw], ALU.mult))
                s["dst"].inc(v.tensor_tensor(wdq[di % 2][:, 0:cw], wt1[:, 0:cw],
                                             ps_z[di % 2][:, 0:cw], ALU.subtract))

        @blk.tensor
        def _(te):
            te.wait_ge(s["ldw"].h, 64)
            for t in range(TC):
                for e in range(EC):
                    i0 = t * EC + e
                    if i0 >= 2:
                        te.wait_ge(s["evq"].h, i0 - 1)
                    for d in range(32):
                        if d == 0:
                            te.wait_ge(LDX[(2 * t) % 3].h, LDXC[2 * t])
                        if d == 16:
                            te.wait_ge(LDX[(2 * t + 1) % 3].h, LDXC[2 * t + 1])
                        xbuf = xsb[(2 * t + d // 16) % 3]
                        mm = te.matmul(ps_qkv[e % 2][:],
                                       wsb[:, (d * EC + e) * 128:(d * EC + e + 1) * 128],
                                       xbuf[:, (d % 16) * TW:(d % 16 + 1) * TW],
                                       start=(d == 0), stop=(d == 31))
                        if d == 31:
                            s["mmq"].inc(mm)
                    if 1 <= e <= 8:
                        es = e - 1
                        if es == 0 and t >= 1:
                            te.wait_ge(s["evst"].h, t)
                        te.wait_ge(s["evq"].h, t * EC + es + 1)
                        selr = sel[:, 0:4] if es < 4 else sel[:, 8:12]
                        te.matmul(ps_st[:], selr, qsb[(t * EC + es) % 4][:],
                                  start=(es == 0), stop=False)
                        te.wait_ge(s["sq"].h, t * 8 + es + 1)
                        sels = sel[:, 4:8] if es < 4 else sel[:, 12:16]
                        mm = te.matmul(ps_st[:], sels, sqsb[es % 2][:],
                                       start=False, stop=(es == 7))
                        s["smm"].inc(mm)
            nkt = 0
            for di, (mat, kt, c0, cw) in enumerate(DQT):
                ng = 32 if mat != "d" else NGC
                if c0 == 0:
                    nkt += 1
                ki = nkt - 1
                te.wait_ge(s["dgt"].h, nkt)
                te.wait_ge(s["zsp"].h, MATIDX[mat] + 1)
                if di >= 2:
                    te.wait_ge(s["dst"].h, di - 1)
                te.matmul(ps_s[di % 2][:, 0:cw], gtf[ki % 2][0:ng, :],
                          scsb[0:ng, c0:c0 + cw], start=True, stop=True)
                mm = te.matmul(ps_z[di % 2][:, 0:cw], gtf[ki % 2][0:ng, :],
                               zssb[0:ng, c0:c0 + cw], start=True, stop=True)
                s["dmm"].inc(mm, 2)

    return nc



class Sched:
    ENGS = ("g", "sy", "te", "v", "sc")

    def __init__(self):
        self.ops = {e: [] for e in self.ENGS}

    def w(self, eng, sem, val):
        if val > 0:
            self.ops[eng].append(lambda e, h=sem.h, v=val: e.wait_ge(h, v))

    def i(self, eng, fn, incs=()):
        vals = []
        for sem, k in incs:
            sem.n += k
            vals.append((sem.h, k))

        def run(e, fn=fn, vals=vals):
            ins = fn(e)
            for h, k in vals:
                ins.then_inc(h, k)
        self.ops[eng].append(run)
        return tuple(sem.n for sem, _ in incs)

    def edge(self, eng, fn, sem):
        self.i(eng, fn, [(sem, 1)])
        self.w(eng, sem, sem.n)

    def replay(self, eng, e):
        for f in self.ops[eng]:
            f(e)



def build_B(SD=S):
    nc = _bass()
    TB = B * SD
    NQT = SD // 128
    TCB = TB // 512

    qkvc = nc.dram_tensor("qkvc", [1536, TB], F32, kind="ExternalInput").ap()
    stall = nc.dram_tensor("stall", [32, TB], F32, kind="ExternalInput").ap()
    stsel = nc.dram_tensor("stsel", [32, 4], F32, kind="ExternalInput").ap()
    bsel = nc.dram_tensor("bsel", [4, 4 * 128], F32, kind="ExternalInput").ap()
    lnwb = nc.dram_tensor("lnwb", [128, 16], F32, kind="ExternalInput").ap()
    slb = nc.dram_tensor("slb", [128, 4], F32, kind="ExternalInput").ap()
    slc = nc.dram_tensor("slc", [128, 4], F32, kind="ExternalInput").ap()
    oo = nc.dram_tensor("oo", [8, SD, HD], F32, kind="ExternalOutput").ap()

    ctx = contextlib.ExitStack()
    with ctx:
        sb = lambda *a: ctx.enter_context(nc.sbuf_tensor(*a))
        pt = lambda *a: ctx.enter_context(nc.psum_tensor(*a))
        stall_sb = [sb(f"stalls{i}", [32, 512], F32) for i in range(2)]
        st4 = sb("st4", [4, 512], F32)
        lnv = sb("lnv", [128, 16], F32)
        lnt = sb("lnt", [128, 16], F32)
        mut = sb("mut", [128, 8], F32)
        vat = sb("vat", [128, 8], F32)
        lnrow = sb("lnrow", [4, TB], F32)
        bselsb = sb("bselsb", [4, 4 * 128], F32)
        stselsb = sb("stselsb", [32, 4], F32)
        bc = [sb(f"bc{i}", [128, SD], F32) for i in range(4)]
        qf = sb("qf", [128, SD], F32)
        tmp1 = sb("tmp1", [128, SD], F32)
        qln = sb("qln", [128, SD], BF16)
        kln = sb("kln", [128, SD], BF16)
        vbf = sb("vbf", [128, SD], BF16)
        vtm = sb("vtm", [128, SD], BF16)
        lnwsb = sb("lnwsb", [128, 16], F32)
        slbsb = sb("slbsb", [128, 4], F32)
        slcsb = sb("slcsb", [128, 4], F32)
        tio = sb("tio", [128, 512], F32)
        mk4 = sb("mk4", [128, 4 * 512], F32)
        ah = sb("ah", [128, 512], F32)
        dv4 = sb("dv4", [128, 4 * 512], F32)
        idf = sb("idf", [128, 128], F32)
        idb = sb("idb", [128, 128], BF16)
        psb = sb("psb", [128, 512], BF16)
        ptsb = sb("ptsb", [128, 512], BF16)
        osb = sb("osb", [128, HD], F32)
        osacc = sb("osacc", [128, HD], F32)
        mrun = sb("mrun", [128, 1], F32)
        mnew = sb("mnew", [128, 1], F32)
        mtile = sb("mtile", [128, 1], F32)
        psmax = sb("psmax", [128, 1], F32)
        offt = sb("offt", [128, 1], F32)
        biast = sb("biast", [128, 1], F32)
        corr = sb("corr", [128, 1], F32)
        mdif = sb("mdif", [128, 1], F32)
        tsum = sb("tsum", [128, 1], F32)
        dnr = sb("dnr", [128, 1], F32)
        rcp = sb("rcp", [128, 1], F32)
        epst = sb("epst", [128, 1], F32)

        ps_s = [pt(f"pscs{i}", [128, 512], F32) for i in range(2)]
        ps_pt = pt("pspt", [128, 512], BF16)
        ps_o = pt("pso", [128, HD], F32)
        ps_bc = pt("psbc", [128, 512], F32)
        ps_su = pt("pssu", [4, 512], F32)
        ps_t1 = pt("pst1", [128, 16], F32)
        ps_t2 = pt("pst2", [4, 512], F32)

        s = make_sems(ctx, nc, [
            "lds", "ld0", "ld1", "ldq", "ldv", "summ", "sue", "tt1", "lnve", "lnm",
            "ttb", "lnre", "bcm", "bce", "lnap", "vgp", "scm", "sadd",
            "smax", "sexp", "scorr", "sdn", "strp", "spte", "sres", "sav",
            "soev", "ost", "vch", "sch", "gch", "idr"])

        S_ = Sched()

        for name, dst, src in (("stsel", stselsb, stsel), ("bsel", bselsb, bsel),
                               ("lnwb", lnwsb, lnwb), ("slb", slbsb, slb),
                               ("slc", slcsb, slc)):
            S_.i("sy", lambda e, d=dst, so=src: e.dma_start(out=d[:], in_=so[:, :]),
                 [(s["lds"], 16)])
        S_.i("g", lambda e: e.iota(idf[:], pattern=[[1, 128]], base=0,
                                   channel_multiplier=-1,
                                   allow_small_or_imprecise_dtypes=True),
             [(s["gch"], 1)])
        S_.i("g", lambda e: e.iota(tio[:], pattern=[[1, 512]], base=0,
                                   channel_multiplier=-1,
                                   allow_small_or_imprecise_dtypes=True),
             [(s["gch"], 1)])
        S_.w("v", s["gch"], 2)
        for vv in range(4):
            S_.edge("v", lambda e, vv=vv: e.tensor_scalar(
                mk4[:, 512 * vv:512 * (vv + 1)], tio[:], float(-128 * vv), None,
                ALU.add), s["vch"])
            S_.edge("v", lambda e, vv=vv: e.tensor_scalar(
                mk4[:, 512 * vv:512 * (vv + 1)], mk4[:, 512 * vv:512 * (vv + 1)],
                0.0, NEGBIG, ALU.is_gt, ALU.mult), s["vch"])
        S_.edge("v", lambda e: e.memset(epst[:], LN_EPS), s["vch"])
        S_.edge("v", lambda e: e.tensor_scalar(idf[:], idf[:], 0.0, None,
                                               ALU.is_equal), s["vch"])
        S_.i("v", lambda e: e.tensor_copy(idb[:], idf[:]), [(s["idr"], 1)])

        for t in range(TCB):
            S_.w("sy", s["summ"], t - 1)
            S_.i("sy", lambda e, t=t: e.dma_start(
                out=stall_sb[t % 2][:], in_=stall[:, 512 * t:512 * (t + 1)]),
                [(s["ld%d" % (t % 2)], 16)])
            S_.w("te", s["lds"], 80)
            S_.w("te", s["ld%d" % (t % 2)], s["ld%d" % (t % 2)].n)
            S_.w("te", s["sue"], t)
            S_.i("te", lambda e, t=t: e.matmul(ps_su[:], stselsb[:, :],
                                               stall_sb[t % 2][:],
                                               start=True, stop=True),
                 [(s["summ"], 1)])
            S_.w("v", s["summ"], t + 1)
            S_.w("v", s["tt1"], t)
            S_.i("v", lambda e: e.tensor_copy(st4[:], ps_su[:]), [(s["sue"], 1)])
            S_.w("te", s["sue"], t + 1)
            S_.w("te", s["idr"], 1)
            S_.w("te", s["lnve"], t)
            for j in range(4):
                S_.i("te", lambda e, j=j: e.matmul(
                    ps_t1[:, 4 * j:4 * (j + 1)], st4[:, 128 * j:128 * (j + 1)],
                    idf[0:4, 0:4], is_transpose=True,
                    start=(j == 0), stop=(j == 3)),
                    [(s["tt1"], 1)] if j == 3 else ())
            S_.w("v", s["tt1"], t + 1)
            S_.w("v", s["ttb"], t)
            S_.i("v", lambda e: e.tensor_copy(lnv[:], ps_t1[:]), [(s["lnve"], 1)])
            S_.w("v", s["lnve"], s["lnve"].n)
            S_.edge("v", lambda e: e.tensor_scalar(mut[:], lnv[:, :].rearrange(
                "p (j s) -> p j s", s=2)[:, :, 0], 1.0 / D, None, ALU.mult), s["vch"])
            S_.edge("v", lambda e: e.tensor_scalar(vat[:], lnv[:, :].rearrange(
                "p (j s) -> p j s", s=2)[:, :, 1], 1.0 / D, None, ALU.mult), s["vch"])
            S_.edge("v", lambda e: e.tensor_tensor(
                lnv[:, 0:8], mut[:], mut[:], ALU.mult), s["vch"])
            S_.i("v", lambda e: e.tensor_tensor(
                vat[:], vat[:], lnv[:, 0:8], ALU.subtract), [(s["lnm"], 1)])
            S_.w("sc", s["lnm"], s["lnm"].n)
            S_.edge("sc", lambda e: e.activation(vat[:], vat[:], AF.Ln,
                                                 bias=epst[:, 0:1]), s["sch"])
            S_.i("sc", lambda e: e.activation(
                lnt[:, :].rearrange("p (j s) -> p j s", s=2)[:, :, 0], vat[:],
                AF.Exp, scale=-0.5), [(s["lnm"], 1)])
            S_.w("v", s["lnm"], s["lnm"].n)
            S_.i("v", lambda e: e.tensor_tensor(
                lnt[:, :].rearrange("p (j s) -> p j s", s=2)[:, :, 1], mut[:],
                lnt[:, :].rearrange("p (j s) -> p j s", s=2)[:, :, 0],
                ALU.mult), [(s["lnm"], 1)])
            S_.w("te", s["lnm"], s["lnm"].n)
            S_.w("te", s["lnre"], t)
            for j in range(4):
                S_.i("te", lambda e, j=j: e.matmul(
                    ps_t2[:, 128 * j:128 * (j + 1)], lnt[:, 4 * j:4 * (j + 1)],
                    idf[:, :], is_transpose=True, start=(j == 0), stop=(j == 3)),
                    [(s["ttb"], 1)] if j == 3 else ())
            S_.w("v", s["ttb"], t + 1)
            S_.i("v", lambda e, t=t: e.tensor_copy(lnrow[:, 512 * t:512 * (t + 1)],
                                                   ps_t2[:]), [(s["lnre"], 1)])

        nlnap = 0
        for b in range(B):
            for tch in range(SD // 512):
                gt = (SD * b + 512 * tch) // 512
                for s4 in range(4):
                    S_.w("te", s["lnre"], gt + 1)
                    S_.w("te", s["bce"], s["bcm"].n)
                    if b > 0:
                        S_.w("te", s["lnap"], nlnap)
                    S_.i("te", lambda e, gt=gt, s4=s4: e.matmul(
                        ps_bc[:], bselsb[:, 128 * s4:128 * (s4 + 1)],
                        lnrow[:, 512 * gt:512 * (gt + 1)], start=True, stop=True),
                        [(s["bcm"], 1)])
                    S_.w("v", s["bcm"], s["bcm"].n)
                    S_.i("v", lambda e, tch=tch, s4=s4: e.tensor_copy(
                        bc[s4][:, 512 * tch:512 * (tch + 1)], ps_bc[:]),
                        [(s["bce"], 1)])
            for h in range(HPC):
                bh = b * HPC + h
                for which in range(2):
                    r0 = 512 * which + 128 * h
                    S_.w("sy", s["lnap"], nlnap)
                    S_.w("sy", s["bce"], s["bce"].n)
                    S_.i("sy", lambda e, r0=r0, b=b: e.dma_start(
                        out=qf[:], in_=qkvc[r0:r0 + 128, SD * b:SD * (b + 1)]),
                        [(s["ldq"], 16)])
                    S_.w("v", s["ldq"], s["ldq"].n)
                    rb, mb = (bc[0], bc[1]) if which == 0 else (bc[2], bc[3])
                    S_.edge("v", lambda e, rb=rb: e.tensor_tensor(
                        tmp1[:], qf[:], rb[:], ALU.mult), s["vch"])
                    S_.i("v", lambda e, mb=mb: e.tensor_tensor(
                        tmp1[:], tmp1[:], mb[:], ALU.subtract), [(s["lnap"], 1)])
                    nlnap += 1
                    dst = qln if which == 0 else kln
                    wc = 4 * h + 2 * which
                    S_.w("v", s["lnap"], nlnap)
                    S_.w("v", s["scm"], s["scm"].n)
                    S_.i("v", lambda e, dst=dst, wc=wc: e.tensor_scalar(
                        dst[:], tmp1[:], lnwsb[:, wc:wc + 1],
                        lnwsb[:, wc + 1:wc + 2], ALU.mult, ALU.add),
                        [(s["lnap"], 1)])
                    nlnap += 1
                S_.w("g", s["strp"], s["strp"].n)
                S_.i("g", lambda e, h=h, b=b: e.dma_start(
                    out=vbf[:], in_=qkvc[1024 + 128 * h:1024 + 128 * (h + 1),
                                         SD * b:SD * (b + 1)]), [(s["ldv"], 16)])
                for gb in range(SD // 512):
                    S_.w("te", s["ldv"], s["ldv"].n)
                    S_.w("te", s["idr"], 1)
                    S_.w("te", s["spte"], s["spte"].n)
                    for j in range(4):
                        S_.i("te", lambda e, gb=gb, j=j: e.matmul(
                            ps_pt[:, 128 * j:128 * (j + 1)],
                            vbf[:, 512 * gb + 128 * j:512 * gb + 128 * (j + 1)],
                            idb[:], is_transpose=True,
                            start=(j == 0), stop=(j == 3)),
                            [(s["strp"], 1)] if j == 3 else ())
                    S_.w("sc", s["strp"], s["strp"].n)
                    S_.w("sc", s["sav"], s["sav"].n)
                    S_.i("sc", lambda e, gb=gb: e.copy(
                        vtm[:, 512 * gb:512 * (gb + 1)], ps_pt[:]), [(s["spte"], 1)])
                S_.edge("v", lambda e, h=h: e.tensor_scalar(
                    ah[:], tio[:], slbsb[:, h:h + 1], None, ALU.mult), s["vch"])
                for vv in range(4):
                    ii = lambda e, vv=vv: e.tensor_tensor(
                        dv4[:, 512 * vv:512 * (vv + 1)], ah[:],
                        mk4[:, 512 * vv:512 * (vv + 1)], ALU.add)
                    if vv == 3:
                        S_.edge("v", ii, s["vch"])
                    else:
                        S_.i("v", ii)
                S_.i("v", lambda e: e.engine_nop(), [(s["lnap"], 1)])
                nlnap += 1
                for qt in range(NQT):
                    S_.edge("v", lambda e: e.memset(mrun[:], -1e30), s["vch"])
                    S_.edge("v", lambda e: e.memset(dnr[:], 0.0), s["vch"])
                    nkt = qt // 4 + 1
                    for kt in range(nkt):
                        diag = (kt == qt // 4)
                        pp = ps_s[s["scm"].n % 2]
                        S_.w("te", s["lnap"], nlnap)
                        S_.w("te", s["sexp"], max(0, s["sexp"].n - 1))
                        S_.i("te", lambda e, pp=pp, qt=qt, kt=kt: e.matmul(
                            pp[:], qln[:, 128 * qt:128 * (qt + 1)],
                            kln[:, 512 * kt:512 * (kt + 1)], start=True, stop=True),
                            [(s["scm"], 1)])
                        S_.w("v", s["scm"], s["scm"].n)
                        if diag:
                            src = dv4[:, 512 * (qt % 4):512 * (qt % 4 + 1)]
                        else:
                            src = ah
                        S_.edge("v", lambda e, pp=pp, src=src: e.tensor_tensor(
                            pp[:], pp[:], src[:] if src is ah else src, ALU.add),
                            s["sadd"])
                        S_.edge("v", lambda e, pp=pp: e.reduce_max(
                            psmax[:], pp[:], axis=AX.X), s["vch"])
                        cval = float(512 * kt - 128 * qt)
                        S_.edge("v", lambda e, h=h, cval=cval: e.tensor_scalar(
                            offt[:], slcsb[:, h:h + 1], cval, None, ALU.mult), s["vch"])
                        S_.edge("v", lambda e: e.scalar_tensor_tensor(
                            mtile[:], psmax[:], SC0, offt[:], ALU.mult, ALU.add), s["vch"])
                        S_.edge("v", lambda e: e.tensor_tensor(
                            mnew[:], mrun[:], mtile[:], ALU.max), s["vch"])
                        S_.edge("v", lambda e: e.tensor_tensor(
                            mdif[:], mrun[:], mnew[:], ALU.subtract), s["vch"])
                        S_.w("v", s["sexp"], s["sexp"].n)
                        S_.edge("v", lambda e: e.tensor_tensor(
                            biast[:], offt[:], mnew[:], ALU.subtract), s["vch"])
                        S_.i("v", lambda e: e.tensor_copy(mrun[:], mnew[:]),
                             [(s["sadd"], 1)])
                        S_.w("sc", s["sadd"], s["sadd"].n)
                        S_.w("sc", s["sdn"], s["sdn"].n)
                        S_.w("sc", s["sres"], s["sres"].n)
                        S_.i("sc", lambda e: e.activation(
                            corr[:], mdif[:], AF.Exp), [(s["scorr"], 1)])
                        S_.w("sc", s["strp"], s["strp"].n)
                        S_.i("sc", lambda e, pp=pp: e.activation(
                            psb[:], pp[:], AF.Exp, bias=biast[:, 0:1], scale=SC0,
                            accum_out=tsum[:, 0:1]), [(s["sexp"], 1)])
                        S_.w("v", s["sexp"], s["sexp"].n)
                        S_.w("v", s["scorr"], s["scorr"].n)
                        S_.i("v", lambda e: e.scalar_tensor_tensor(
                            dnr[:], dnr[:], corr[:, 0:1], tsum[:], ALU.mult,
                            ALU.add), [(s["sdn"], 1)])
                        S_.w("v", s["sdn"], s["sdn"].n)
                        S_.w("te", s["sexp"], s["sexp"].n)
                        S_.w("te", s["spte"], s["spte"].n)
                        for j in range(4):
                            S_.i("te", lambda e, j=j: e.matmul(
                                ps_pt[:, 128 * j:128 * (j + 1)],
                                psb[:, 128 * j:128 * (j + 1)], idb[:],
                                is_transpose=True, start=(j == 0), stop=(j == 3)),
                                [(s["strp"], 1)] if j == 3 else ())
                        S_.w("sc", s["strp"], s["strp"].n)
                        S_.w("sc", s["sav"], s["sav"].n)
                        S_.i("sc", lambda e: e.copy(ptsb[:], ps_pt[:]),
                             [(s["spte"], 1)])
                        S_.w("te", s["spte"], s["spte"].n)
                        S_.w("te", s["sres"], s["sres"].n)
                        for j in range(4):
                            S_.i("te", lambda e, kt=kt, j=j: e.matmul(
                                ps_o[:], ptsb[:, 128 * j:128 * (j + 1)],
                                vtm[:, (4 * kt + j) * 128:(4 * kt + j + 1) * 128],
                                start=(j == 0), stop=(j == 3)),
                                [(s["sav"], 1)] if j == 3 else ())
                        S_.w("v", s["sav"], s["sav"].n)
                        S_.w("v", s["scorr"], s["scorr"].n)
                        if kt == 0:
                            S_.i("v", lambda e: e.tensor_copy(osacc[:], ps_o[:]),
                                 [(s["sres"], 1)])
                        else:
                            S_.i("v", lambda e: e.scalar_tensor_tensor(
                                osacc[:], osacc[:], corr[:, 0:1], ps_o[:],
                                ALU.mult, ALU.add), [(s["sres"], 1)])
                        S_.w("v", s["sres"], s["sres"].n)
                    S_.w("v", s["sdn"], s["sdn"].n)
                    S_.edge("v", lambda e: e.reciprocal(rcp[:], dnr[:]), s["vch"])
                    S_.w("v", s["ost"], s["ost"].n)
                    S_.i("v", lambda e: e.tensor_scalar(
                        osb[:], osacc[:], rcp[:, 0:1], None, ALU.mult),
                        [(s["soev"], 1)])
                    S_.w("sy", s["soev"], s["soev"].n)
                    S_.i("sy", lambda e, bh=bh, qt=qt: e.dma_start(
                        out=oo[bh, 128 * qt:128 * (qt + 1), :], in_=osb[:]),
                        [(s["ost"], 16)])
        S_.w("sy", s["ost"], s["ost"].n)

        blk = ctx.enter_context(nc.Block())

        @blk.gpsimd
        def _(e):
            S_.replay("g", e)

        @blk.sync
        def _(e):
            S_.replay("sy", e)

        @blk.tensor
        def _(e):
            S_.replay("te", e)

        @blk.vector
        def _(e):
            S_.replay("v", e)

        @blk.scalar
        def _(e):
            S_.replay("sc", e)

    return nc



def build_E(TPE=TPC, NIC=NG, NIH=4):
    nc = _bass()
    NTT = TPE // 128
    NTH2 = TPE // 512
    base, rem = NIC // NIH, NIC % NIH
    lens = [base + (1 if i < rem else 0) for i in range(NIH)]
    ihr, st = [], 0
    for L in lens:
        ihr.append((st, st + L))
        st += L
    HMAX = max(lens)

    ops = nc.dram_tensor("ops", [NCORES, 128, TPE], F32, kind="ExternalInput").ap()
    x2tm = nc.dram_tensor("x2tm_in", [TPE, D], BF16, kind="ExternalInput").ap()
    wo = nc.dram_tensor("wo", [128, D], F32, kind="ExternalInput").ap()
    bo = nc.dram_tensor("bo", [1, D], F32, kind="ExternalInput").ap()
    wgt = nc.dram_tensor("wgt", [NIC, 128, 32 * 128], BF16, kind="ExternalInput").ap()
    wut = nc.dram_tensor("wut", [NIC, 128, 32 * 128], BF16, kind="ExternalInput").ap()
    wdt = nc.dram_tensor("wdt", [NIC, 8, 128, 512], BF16, kind="ExternalInput").ap()
    out = nc.dram_tensor("out", [TPE, D], F32, kind="ExternalOutput").ap()
    x3d = nc.dram_tensor("x3d", [NTT, 128, D], BF16).ap()
    ypart = nc.dram_tensor("ypart", [8, NTT, 128, 512], F32).ap()

    ctx = contextlib.ExitStack()
    with ctx:
        sb = lambda *a: ctx.enter_context(nc.sbuf_tensor(*a))
        pt = lambda *a: ctx.enter_context(nc.psum_tensor(*a))
        opb = [sb(f"opbE{i}", [128, 512], F32) for i in range(2)]
        osf = sb("osfE", [128, TPE], F32)
        osb = sb("osbE", [128, TPE], BF16)
        wotb = sb("wotbE", [128, D], BF16)
        bob = sb("bobE", [128, D], BF16)
        x2sl = sb("x2slE", [128, D], BF16)
        rsl = sb("rslE", [128, D], BF16)
        ybuf = sb("ybufE", [128, D // 2], F32)
        x3fm = sb("x3fmE", [128, 32 * TPE], BF16)
        hfm = sb("hfmE", [128, HMAX * TPE], BF16)
        wgsl = sb("wgslE", [128, 32 * 128], BF16)
        wusl = sb("wuslE", [128, 32 * 128], BF16)
        wdsl = [sb(f"wdslE{i}", [128, 512], BF16) for i in range(2)]
        ysl = [sb(f"yslE{i}", [128, 512], F32) for i in range(2)]
        ypv = [sb(f"ypvE{i}", [128, 512], F32) for i in range(2)]
        idf = sb("idfE", [128, 128], F32)
        epst = sb("epstE", [128, 1], F32)
        ssum = sb("ssumE", [128, 1], F32)
        sqs = sb("sqsE", [128, 1], F32)
        mu = sb("muE", [128, 1], F32)
        vv = sb("vvE", [128, 1], F32)
        m2 = sb("m2E", [128, 1], F32)
        rr = sb("rrE", [128, 1], F32)

        ps8 = [pt(f"psE{i}", [128, 512], F32) for i in range(8)]

        s = make_sems(ctx, nc, [
            "lop0", "lop1", "lwb", "lx2", "lwg", "lwd0", "lwd1", "lyp0",
            "lyp1", "lyg", "aop", "acast", "amm", "aev", "sqd", "sqr",
            "lnm", "x3p", "x3w", "trp", "tre", "f1m", "f1s", "f1h", "f2t",
            "f2e", "ypw0", "ypw1", "outw", "vch", "sch", "gch", "idr"])
        S_ = Sched()

        S_.i("g", lambda e: e.iota(idf[:], pattern=[[1, 128]], base=0,
                                   channel_multiplier=-1,
                                   allow_small_or_imprecise_dtypes=True),
             [(s["gch"], 1)])
        S_.i("g", lambda e: e.dma_start(out=wotb[:], in_=wo[:, :]),
             [(s["lwb"], 16)])
        S_.i("g", lambda e: e.dma_start(
            out=bob[:], in_=bo[0:1, :].broadcast_to([128, D])),
            [(s["lwb"], 16)])
        S_.w("v", s["gch"], 1)
        S_.edge("v", lambda e: e.tensor_scalar(idf[:], idf[:], 0.0, None,
                                               ALU.is_equal), s["vch"])
        S_.i("v", lambda e: e.memset(epst[:], LN_EPS), [(s["idr"], 1)])

        nop = 0
        for j in range(NCORES):
            for hh in range(TPE // 512):
                sl = nop % 2
                lsem = s["lop%d" % sl]
                if nop >= 2:
                    S_.w("sy", s["aop"], nop - 1)
                S_.i("sy", lambda e, j=j, hh=hh, sl=sl: e.dma_start(
                    out=opb[sl][:], in_=ops[j, :, 512 * hh:512 * (hh + 1)]),
                    [(lsem, 16)])
                nop += 1
                S_.w("v", lsem, 16 * ((nop + 1) // 2))
                S_.w("v", s["aop"], s["aop"].n)
                dsl = slice(512 * hh, 512 * (hh + 1))
                if j == 0:
                    S_.i("v", lambda e, sl=sl, dsl=dsl: e.tensor_copy(
                        osf[:, dsl], opb[sl][:]), [(s["aop"], 1)])
                else:
                    S_.i("v", lambda e, sl=sl, dsl=dsl: e.tensor_tensor(
                        osf[:, dsl], osf[:, dsl], opb[sl][:], ALU.add),
                        [(s["aop"], 1)])
        S_.w("v", s["aop"], s["aop"].n)
        S_.i("v", lambda e: e.tensor_copy(osb[:], osf[:]), [(s["acast"], 1)])

        nx2 = 0
        for tt in range(NTT):
            S_.w("sy", s["sqr"], max(0, tt))
            S_.i("sy", lambda e, tt=tt: e.dma_start(
                out=x2sl[:], in_=x2tm[128 * tt:128 * (tt + 1), :]),
                [(s["lx2"], 16)])
            nx2 += 1
            S_.w("v", s["x3w"], 16 * tt)
            for dc in range(8):
                bank = 4 + (dc % 2)
                idx = 8 * tt + dc
                S_.w("te", s["lwb"], 32)
                S_.w("te", s["acast"], 1)
                S_.w("te", s["aev"], max(0, idx - 1))
                S_.i("te", lambda e, tt=tt, dc=dc, bank=bank: e.matmul(
                    ps8[bank][:], osb[:, 128 * tt:128 * (tt + 1)],
                    wotb[:, 512 * dc:512 * (dc + 1)], start=True, stop=True),
                    [(s["amm"], 1)])
                S_.w("v", s["amm"], s["amm"].n)
                S_.w("v", s["lx2"], 16 * nx2)
                dsl = slice(512 * dc, 512 * (dc + 1))
                S_.edge("v", lambda e, bank=bank, dsl=dsl: e.tensor_tensor(
                    rsl[:, dsl], ps8[bank][:], x2sl[:, dsl], ALU.add), s["vch"])
                S_.i("v", lambda e, dsl=dsl: e.tensor_tensor(
                    rsl[:, dsl], rsl[:, dsl], bob[:, dsl], ALU.add),
                    [(s["aev"], 1)])
            S_.w("v", s["aev"], s["aev"].n)
            S_.edge("v", lambda e: e.reduce_sum(ssum[:], rsl[:], axis=AX.X),
                    s["vch"])
            S_.w("sc", s["aev"], s["aev"].n)
            S_.i("sc", lambda e: e.activation(x2sl[:], rsl[:], AF.Square),
                 [(s["sqd"], 1)])
            S_.w("v", s["sqd"], s["sqd"].n)
            S_.i("v", lambda e: e.reduce_sum(sqs[:], x2sl[:], axis=AX.X),
                 [(s["sqr"], 1)])
            S_.edge("v", lambda e: e.tensor_scalar(mu[:], ssum[:], 1.0 / D,
                                                   None, ALU.mult), s["vch"])
            S_.edge("v", lambda e: e.tensor_scalar(vv[:], sqs[:], 1.0 / D,
                                                   None, ALU.mult), s["vch"])
            S_.edge("v", lambda e: e.tensor_tensor(m2[:], mu[:], mu[:],
                                                   ALU.mult), s["vch"])
            S_.i("v", lambda e: e.tensor_tensor(vv[:], vv[:], m2[:],
                                                ALU.subtract), [(s["lnm"], 1)])
            S_.w("sc", s["lnm"], s["lnm"].n)
            S_.edge("sc", lambda e: e.activation(vv[:], vv[:], AF.Ln,
                                                 bias=epst[:, 0:1]), s["sch"])
            S_.i("sc", lambda e: e.activation(rr[:], vv[:], AF.Exp,
                                              scale=-0.5), [(s["lnm"], 1)])
            S_.w("v", s["lnm"], s["lnm"].n)
            for dg in range(8):
                dsl = slice(512 * dg, 512 * (dg + 1))
                S_.w("v", s["trp"], s["trp"].n)
                S_.edge("v", lambda e, dsl=dsl: e.tensor_scalar(
                    ysl[0][:], rsl[:, dsl], mu[:, 0:1], rr[:, 0:1],
                    ALU.subtract, ALU.mult), s["vch"])
                S_.i("v", lambda e, dsl=dsl: e.tensor_scalar(
                    rsl[:, dsl], rsl[:, dsl], mu[:, 0:1], rr[:, 0:1],
                    ALU.subtract, ALU.mult), [(s["x3p"], 1)])
                S_.w("te", s["x3p"], s["x3p"].n)
                S_.w("te", s["idr"], 1)
                S_.w("te", s["tre"], s["tre"].n)
                for jj in range(4):
                    S_.i("te", lambda e, jj=jj: e.matmul(
                        ps8[6][:, 128 * jj:128 * (jj + 1)],
                        ysl[0][:, 128 * jj:128 * (jj + 1)], idf[:],
                        is_transpose=True, start=(jj == 0), stop=(jj == 3)),
                        [(s["trp"], 1)] if jj == 3 else ())
                S_.w("sc", s["trp"], s["trp"].n)
                S_.i("sc", lambda e, dg=dg, tt=tt, TPE=TPE: e.copy(
                    x3fm[:, :].rearrange("p (c t) -> p c t", t=TPE)[
                        :, 4 * dg:4 * (dg + 1), 128 * tt:128 * (tt + 1)],
                    ps8[6][:, :].rearrange("p (c x) -> p c x", x=128)),
                    [(s["tre"], 1)])
            S_.w("sy", s["x3p"], s["x3p"].n)
            S_.i("sy", lambda e, tt=tt: e.dma_start(
                out=x3d[tt], in_=rsl[:]), [(s["x3w"], 16)])

        nwg = 0
        nwd = [0, 0]
        nyr = [0, 0]
        nyw = [0, 0]
        gpi = 2 * NTH2
        for p, (i0_, i1_) in enumerate(ihr):
            L = i1_ - i0_
            for jl, ich in enumerate(range(i0_, i1_)):
                S_.w("sy", s["f1m"], s["f1m"].n)
                S_.i("sy", lambda e, ich=ich: e.dma_start(
                    out=wgsl[:], in_=wgt[ich]), [(s["lwg"], 16)])
                S_.i("sy", lambda e, ich=ich: e.dma_start(
                    out=wusl[:], in_=wut[ich]), [(s["lwg"], 16)])
                nwg += 2
                for proj in range(2):
                    wsl = wgsl if proj == 0 else wusl
                    for th2 in range(NTH2):
                        bank = proj * NTH2 + th2
                        S_.w("te", s["lwg"], 16 * nwg)
                        if jl == 0 and proj == 0 and th2 == 0:
                            S_.w("te", s["tre"], s["tre"].n)
                            S_.w("te", s["f2e"], s["f2e"].n)
                        S_.w("te", s["f1h"], s["f1h"].n)
                        for dch in range(32):
                            S_.i("te", lambda e, wsl=wsl, dch=dch, th2=th2,
                                 bank=bank, TPE=TPE: e.matmul(
                                ps8[bank][:],
                                wsl[:, 128 * dch:128 * (dch + 1)],
                                x3fm[:, dch * TPE + 512 * th2:
                                     dch * TPE + 512 * (th2 + 1)],
                                start=(dch == 0), stop=(dch == 31)),
                                [(s["f1m"], 1)] if dch == 31 else ())
                for th2 in range(NTH2):
                    su = NTH2 * jl + th2
                    slq = su % 2
                    S_.w("sc", s["f1m"], s["f1m"].n - 2 * NTH2 + th2 + 1)
                    S_.w("sc", s["f1h"], s["f1h"].n)
                    if jl == 0 and th2 == 0:
                        S_.w("sc", s["trp"], s["trp"].n)
                        S_.w("sc", s["ypw0"], 16 * nyw[0])
                        S_.w("sc", s["ypw1"], 16 * nyw[1])
                    S_.i("sc", lambda e, th2=th2, slq=slq: e.activation(
                        ysl[slq][:], ps8[th2][:], AF.Silu), [(s["f1s"], 1)])
                    S_.w("v", s["f1s"], s["f1s"].n)
                    S_.w("v", s["f1m"], s["f1m"].n)
                    if p > 0 and jl == 0 and th2 == 0:
                        S_.w("v", s["f2t"], s["f2t"].n)
                    S_.i("v", lambda e, jl=jl, th2=th2, slq=slq,
                         TPE=TPE: e.tensor_tensor(
                        hfm[:, jl * TPE + 512 * th2:jl * TPE + 512 * (th2 + 1)],
                        ysl[slq][:], ps8[NTH2 + th2][:], ALU.mult),
                        [(s["f1h"], 1)])
            if p > 0:
                S_.w("g", s["ypw0"], 16 * nyw[0])
                S_.w("g", s["ypw1"], 16 * nyw[1])
            for dch in range(8):
                S_.w("te", s["f2e"], s["f2e"].n)
                if dch == 0:
                    S_.w("te", s["f1h"], s["f1h"].n)
                for jl in range(L):
                    li = dch * L + jl
                    slot = li % 2
                    S_.w("g", s["f2t"], max(0, s["f2t"].n - 1))
                    S_.i("g", lambda e, ich=i0_ + jl, dch=dch, slot=slot:
                         e.dma_start(out=wdsl[slot][:], in_=wdt[ich, dch]),
                         [(s["lwd%d" % slot], 16)])
                    nwd[slot] += 1
                    S_.w("te", s["lwd%d" % slot], 16 * nwd[slot])
                    for tt in range(NTT):
                        S_.i("te", lambda e, jl=jl, tt=tt, slot=slot, L=L,
                             TPE=TPE: e.matmul(
                            ps8[tt][:],
                            hfm[:, jl * TPE + 128 * tt:jl * TPE + 128 * (tt + 1)],
                            wdsl[slot][:], start=(jl == 0), stop=(jl == L - 1)),
                            [(s["f2t"], 1)] if tt == NTT - 1 else ())
                for tt in range(NTT):
                    sl2 = tt % 2
                    if p > 0:
                        slr = (dch * NTT + tt) % 2
                        S_.w("g", s["f2e"], max(0, s["f2e"].n - 1))
                        S_.i("g", lambda e, dch=dch, tt=tt, slr=slr:
                             e.dma_start(out=ypv[slr][:], in_=ypart[dch, tt]),
                             [(s["lyp%d" % slr], 16)])
                        nyr[slr] += 1
                    S_.w("v", s["f2t"], s["f2t"].n)
                    S_.w("v", s["ypw%d" % sl2], 16 * nyw[sl2])
                    if p == 0:
                        S_.i("v", lambda e, tt=tt, sl2=sl2: e.tensor_copy(
                            ysl[sl2][:], ps8[tt][:]), [(s["f2e"], 1)])
                    else:
                        S_.w("v", s["lyp%d" % slr], 16 * nyr[slr])
                        S_.i("v", lambda e, tt=tt, sl2=sl2, slr=slr:
                             e.tensor_tensor(ysl[sl2][:], ps8[tt][:],
                                             ypv[slr][:], ALU.add),
                             [(s["f2e"], 1)])
                    S_.w("sy", s["f2e"], s["f2e"].n)
                    S_.i("sy", lambda e, dch=dch, tt=tt, sl2=sl2: e.dma_start(
                        out=ypart[dch, tt], in_=ysl[sl2][:]),
                        [(s["ypw%d" % sl2], 16)])
                    nyw[sl2] += 1

        nyg = 0
        nout = 0
        for tt in range(NTT):
            S_.w("sy", s["sqr"], NTT + tt)
            S_.w("sy", s["x3w"], 16 * NTT)
            S_.i("sy", lambda e, tt=tt: e.dma_start(
                out=x2sl[:], in_=x3d[tt]), [(s["lx2"], 16)])
            nx2 += 1
            for hh in range(2):
                S_.w("sy", s["ypw0"], 16 * nyw[0])
                S_.w("sy", s["ypw1"], 16 * nyw[1])
                S_.w("sy", s["outw"], 16 * nout)
                S_.i("sy", lambda e, tt=tt, hh=hh: e.dma_start(
                    out=ybuf[:, :].rearrange("p (c x) -> p c x", x=512),
                    in_=ypart[4 * hh:4 * (hh + 1), tt].rearrange(
                        "d p x -> p d x")), [(s["lyg"], 16)])
                nyg += 1
                S_.w("v", s["lyg"], 16 * nyg)
                S_.w("v", s["lx2"], 16 * nx2)
                dsl = slice(2048 * hh, 2048 * (hh + 1))
                S_.i("v", lambda e, dsl=dsl: e.tensor_tensor(
                    rsl[:, dsl], ybuf[:], x2sl[:, dsl], ALU.add),
                    [(s["aev"], 1)])
                S_.w("sy", s["aev"], s["aev"].n)
            S_.w("v", s["aev"], s["aev"].n)
            S_.edge("v", lambda e: e.reduce_sum(ssum[:], rsl[:], axis=AX.X),
                    s["vch"])
            S_.w("sc", s["aev"], s["aev"].n)
            S_.i("sc", lambda e: e.activation(x2sl[:], rsl[:], AF.Square),
                 [(s["sqd"], 1)])
            S_.w("v", s["sqd"], s["sqd"].n)
            S_.i("v", lambda e: e.reduce_sum(sqs[:], x2sl[:], axis=AX.X),
                 [(s["sqr"], 1)])
            S_.edge("v", lambda e: e.tensor_scalar(mu[:], ssum[:], 1.0 / D,
                                                   None, ALU.mult), s["vch"])
            S_.edge("v", lambda e: e.tensor_scalar(vv[:], sqs[:], 1.0 / D,
                                                   None, ALU.mult), s["vch"])
            S_.edge("v", lambda e: e.tensor_tensor(m2[:], mu[:], mu[:],
                                                   ALU.mult), s["vch"])
            S_.i("v", lambda e: e.tensor_tensor(vv[:], vv[:], m2[:],
                                                ALU.subtract), [(s["lnm"], 1)])
            S_.w("sc", s["lnm"], s["lnm"].n)
            S_.edge("sc", lambda e: e.activation(vv[:], vv[:], AF.Ln,
                                                 bias=epst[:, 0:1]), s["sch"])
            S_.i("sc", lambda e: e.activation(rr[:], vv[:], AF.Exp,
                                              scale=-0.5), [(s["lnm"], 1)])
            S_.w("v", s["lnm"], s["lnm"].n)
            for hh in range(2):
                dsl = slice(2048 * hh, 2048 * (hh + 1))
                S_.w("v", s["outw"], 16 * nout)
                S_.i("v", lambda e, dsl=dsl: e.tensor_scalar(
                    ybuf[:], rsl[:, dsl], mu[:, 0:1], rr[:, 0:1],
                    ALU.subtract, ALU.mult), [(s["x3p"], 1)])
                S_.w("sy", s["x3p"], s["x3p"].n)
                S_.i("sy", lambda e, tt=tt, hh=hh: e.dma_start(
                    out=out[128 * tt:128 * (tt + 1),
                            2048 * hh:2048 * (hh + 1)], in_=ybuf[:]),
                    [(s["outw"], 16)])
                nout += 1
        S_.w("sy", s["outw"], 16 * nout)

        blk = ctx.enter_context(nc.Block())

        @blk.gpsimd
        def _(e):
            S_.replay("g", e)

        @blk.sync
        def _(e):
            S_.replay("sy", e)

        @blk.tensor
        def _(e):
            S_.replay("te", e)

        @blk.vector
        def _(e):
            S_.replay("v", e)

        @blk.scalar
        def _(e):
            S_.replay("sc", e)

    return nc



def _core_heads(c):
    return list(range(4 * c, 4 * c + 4))


def _qkv_cols(c):
    hs = _core_heads(c)
    cols = []
    for part in range(3):
        for h in hs:
            cols.extend(range(part * D + h * HD, part * D + (h + 1) * HD))
    return np.array(cols)


def _pad_cols(a, total, axis):
    pad = total - a.shape[axis]
    if pad == 0:
        return a
    widths = [(0, 0)] * a.ndim
    widths[axis] = (0, pad)
    return np.pad(a, widths, mode="edge")


def stage_A(inputs):
    x = np.ascontiguousarray(
        inputs["L0_attn_hidden_states"].reshape(T, D).T)
    wqkv_t = np.ascontiguousarray(inputs["L0_attn_Wqkv"].T)
    shifts = np.tile(np.arange(8, dtype=np.int32) * 4, 16).reshape(128, 1)
    in_maps = []
    gqw = inputs["L1_ffn_gate_qweight"]
    gsc = inputs["L1_ffn_gate_scales"]
    gqz = inputs["L1_ffn_gate_qzeros"]
    ggi = inputs["L1_ffn_gate_g_idx"]
    uqw = inputs["L1_ffn_up_qweight"]
    usc = inputs["L1_ffn_up_scales"]
    uqz = inputs["L1_ffn_up_qzeros"]
    ugi = inputs["L1_ffn_up_g_idx"]
    dqw = inputs["L1_ffn_down_qweight"]
    dsc = inputs["L1_ffn_down_scales"]
    dqz = inputs["L1_ffn_down_qzeros"]
    dgi = inputs["L1_ffn_down_g_idx"]
    for c in range(NCORES):
        cols = _qkv_cols(c)
        wq = np.ascontiguousarray(wqkv_t[:, cols])
        n0, n1 = core_inter_range(c)
        m = {
            "xfm": x, "wq": wq, "shifts": shifts,
            "qwg": _pad_cols(np.ascontiguousarray(gqw[:, n0:n1]), NPC, 1),
            "scg": _pad_cols(np.ascontiguousarray(gsc[:, n0:n1]), NPC, 1),
            "qzg": _pad_cols(np.ascontiguousarray(gqz[:, n0 // 8:n1 // 8]), NPC // 8, 1),
            "gig": np.ascontiguousarray(np.broadcast_to(ggi[None, :], (32, D))),
            "qwu": _pad_cols(np.ascontiguousarray(uqw[:, n0:n1]), NPC, 1),
            "scu": _pad_cols(np.ascontiguousarray(usc[:, n0:n1]), NPC, 1),
            "qzu": _pad_cols(np.ascontiguousarray(uqz[:, n0 // 8:n1 // 8]), NPC // 8, 1),
            "qwd": _pad_rows(np.ascontiguousarray(dqw[n0 // 8:n1 // 8, :]), NPC // 8),
            "scd": _pad_rows_groups(dsc, c),
            "qzd": _pad_rows_groups(dqz, c),
            "gid": _pad_cols(np.ascontiguousarray(dgi[None, n0:n1]), NPC, 1).repeat(NGC, 0).copy(),
            "gidv": group_ids_down(c),
        }
        in_maps.append(m)
    return in_maps


def core_inter_range(c):
    gstarts = []
    g0 = 0
    for i in range(NCORES):
        ng = 11 if i < 6 else 10
        gstarts.append((g0, g0 + ng))
        g0 += ng
    a, b = gstarts[c]
    return a * GRP, b * GRP


def group_ids_down(c):
    n0, _ = core_inter_range(c)
    return np.arange(n0 // GRP, n0 // GRP + NGC, dtype=np.float32).reshape(NGC, 1)


def _pad_rows(a, total):
    if a.shape[0] >= total:
        return a
    return np.pad(a, ((0, total - a.shape[0]), (0, 0)), mode="edge")


def _pad_rows_groups(a, c):
    n0, n1 = core_inter_range(c)
    g0, g1 = n0 // GRP, n1 // GRP
    sl = np.ascontiguousarray(a[g0:g1])
    if sl.shape[0] < NGC:
        sl = np.pad(sl, ((0, NGC - sl.shape[0]), (0, 0)), mode="edge")
    return sl



def build_C(TP=TPC, DBG=False, PH=5):
    nc = _bass()
    NT = TP // 128

    ofm = nc.dram_tensor("ofm", [D, TP], F32, kind="ExternalInput").ap()
    xtm = nc.dram_tensor("xtm", [TP, D], F32, kind="ExternalInput").ap()
    wot = nc.dram_tensor("wot", [D, D], F32, kind="ExternalInput").ap()
    kts = [nc.dram_tensor(f"k{i}t", [8, 128, 32, 512], U8, kind="ExternalInput").ap()
           for i in range(3)]
    s0b = nc.dram_tensor("s0b", [128, 4], F32, kind="ExternalInput").ap()
    eic = nc.dram_tensor("eic", [128, 32], F32, kind="ExternalInput").ap()
    wqct = nc.dram_tensor("wqct", [D, QL], F32, kind="ExternalInput").ap()
    bqc = nc.dram_tensor("bqc", [128, 12], F32, kind="ExternalInput").ap()
    wkvct = nc.dram_tensor("wkvct", [D, 576], F32, kind="ExternalInput").ap()
    bkvc = nc.dram_tensor("bkvc", [128, 5], F32, kind="ExternalInput").ap()

    x2tm = nc.dram_tensor("x2tm", [TP, D], BF16, kind="ExternalOutput").ap()
    qc = nc.dram_tensor("qc", [QL, TP], BF16, kind="ExternalOutput").ap()
    kvc = nc.dram_tensor("kvc", [576, TP], BF16, kind="ExternalOutput").ap()
    att = nc.dram_tensor("attb", [TP, D], BF16).ap()
    dbgs = [nc.dram_tensor(f"dbg{i}", [128, 32 * TP], BF16,
                           kind="ExternalOutput").ap() for i in range(2)] if DBG else None
    dbga = nc.dram_tensor("dbga", [128, 32 * TP], BF16,
                          kind="ExternalOutput").ap() if DBG else None
    dbgb = nc.dram_tensor("dbgb", [128, 32 * TP], F32,
                          kind="ExternalOutput").ap() if DBG else None
    dbgr = nc.dram_tensor("dbgr", [1, TP], F32,
                          kind="ExternalOutput").ap() if DBG else None
    dbgw = nc.dram_tensor("dbgw", [128, 32 * 512], BF16,
                          kind="ExternalOutput").ap() if DBG else None
    dbgsx = nc.dram_tensor("dbgsx", [128, 4], F32,
                           kind="ExternalOutput").ap() if DBG else None
    z2d = nc.dram_tensor("z2d", [TP, D], BF16).ap()
    x1d = nc.dram_tensor("x1d", [TP, D], BF16).ap()

    ctx = contextlib.ExitStack()
    with ctx:
        sb = lambda *a: ctx.enter_context(nc.sbuf_tensor(*a))
        pt = lambda *a: ctx.enter_context(nc.psum_tensor(*a))
        arA = sb("arA", [128, 32 * TP], BF16)
        arB = sb("arB", [128, 32 * TP], BF16)
        wk = sb("wk", [128, 32 * 512], BF16)
        ku = [sb("ku0", [128, 32 * 512], U8)]
        slab1 = sb("slab1", [128, D], BF16)
        slab2 = sb("slab2", [128, D], BF16)
        asb = [sb(f"asb{i}", [128, 512], BF16) for i in range(2)]
        rsx = sb("rsx", [1, TP], F32)
        nego = sb("nego", [1, 512], F32)
        onec = sb("onec", [128, 1], BF16)
        sc4 = sb("sc4", [128, 4], F32)
        eisb = sb("eisb", [128, 32], F32)
        sums = sb("sums", [128, 1], F32)
        sqs = sb("sqs", [128, 1], F32)
        mu = sb("mu", [128, 1], F32)
        var = sb("var", [128, 1], F32)
        rr = sb("rr", [128, 1], F32)
        epst = sb("epstC", [128, 1], F32)
        s2x = sb("s2x", [128, 4], F32)
        idf = sb("idfC", [128, 128], F32)
        idb = sb("idbC", [128, 128], BF16)
        bsb = sb("bsb", [128, 12], F32)
        bsb2 = sb("bsb2", [128, 5], F32)

        ps_a = [pt(f"psa{i}", [128, 512], F32) for i in range(2)]
        ps_r = pt("psr", [1, TP], F32)
        ps_t = pt("pstC", [128, 512], BF16)

        s = make_sems(ctx, nc, [
            "ldo", "ldw", "ldx", "lds", "ldk", "lda", "ldz", "ldx1",
            "mm", "ev", "sq", "sts", "sts0", "sts1", "trp", "tre", "rsm", "rse",
            "stx", "stq", "vch", "sch", "gch"])

        S_ = Sched()

        class StopSched(Exception):
            pass

        S_.i("g", lambda e: e.iota(idf[:], pattern=[[1, 128]], base=0,
                                   channel_multiplier=-1,
                                   allow_small_or_imprecise_dtypes=True),
             [(s["gch"], 1)])
        S_.w("v", s["gch"], 1)
        S_.edge("v", lambda e: e.tensor_scalar(idf[:], idf[:], 0.0, None,
                                               ALU.is_equal), s["vch"])
        S_.edge("v", lambda e: e.tensor_copy(idb[:], idf[:]), s["vch"])
        S_.edge("v", lambda e: e.memset(epst[:], LN_EPS), s["vch"])
        S_.edge("v", lambda e: e.memset(nego[:], -1.0), s["vch"])
        S_.edge("v", lambda e: e.memset(onec[:], 1.0), s["vch"])
        S_.i("sy", lambda e: e.dma_start(out=sc4[:], in_=s0b[:, :]), [(s["lds"], 16)])
        S_.i("sy", lambda e: e.dma_start(out=eisb[:], in_=eic[:, :]), [(s["lds"], 16)])
        S_.i("sy", lambda e: e.dma_start(out=bsb[:, 0:12], in_=bqc[:, :]),
             [(s["lds"], 16)])
        S_.i("sy", lambda e: e.dma_start(out=bsb2[:, 0:5], in_=bkvc[:, :]),
             [(s["lds"], 16)])
        S_.w("v", s["lds"], 64)
        S_.edge("v", lambda e: e.tensor_scalar(s2x[:, 0:2], sc4[:, 0:2], 2.0,
                                               None, ALU.mult), s["vch"])
        S_.edge("v", lambda e: e.tensor_copy(s2x[:, 2:3], sc4[:, 2:3]), s["vch"])
        S_.i("g", lambda e: e.dma_start(
            out=arA[:], in_=ofm.rearrange("(c p) t -> p c t", p=128)),
            [(s["ldo"], 16)])

        try:
            for ech in range(8):
                S_.w("g", s["mm"], ech * NT)
                S_.i("g", lambda e, ech=ech: e.dma_start(
                    out=wk[:], in_=wot.rearrange("(c p) e -> p c e", p=128)
                    [:, :, 512 * ech:512 * (ech + 1)]), [(s["ldw"], 16)])
                for tch in range(NT):
                    i0 = ech * NT + tch
                    S_.w("te", s["ldo"], 16)
                    S_.w("te", s["ldw"], 16 * (ech + 1))
                    S_.w("te", s["ev"], max(0, i0 - 1))
                    for dd in range(32):
                        S_.i("te", lambda e, dd=dd, tch=tch, i0=i0: e.matmul(
                            ps_a[i0 % 2][:],
                            arA[:, dd * TP + 128 * tch:dd * TP + 128 * (tch + 1)],
                            wk[:, 512 * dd:512 * (dd + 1)],
                            start=(dd == 0), stop=(dd == 31)),
                            [(s["mm"], 1)] if dd == 31 else ())
                    S_.w("v", s["mm"], s["mm"].n)
                    S_.w("v", s["sts%d" % (i0 % 2)].h if False else s["sts%d" % (i0 % 2)],
                         s["sts%d" % (i0 % 2)].n)
                    S_.i("v", lambda e, i0=i0: e.tensor_copy(asb[i0 % 2][:],
                                                             ps_a[i0 % 2][:]),
                         [(s["ev"], 1)])
                    S_.w("sy", s["ev"], s["ev"].n)
                    S_.i("sy", lambda e, ech=ech, tch=tch, i0=i0: e.dma_start(
                        out=att[128 * tch:128 * (tch + 1), 512 * ech:512 * (ech + 1)],
                        in_=asb[i0 % 2][:]), [(s["sts%d" % (i0 % 2)], 16)])

            NSTS0, NSTS1 = s["sts0"].n, s["sts1"].n
            prev_vx1 = 0
            for tch in range(NT):
                S_.w("sy", s["sts0"], NSTS0)
                S_.w("sy", s["sts1"], NSTS1)
                S_.w("sy", s["vch"], prev_vx1)
                S_.i("sy", lambda e, tch=tch: e.dma_start(
                    out=slab1[:], in_=att[128 * tch:128 * (tch + 1), :]),
                    [(s["lda"], 16)])
                S_.w("g", s["stx"], 16 * tch)
                S_.w("g", s["trp"], 8 * tch)
                S_.w("g", s["vch"], prev_vx1)
                S_.i("g", lambda e, tch=tch: e.dma_start(
                    out=slab2[:], in_=xtm[128 * tch:128 * (tch + 1), :]),
                    [(s["ldx"], 16)])
                S_.w("v", s["lda"], 16 * (tch + 1))
                S_.w("v", s["ldx"], 16 * (tch + 1))
                S_.edge("v", lambda e: e.tensor_tensor(slab1[:], slab1[:], slab2[:],
                                                       ALU.add), s["vch"])
                S_.edge("v", lambda e: e.reduce_sum(sums[:], slab1[:], axis=AX.X),
                        s["vch"])
                VCHQ = s["vch"].n
                S_.w("sc", s["vch"], VCHQ)
                S_.i("sc", lambda e: e.activation(slab2[:], slab1[:], AF.Square),
                     [(s["sq"], 1)])
                S_.w("v", s["sq"], s["sq"].n)
                S_.edge("v", lambda e: e.reduce_sum(sqs[:], slab2[:], axis=AX.X),
                        s["vch"])
                S_.edge("v", lambda e: e.tensor_scalar(mu[:], sums[:], 1.0 / D, None,
                                                       ALU.mult), s["vch"])
                S_.edge("v", lambda e: e.tensor_tensor(var[:], mu[:], mu[:],
                                                       ALU.mult), s["vch"])
                S_.edge("v", lambda e: e.scalar_tensor_tensor(
                    var[:], sqs[:], 1.0 / D, var[:], ALU.mult, ALU.subtract),
                    s["vch"])
                VCHV = s["vch"].n
                S_.w("sc", s["vch"], VCHV)
                S_.edge("sc", lambda e: e.activation(var[:], var[:], AF.Ln,
                                                     bias=epst[:, 0:1]), s["sch"])
                S_.i("sc", lambda e: e.activation(rr[:], var[:], AF.Exp, scale=-0.5),
                     [(s["sq"], 1)])
                S_.w("v", s["sq"], s["sq"].n)
                S_.edge("v", lambda e: e.tensor_scalar(slab2[:], slab1[:], mu[:, 0:1],
                                                       rr[:, 0:1], ALU.subtract,
                                                       ALU.mult), s["vch"])
                prev_vx1 = s["vch"].n
                S_.w("sy", s["vch"], s["vch"].n)
                S_.i("sy", lambda e, tch=tch: e.dma_start(
                    out=x1d[128 * tch:128 * (tch + 1), :], in_=slab2[:]),
                    [(s["stx"], 16)])
                for j4 in range(8):
                    S_.w("te", s["vch"], s["vch"].n)
                    S_.w("te", s["tre"], s["tre"].n)
                    for j in range(4):
                        S_.i("te", lambda e, j4=j4, j=j: e.matmul(
                            ps_t[:, 128 * j:128 * (j + 1)],
                            slab2[:, 512 * j4 + 128 * j:512 * j4 + 128 * (j + 1)],
                            idb[:], is_transpose=True, start=(j == 0), stop=(j == 3)),
                            [(s["trp"], 1)] if j == 3 else ())
                    S_.w("sc", s["trp"], s["trp"].n)
                    dst = arB[:, :].rearrange("p (c t) -> p c t", c=32)[
                        :, 4 * j4:4 * (j4 + 1), 128 * tch:128 * (tch + 1)]
                    S_.i("sc", lambda e, dst=dst: e.copy(
                        dst, ps_t[:].rearrange("p (c t) -> p c t", c=4)),
                        [(s["tre"], 1)])

            if DBG:
                S_.w("sy", s["tre"], s["tre"].n)
                S_.i("sy", lambda e: e.dma_start(out=dbga[:, :], in_=arB[:]),
                     [(s["stq"], 16)])
            def rowsums(src_ar, stg, only=0):
                S_.w("te", s["ev"], s["ev"].n)
                S_.w("te", s["tre"], s["tre"].n)
                if only == 1:
                    return
                for th in range(TP // 512):
                    S_.w("te", s["rse"], s["rse"].n)
                    for dd in range(32):
                        S_.i("te", lambda e, dd=dd, th=th: e.matmul(
                            ps_r[:, 512 * th:512 * (th + 1)],
                            onec[:],
                            src_ar[:, dd * TP + 512 * th:dd * TP + 512 * (th + 1)],
                            start=(dd == 0), stop=(dd == 31)),
                            [(s["rsm"], 1)] if dd == 31 else ())
                    if only == 2:
                        continue
                    S_.w("v", s["rsm"], s["rsm"].n)
                    S_.i("v", lambda e, th=th: e.tensor_copy(
                        rsx[:, 512 * th:512 * (th + 1)],
                        ps_r[:, 512 * th:512 * (th + 1)]), [(s["rse"], 1)])

            for stg in range(2):
                if PH < stg + 1:
                    break
                src_ar = arB if stg == 0 else arA
                dst_ar = arA if stg == 0 else arB
                kt = kts[stg]
                rowsums(src_ar if PH != 16 else arB, stg,
                        only=(1 if PH == 17 else 2 if PH == 18 else 0))
                RSE = s["rse"].n
                if PH in (15, 16, 17, 18) and stg == 1:
                    raise StopSched
                if DBG and stg == 0:
                    S_.w("sy", s["rse"], RSE)
                    S_.i("sy", lambda e: e.dma_start(out=dbgr[:, :], in_=rsx[:, 0:TP]),
                         [(s["stq"], 16)])
                    S_.w("sy", s["sq"], s["sq"].n + 1)
                    S_.i("sy", lambda e: e.dma_start(out=dbgw[:, :], in_=wk[:]),
                         [(s["stq"], 16)])
                    S_.i("sy", lambda e: e.dma_start(out=dbgsx[:, :], in_=sc4[:]),
                         [(s["stq"], 16)])
                for ng in range(8):
                    li = stg * 8 + ng
                    S_.w("g", s["mm"], s["mm"].n)
                    S_.w("g", s["sq"], s["sq"].n)
                    S_.i("g", lambda e, kt=kt, ng=ng: e.dma_start(
                        out=ku[0][:], in_=kt[ng, :, :, :]), [(s["ldk"], 16)])
                    S_.w("sc", s["ldk"], 16 * (li + 1))
                    S_.w("sc", s["mm"], s["mm"].n)
                    S_.i("sc", lambda e, ng=ng: e.activation(
                        wk[:], ku[0][:], AF.Copy, scale=2.0), [(s["sq"], 1)])
                    for nq in range(4):
                        nch = 4 * ng + nq
                        for th in range(TP // 512):
                            i0 = (stg * 8 + ng) * 4 * (TP // 512) + nq * (TP // 512) + th
                            S_.w("te", s["sq"], s["sq"].n)
                            S_.w("te", s["rse"], RSE)
                            S_.w("te", s["tre"], s["tre"].n)
                            S_.w("te", s["ev"], s["ev"].n)
                            PSI = (s["mm"].n) % 2
                            for dd in range(32):
                                S_.i("te", lambda e, dd=dd, nq=nq, th=th, PSI=PSI,
                                     src_ar=src_ar: e.matmul(
                                    ps_a[PSI][:],
                                    wk[:, 512 * dd + 128 * nq:512 * dd + 128 * (nq + 1)],
                                    src_ar[:, dd * TP + 512 * th:dd * TP + 512 * (th + 1)],
                                    start=(dd == 0), stop=False))
                            S_.i("te", lambda e, th=th, PSI=PSI: e.matmul(
                                ps_a[PSI][:], nego[0:1, 0:128],
                                rsx[0:1, 512 * th:512 * (th + 1)],
                                start=False, stop=True), [(s["mm"], 1)])
                            S_.w("sc", s["mm"], s["mm"].n)
                            S_.w("sc", s["ev"], s["ev"].n)
                            S_.i("sc", lambda e, PSI=PSI, stg=stg: e.activation(
                                asb[PSI][:], ps_a[PSI][:], AF.Relu,
                                scale=s2x[:, stg:stg + 1]), [(s["sq"], 1)])
                            S_.w("v", s["sq"], s["sq"].n)
                            dst = dst_ar[:, nch * TP + 512 * th:nch * TP + 512 * (th + 1)]
                            S_.i("v", lambda e, dst=dst, PSI=PSI, nch=nch: e.tensor_scalar(
                                dst, asb[PSI][:], eisb[:, nch:nch + 1], None,
                                ALU.mult), [(s["ev"], 1)])
                            if DBG:
                                S_.w("sy", s["ev"], s["ev"].n)
                                S_.i("sy", lambda e, stg=stg, nch=nch, th=th, dst=dst: e.dma_start(
                                    out=dbgs[stg][:, nch * TP + 512 * th:nch * TP + 512 * (th + 1)],
                                    in_=dst), [(s["stq"], 16)])
                                if stg == 0:
                                    S_.i("sy", lambda e, nch=nch, th=th, PSI=PSI: e.dma_start(
                                        out=dbgb[:, nch * TP + 512 * th:nch * TP + 512 * (th + 1)],
                                        in_=asb[PSI][:]), [(s["stq"], 16)])

            if PH < 3:
                raise StopSched
            rowsums(arB, 2)
            RSE = s["rse"].n
            for ech in range(8):
                li = 16 + ech
                S_.w("g", s["mm"], s["mm"].n)
                S_.w("g", s["sq"], s["sq"].n)
                S_.i("g", lambda e, ech=ech: e.dma_start(
                    out=ku[0][:], in_=kts[2][ech, :, :, :]), [(s["ldk"], 16)])
                S_.w("sc", s["ldk"], 16 * (li + 1))
                S_.w("sc", s["mm"], s["mm"].n)
                S_.i("sc", lambda e, ech=ech: e.activation(
                	wk[:], ku[0][:], AF.Copy, scale=2.0), [(s["sq"], 1)])
                for tch in range(NT):
                    i0 = ech * NT + tch
                    S_.w("te", s["sq"], s["sq"].n)
                    S_.w("te", s["rse"], RSE)
                    S_.w("te", s["tre"], s["tre"].n)
                    S_.w("te", s["ev"], s["ev"].n)
                    PSI = s["mm"].n % 2
                    for dd in range(32):
                        S_.i("te", lambda e, dd=dd, tch=tch, PSI=PSI: e.matmul(
                            ps_a[PSI][:],
                            arB[:, dd * TP + 128 * tch:dd * TP + 128 * (tch + 1)],
                            wk[:, 512 * dd:512 * (dd + 1)],
                            start=(dd == 0), stop=False))
                    S_.i("te", lambda e, tch=tch, PSI=PSI: e.matmul(
                        ps_a[PSI][:], rsx[0:1, 128 * tch:128 * (tch + 1)],
                        nego[0:1, 0:512], start=False, stop=True), [(s["mm"], 1)])
                    S_.w("v", s["mm"], s["mm"].n)
                    S_.w("v", s["sts"], s["sts"].n)
                    S_.i("v", lambda e, PSI=PSI: e.tensor_scalar(
                        asb[PSI][:], ps_a[PSI][:], s2x[:, 2:3], None, ALU.mult),
                        [(s["ev"], 1)])
                    S_.w("sy", s["ev"], s["ev"].n)
                    S_.i("sy", lambda e, ech=ech, tch=tch, PSI=PSI: e.dma_start(
                        out=z2d[128 * tch:128 * (tch + 1), 512 * ech:512 * (ech + 1)],
                        in_=asb[PSI][:]), [(s["sts"], 16)])

            if PH < 4:
                raise StopSched
            NSTS2 = s["sts"].n
            prev_vx2 = 0
            for tch in range(NT):
                S_.w("sy", s["sts"], NSTS2)
                S_.w("sy", s["vch"], prev_vx2)
                S_.i("sy", lambda e, tch=tch: e.dma_start(
                    out=slab1[:], in_=z2d[128 * tch:128 * (tch + 1), :]),
                    [(s["ldz"], 16)])
                S_.w("g", s["stx"], s["stx"].n)
                S_.w("g", s["trp"], s["trp"].n)
                S_.w("g", s["tre"], s["tre"].n)
                S_.i("g", lambda e, tch=tch: e.dma_start(
                    out=slab2[:], in_=x1d[128 * tch:128 * (tch + 1), :]),
                    [(s["ldx1"], 16)])
                S_.w("v", s["ldz"], s["ldz"].n)
                S_.w("v", s["ldx1"], s["ldx1"].n)
                S_.edge("v", lambda e: e.tensor_tensor(slab1[:], slab1[:], slab2[:],
                                                       ALU.add), s["vch"])
                S_.edge("v", lambda e: e.reduce_sum(sums[:], slab1[:], axis=AX.X),
                        s["vch"])
                S_.w("sc", s["vch"], s["vch"].n)
                S_.i("sc", lambda e: e.activation(slab2[:], slab1[:], AF.Square),
                     [(s["sq"], 1)])
                S_.w("v", s["sq"], s["sq"].n)
                S_.edge("v", lambda e: e.reduce_sum(sqs[:], slab2[:], axis=AX.X),
                        s["vch"])
                S_.edge("v", lambda e: e.tensor_scalar(mu[:], sums[:], 1.0 / D, None,
                                                       ALU.mult), s["vch"])
                S_.edge("v", lambda e: e.tensor_tensor(var[:], mu[:], mu[:],
                                                       ALU.mult), s["vch"])
                S_.edge("v", lambda e: e.scalar_tensor_tensor(
                    var[:], sqs[:], 1.0 / D, var[:], ALU.mult, ALU.subtract),
                    s["vch"])
                S_.w("sc", s["vch"], s["vch"].n)
                S_.edge("sc", lambda e: e.activation(var[:], var[:], AF.Ln,
                                                     bias=epst[:, 0:1]), s["sch"])
                S_.i("sc", lambda e: e.activation(rr[:], var[:], AF.Exp, scale=-0.5),
                     [(s["sq"], 1)])
                S_.w("v", s["sq"], s["sq"].n)
                S_.edge("v", lambda e: e.tensor_scalar(slab2[:], slab1[:], mu[:, 0:1],
                                                       rr[:, 0:1], ALU.subtract,
                                                       ALU.mult), s["vch"])
                prev_vx2 = s["vch"].n
                S_.w("sy", s["vch"], s["vch"].n)
                S_.i("sy", lambda e, tch=tch: e.dma_start(
                    out=x2tm[128 * tch:128 * (tch + 1), :], in_=slab2[:]),
                    [(s["stx"], 16)])
                for j4 in range(8):
                    S_.w("te", s["vch"], s["vch"].n)
                    S_.w("te", s["tre"], s["tre"].n)
                    for j in range(4):
                        S_.i("te", lambda e, j4=j4, j=j: e.matmul(
                            ps_t[:, 128 * j:128 * (j + 1)],
                            slab2[:, 512 * j4 + 128 * j:512 * j4 + 128 * (j + 1)],
                            idb[:], is_transpose=True, start=(j == 0), stop=(j == 3)),
                            [(s["trp"], 1)] if j == 3 else ())
                    S_.w("sc", s["trp"], s["trp"].n)
                    dst = arA[:, :].rearrange("p (c t) -> p c t", c=32)[
                        :, 4 * j4:4 * (j4 + 1), 128 * tch:128 * (tch + 1)]
                    S_.i("sc", lambda e, dst=dst: e.copy(
                        dst, ps_t[:].rearrange("p (c t) -> p c t", c=4)),
                        [(s["tre"], 1)])

            if PH < 5:
                raise StopSched
            for proj, wsrc, bias, nchk, outd in ((0, wqct, bsb, 12, qc),
                                                 (1, wkvct, bsb2, 5, kvc)):
                for ec in range(nchk):
                    mrows = 128 if not (proj == 1 and ec == 4) else 64
                    S_.w("g", s["mm"], s["mm"].n)
                    S_.i("g", lambda e, wsrc=wsrc, ec=ec, mrows=mrows: e.dma_start(
                        out=wk[:, 0:32 * mrows],
                        in_=wsrc.rearrange("(c p) q -> p c q", p=128)
                        [:, :, 128 * ec:128 * ec + mrows]), [(s["ldw"], 16)])
                    for th in range(TP // 512):
                        S_.w("te", s["ldw"], 16 * (8 + ec + 1 + (12 if proj else 0)))
                        S_.w("te", s["tre"], s["tre"].n)
                        S_.w("te", s["ev"], s["ev"].n)
                        PSI = s["mm"].n % 2
                        for dd in range(32):
                            S_.i("te", lambda e, dd=dd, th=th, PSI=PSI, mrows=mrows: e.matmul(
                                ps_a[PSI][0:mrows, :],
                                wk[:, mrows * dd:mrows * (dd + 1)],
                                arA[:, dd * TP + 512 * th:dd * TP + 512 * (th + 1)],
                                start=(dd == 0), stop=(dd == 31)),
                                [(s["mm"], 1)] if dd == 31 else ())
                        S_.w("v", s["mm"], s["mm"].n)
                        S_.w("v", s["sts"], s["sts"].n)
                        S_.i("v", lambda e, PSI=PSI, bias=bias, ec=ec, mrows=mrows: e.tensor_scalar(
                            asb[PSI][0:mrows, :], ps_a[PSI][0:mrows, :],
                            bias[0:mrows, ec:ec + 1], None, ALU.add), [(s["ev"], 1)])
                        S_.w("sy", s["ev"], s["ev"].n)
                        S_.i("sy", lambda e, outd=outd, ec=ec, th=th, PSI=PSI, mrows=mrows: e.dma_start(
                            out=outd[128 * ec:128 * ec + mrows, 512 * th:512 * (th + 1)],
                            in_=asb[PSI][0:mrows, :]), [(s["sts"], 16)])

        except StopSched:
            pass
        if DBG:
            S_.w("sy", s["ev"], s["ev"].n)
            S_.w("sy", s["tre"], s["tre"].n)
            S_.w("sy", s["mm"], s["mm"].n)
            S_.i("sy", lambda e: e.dma_start(out=dbga[:, :], in_=arA[:]),
                 [(s["stq"], 16)])
        S_.w("sy", s["sts"], s["sts"].n)
        S_.w("sy", s["stx"], s["stx"].n)
        S_.w("sy", s["stq"], s["stq"].n)

        blk = ctx.enter_context(nc.Block())

        @blk.gpsimd
        def _(e):
            S_.replay("g", e)

        @blk.sync
        def _(e):
            S_.replay("sy", e)

        @blk.tensor
        def _(e):
            S_.replay("te", e)

        @blk.vector
        def _(e):
            S_.replay("v", e)

        @blk.scalar
        def _(e):
            S_.replay("sc", e)

    return nc



def build_D(SD=S):
    nc = _bass()
    TB = B * SD
    NQT = SD // 128
    NKT = SD // 512
    NTH = SD // 512

    qcb = nc.dram_tensor("qcb", [QL, TB], BF16, kind="ExternalInput").ap()
    kvcb = nc.dram_tensor("kvcb", [576, TB], BF16, kind="ExternalInput").ap()
    wqupt = nc.dram_tensor("wqupt", [QL, 768], F32, kind="ExternalInput").ap()
    bqup = nc.dram_tensor("bqup", [128, 8], F32, kind="ExternalInput").ap()
    wkvupt = nc.dram_tensor("wkvupt", [KVL, 1024], F32, kind="ExternalInput").ap()
    bkvup = nc.dram_tensor("bkvup", [128, 8], F32, kind="ExternalInput").ap()
    op = nc.dram_tensor("op", [B, SD, HD], F32, kind="ExternalOutput").ap()

    ctx = contextlib.ExitStack()
    with ctx:
        sb = lambda *a: ctx.enter_context(nc.sbuf_tensor(*a))
        pt = lambda *a: ctx.enter_context(nc.psum_tensor(*a))
        qah = sb("qah", [128, 2 * SD], BF16)
        kvh = sb("kvh", [128, 2 * SD], BF16)
        krop = sb("krop", [64, SD], BF16)
        vtm = sb("vtmD", [128, SD], BF16)
        wq = sb("wqD", [128, 12 * 768], BF16)
        wkv = sb("wkvD", [128, 4 * 1024], BF16)
        bqs = sb("bqsD", [128, 8], F32)
        bkvs = sb("bkvsD", [128, 8], F32)
        qcs = sb("qcsD", [128, 12 * 512], BF16)
        idf = sb("idfD", [128, 128], F32)
        idb = sb("idbD", [128, 128], BF16)
        psb = sb("psbD", [128, 512], BF16)
        ptsb = sb("ptsbD", [128, 512], BF16)
        osum = sb("osumD", [128, SD], F32)
        oacc = sb("oaccD", [128, HD], F32)
        mrun = sb("mrunD", [128, 1], F32)
        mnew = sb("mnewD", [128, 1], F32)
        psmax = sb("psmaxD", [128, 1], F32)
        biast = sb("biastD", [128, 1], F32)
        corr = sb("corrD", [128, 1], F32)
        mdif = sb("mdifD", [128, 1], F32)
        tsum = sb("tsumD", [128, 1], F32)
        dnr = sb("dnrD", [128, 1], F32)
        rcp = sb("rcpD", [128, 1], F32)

        ps_p = [pt(f"pspD{i}", [128, 512], F32) for i in range(2)]
        ps_s = [pt(f"pssD{i}", [128, 512], F32) for i in range(2)]
        ps_pt = pt("psptD", [128, 512], BF16)
        ps_o = pt("psoD", [128, HD], F32)

        s = make_sems(ctx, nc, [
            "ldw", "ldk", "ldb", "ldq", "pjm", "pev", "scm", "sadd", "scorr",
            "sexp", "sdn", "strp", "spte", "sres", "sav", "osu", "ost",
            "vch", "sch", "idr", "gch"])

        S_ = Sched()

        S_.i("g", lambda e: e.iota(idf[:], pattern=[[1, 128]], base=0,
                                   channel_multiplier=-1,
                                   allow_small_or_imprecise_dtypes=True),
             [(s["gch"], 1)])
        S_.i("g", lambda e: e.dma_start(
            out=wq[:], in_=wqupt.rearrange("(c p) q -> p c q", p=128)),
            [(s["ldw"], 16)])
        S_.i("g", lambda e: e.dma_start(
            out=wkv[:], in_=wkvupt.rearrange("(c p) q -> p c q", p=128)),
            [(s["ldw"], 16)])
        S_.i("sy", lambda e: e.dma_start(out=bqs[:], in_=bqup[:, :]),
             [(s["ldb"], 16)])
        S_.i("sy", lambda e: e.dma_start(out=bkvs[:], in_=bkvup[:, :]),
             [(s["ldb"], 16)])
        S_.w("v", s["gch"], 1)
        S_.edge("v", lambda e: e.tensor_scalar(idf[:], idf[:], 0.0, None,
                                               ALU.is_equal), s["vch"])
        S_.i("v", lambda e: e.tensor_copy(idb[:], idf[:]), [(s["idr"], 1)])

        nload = 0
        for b in range(B):
            if b > 0:
                S_.w("g", s["scm"], s["scm"].n)
                S_.w("v", s["ost"], 16 * b)
            S_.i("g", lambda e, b=b: e.dma_start(
                out=krop[:], in_=kvcb[KVL:576, SD * b:SD * (b + 1)]),
                [(s["ldk"], 16)])
            for h in range(HPC):
                for which, nkch, wsb_, bias_, dst_ar in (
                        (0, 12, wq, bqs, qah), (1, 4, wkv, bkvs, kvh)):
                    for th in range(NTH):
                        S_.w("sy", s["pjm"], s["pjm"].n)
                        S_.w("sy", s["ldb"], 32)
                        if which == 0:
                            src = qcb.rearrange("(c p) t -> p c t", p=128)[
                                :, :, SD * b + 512 * th:SD * b + 512 * (th + 1)]
                            dwid = 12
                        else:
                            src = kvcb[0:KVL].rearrange("(c p) t -> p c t", p=128)[
                                :, :, SD * b + 512 * th:SD * b + 512 * (th + 1)]
                            dwid = 4
                        S_.i("sy", lambda e, src=src, dwid=dwid: e.dma_start(
                            out=qcs[:, 0:dwid * 512].rearrange(
                                "p (c x) -> p c x", x=512), in_=src),
                            [(s["ldq"], 16)])
                        nload += 1
                        for sub in range(2):
                            if which == 0:
                                mof = 192 * h + 128 * sub
                                mw = 128 if sub == 0 else 64
                            else:
                                mof = 256 * h + 128 * sub
                                mw = 128
                            ppi = s["pjm"].n % 2
                            S_.w("te", s["ldw"], 32)
                            S_.w("te", s["ldq"], 16 * nload)
                            S_.w("te", s["pev"], max(0, s["pjm"].n - 1))
                            if th == 0 and sub == 0:
                                S_.w("te", s["scm"], s["scm"].n)
                            for kc in range(nkch):
                                S_.i("te", lambda e, wsb_=wsb_, kc=kc, mof=mof,
                                     mw=mw, ppi=ppi, which=which,
                                     nkch=nkch: e.matmul(
                                    ps_p[ppi][0:mw, :],
                                    wsb_[:, kc * (768 if which == 0 else 1024) + mof:
                                         kc * (768 if which == 0 else 1024) + mof + mw],
                                    qcs[:, kc * 512:(kc + 1) * 512],
                                    start=(kc == 0), stop=(kc == nkch - 1)),
                                    [(s["pjm"], 1)] if kc == nkch - 1 else ())
                            bcol = 2 * h + sub
                            S_.w("v", s["pjm"], s["pjm"].n)
                            if th == 0 and sub == 0 and (b > 0 or h > 0):
                                pass
                            S_.i("v", lambda e, dst_ar=dst_ar, mw=mw, th=th,
                                 ppi=ppi, bias_=bias_, bcol=bcol, sub=sub,
                                 SD=SD: e.tensor_scalar(
                                dst_ar[0:mw, sub * SD + 512 * th:
                                       sub * SD + 512 * (th + 1)],
                                ps_p[ppi][0:mw, :], bias_[0:mw, bcol:bcol + 1],
                                None, ALU.add), [(s["pev"], 1)])
                for gb in range(SD // 512):
                    S_.w("te", s["pev"], s["pev"].n)
                    S_.w("te", s["idr"], 1)
                    S_.w("te", s["spte"], s["spte"].n)
                    for j in range(4):
                        S_.i("te", lambda e, gb=gb, j=j: e.matmul(
                            ps_pt[:, 128 * j:128 * (j + 1)],
                            kvh[:, SD + 512 * gb + 128 * j:SD + 512 * gb + 128 * (j + 1)],
                            idb[:], is_transpose=True,
                            start=(j == 0), stop=(j == 3)),
                            [(s["strp"], 1)] if j == 3 else ())
                    S_.w("sc", s["strp"], s["strp"].n)
                    S_.w("sc", s["sav"], s["sav"].n)
                    S_.i("sc", lambda e, gb=gb: e.copy(
                        vtm[:, 512 * gb:512 * (gb + 1)], ps_pt[:]), [(s["spte"], 1)])
                for qt in range(NQT):
                    S_.edge("v", lambda e: e.memset(mrun[:], -1e30), s["vch"])
                    S_.edge("v", lambda e: e.memset(dnr[:], 0.0), s["vch"])
                    for kt in range(NKT):
                        pp = ps_s[s["scm"].n % 2]
                        S_.w("te", s["ldk"], 16 * (b + 1))
                        S_.w("te", s["sexp"], max(0, s["sexp"].n - 1))
                        S_.i("te", lambda e, pp=pp, qt=qt, kt=kt: e.matmul(
                            pp[:], qah[:, 128 * qt:128 * (qt + 1)],
                            kvh[:, 512 * kt:512 * (kt + 1)],
                            start=True, stop=False))
                        S_.i("te", lambda e, pp=pp, qt=qt, kt=kt: e.matmul(
                            pp[:], qah[0:64, SD + 128 * qt:SD + 128 * (qt + 1)],
                            krop[0:64, 512 * kt:512 * (kt + 1)],
                            start=False, stop=True), [(s["scm"], 1)])
                        S_.w("v", s["scm"], s["scm"].n)
                        S_.edge("v", lambda e, pp=pp: e.reduce_max(
                            psmax[:], pp[:], axis=AX.X), s["vch"])
                        S_.edge("v", lambda e: e.tensor_scalar(
                            mnew[:], psmax[:], SC1, None, ALU.mult), s["vch"])
                        S_.edge("v", lambda e: e.tensor_tensor(
                            mnew[:], mrun[:], mnew[:], ALU.max), s["vch"])
                        S_.edge("v", lambda e: e.tensor_tensor(
                            mdif[:], mrun[:], mnew[:], ALU.subtract), s["vch"])
                        S_.w("v", s["sexp"], s["sexp"].n)
                        S_.edge("v", lambda e: e.tensor_scalar(
                            biast[:], mnew[:], -1.0, None, ALU.mult), s["vch"])
                        S_.i("v", lambda e: e.tensor_copy(mrun[:], mnew[:]),
                             [(s["sadd"], 1)])
                        S_.w("sc", s["sadd"], s["sadd"].n)
                        S_.w("sc", s["sdn"], s["sdn"].n)
                        S_.w("sc", s["sres"], s["sres"].n)
                        S_.i("sc", lambda e: e.activation(
                            corr[:], mdif[:], AF.Exp), [(s["scorr"], 1)])
                        S_.w("sc", s["strp"], s["strp"].n)
                        S_.i("sc", lambda e, pp=pp: e.activation(
                            psb[:], pp[:], AF.Exp, bias=biast[:, 0:1], scale=SC1,
                            accum_out=tsum[:, 0:1]), [(s["sexp"], 1)])
                        S_.w("v", s["sexp"], s["sexp"].n)
                        S_.w("v", s["scorr"], s["scorr"].n)
                        S_.i("v", lambda e: e.scalar_tensor_tensor(
                            dnr[:], dnr[:], corr[:, 0:1], tsum[:], ALU.mult,
                            ALU.add), [(s["sdn"], 1)])
                        S_.w("v", s["sdn"], s["sdn"].n)
                        S_.w("te", s["sexp"], s["sexp"].n)
                        S_.w("te", s["spte"], s["spte"].n)
                        for j in range(4):
                            S_.i("te", lambda e, j=j: e.matmul(
                                ps_pt[:, 128 * j:128 * (j + 1)],
                                psb[:, 128 * j:128 * (j + 1)], idb[:],
                                is_transpose=True, start=(j == 0), stop=(j == 3)),
                                [(s["strp"], 1)] if j == 3 else ())
                        S_.w("sc", s["strp"], s["strp"].n)
                        S_.w("sc", s["sav"], s["sav"].n)
                        S_.i("sc", lambda e: e.copy(ptsb[:], ps_pt[:]),
                             [(s["spte"], 1)])
                        S_.w("te", s["spte"], s["spte"].n)
                        S_.w("te", s["sres"], s["sres"].n)
                        for j in range(4):
                            S_.i("te", lambda e, kt=kt, j=j: e.matmul(
                                ps_o[:], ptsb[:, 128 * j:128 * (j + 1)],
                                vtm[:, (4 * kt + j) * 128:(4 * kt + j + 1) * 128],
                                start=(j == 0), stop=(j == 3)),
                                [(s["sav"], 1)] if j == 3 else ())
                        S_.w("v", s["sav"], s["sav"].n)
                        S_.w("v", s["scorr"], s["scorr"].n)
                        if kt == 0:
                            S_.i("v", lambda e: e.tensor_copy(oacc[:], ps_o[:]),
                                 [(s["sres"], 1)])
                        else:
                            S_.i("v", lambda e: e.scalar_tensor_tensor(
                                oacc[:], oacc[:], corr[:, 0:1], ps_o[:],
                                ALU.mult, ALU.add), [(s["sres"], 1)])
                        S_.w("v", s["sres"], s["sres"].n)
                    S_.w("v", s["sdn"], s["sdn"].n)
                    S_.edge("v", lambda e: e.reciprocal(rcp[:], dnr[:]), s["vch"])
                    if h == 0:
                        S_.i("v", lambda e, qt=qt: e.tensor_scalar(
                            osum[:, 128 * qt:128 * (qt + 1)], oacc[:],
                            rcp[:, 0:1], None, ALU.mult), [(s["osu"], 1)])
                    else:
                        S_.i("v", lambda e, qt=qt: e.scalar_tensor_tensor(
                            osum[:, 128 * qt:128 * (qt + 1)], oacc[:],
                            rcp[:, 0:1], osum[:, 128 * qt:128 * (qt + 1)],
                            ALU.mult, ALU.add), [(s["osu"], 1)])
            S_.w("sy", s["osu"], s["osu"].n)
            S_.i("sy", lambda e, b=b: e.dma_start(
                out=op[b].rearrange("(c p) v -> p c v", p=128),
                in_=osum[:, :].rearrange("p (c v) -> p c v", v=HD)),
                [(s["ost"], 16)])
        S_.w("sy", s["ost"], 16 * B)

        blk = ctx.enter_context(nc.Block())

        @blk.gpsimd
        def _(e):
            S_.replay("g", e)

        @blk.sync
        def _(e):
            S_.replay("sy", e)

        @blk.tensor
        def _(e):
            S_.replay("te", e)

        @blk.vector
        def _(e):
            S_.replay("v", e)

        @blk.scalar
        def _(e):
            S_.replay("sc", e)

    return nc


def stage_B(inputs, ares):
    slopes = _alibi_slopes()
    stall = np.concatenate([ares[j]["stats"] for j in range(NCORES)], axis=0)
    stsel = np.zeros((32, 4), np.float32)
    for j in range(NCORES):
        for si in range(4):
            stsel[4 * j + si, si] = 1.0
    bsel = np.zeros((4, 4 * 128), np.float32)
    for si in range(4):
        bsel[si, si * 128:(si + 1) * 128] = 1.0
    qw = inputs["L0_attn_q_ln_w"].astype(np.float32)
    qb = inputs["L0_attn_q_ln_b"].astype(np.float32)
    kw = inputs["L0_attn_k_ln_w"].astype(np.float32)
    kb = inputs["L0_attn_k_ln_b"].astype(np.float32)
    in_maps = []
    for c in range(NCORES):
        heads = _core_heads(c)
        lnwb = np.zeros((128, 16), np.float32)
        for hl, h in enumerate(heads):
            lnwb[:, 4 * hl + 0] = qw[128 * h:128 * (h + 1)]
            lnwb[:, 4 * hl + 1] = qb[128 * h:128 * (h + 1)]
            lnwb[:, 4 * hl + 2] = kw[128 * h:128 * (h + 1)]
            lnwb[:, 4 * hl + 3] = kb[128 * h:128 * (h + 1)]
        slb = np.broadcast_to(slopes[heads][None, :] / SC0, (128, 4)).copy()
        slc = np.broadcast_to(slopes[heads][None, :], (128, 4)).copy()
        in_maps.append({"qkvc": ares[c]["qkvc"], "stall": stall, "stsel": stsel,
                        "bsel": bsel, "lnwb": lnwb, "slb": slb, "slc": slc})
    return in_maps


def stage_C(inputs, ares, bres):
    o_full = np.empty((T, D), np.float32)
    for j in range(NCORES):
        for hl, h in enumerate(_core_heads(j)):
            for b in range(B):
                o_full[b * S:(b + 1) * S, h * HD:(h + 1) * HD] = \
                    bres[j]["oo"][b * HPC + hl]
    x = inputs["L0_attn_hidden_states"].reshape(T, D).astype(np.float32, copy=False)
    wot = np.ascontiguousarray(inputs["L0_attn_Wout"].T)
    kts = []
    for nm in ("L0_ffn_k0", "L0_ffn_k1", "L0_ffn_k2"):
        k = inputs[nm].view(np.uint8) if inputs[nm].dtype == np.bool_ else inputs[nm].astype(np.uint8)
        kts.append(np.ascontiguousarray(
            k.reshape(32, 128, 8, 512).transpose(2, 1, 0, 3)))
    s0b = np.broadcast_to(np.stack([
        np.float32(inputs["L0_ffn_s0"]), np.float32(inputs["L0_ffn_s1"]),
        np.float32(inputs["L0_ffn_s2"]), np.float32(0.0)])[None, :], (128, 4)).copy()
    eic = np.where(np.arange(D) >= NE, np.float32(EI_NEG),
                   np.float32(1.0)).reshape(32, 128).T.copy()
    wqct = np.ascontiguousarray(inputs["L1_attn_Wq_c"].T)
    bqc = inputs["L1_attn_bq_c"].astype(np.float32).reshape(12, 128).T.copy()
    wkvct = np.ascontiguousarray(inputs["L1_attn_Wkv_c"].T)
    bkvc = np.pad(inputs["L1_attn_bkv_c"].astype(np.float32),
                  (0, 64)).reshape(5, 128).T.copy()
    in_maps = []
    for c in range(NCORES):
        tsl = slice(TPC * c, TPC * (c + 1))
        in_maps.append({
            "ofm": np.ascontiguousarray(o_full[tsl].T),
            "xtm": np.ascontiguousarray(x[tsl]),
            "wot": wot, "k0t": kts[0], "k1t": kts[1], "k2t": kts[2],
            "s0b": s0b, "eic": eic, "wqct": wqct, "bqc": bqc,
            "wkvct": wkvct, "bkvc": bkvc,
        })
    return in_maps


def stage_D(inputs, cres):
    qcb = np.concatenate([cres[c]["qc"] for c in range(NCORES)], 1)
    kvcb = np.concatenate([cres[c]["kvc"] for c in range(NCORES)], 1)
    wqupT = inputs["L1_attn_Wq_up"].T.astype(np.float32, copy=False)
    bqu = inputs["L1_attn_bq_up"].astype(np.float32, copy=False)
    wkvupT = inputs["L1_attn_Wkv_up"].T.astype(np.float32, copy=False)
    bkvu = inputs["L1_attn_bkv_up"].astype(np.float32, copy=False)
    in_maps = []
    for c in range(NCORES):
        bqup = np.zeros((128, 8), np.float32)
        bkvup = np.zeros((128, 8), np.float32)
        for hh in range(HPC):
            gh = HPC * c + hh
            bqup[:, 2 * hh] = bqu[192 * gh:192 * gh + 128]
            bqup[0:64, 2 * hh + 1] = bqu[192 * gh + 128:192 * gh + 192]
            bkvup[:, 2 * hh] = bkvu[256 * gh:256 * gh + 128]
            bkvup[:, 2 * hh + 1] = bkvu[256 * gh + 128:256 * gh + 256]
        in_maps.append({
            "qcb": qcb, "kvcb": kvcb,
            "wqupt": np.ascontiguousarray(wqupT[:, 768 * c:768 * (c + 1)]),
            "bqup": bqup,
            "wkvupt": np.ascontiguousarray(wkvupT[:, 1024 * c:1024 * (c + 1)]),
            "bkvup": bkvup,
        })
    return in_maps


def stage_E(inputs, ares, cres, dres):
    dqg, dqu, dqd = [], [], []
    for j in range(NCORES):
        a, b = core_inter_range(j)
        w = b - a
        dqg.append(ares[j]["dqg"][:, :w])
        dqu.append(ares[j]["dqu"][:, :w])
        dqd.append(ares[j]["dqd"][:w, :])
    DQG = np.concatenate(dqg, 1)
    DQU = np.concatenate(dqu, 1)
    DQD = np.concatenate(dqd, 0)
    wgt = np.ascontiguousarray(
        DQG.reshape(32, 128, NG, 128).transpose(2, 1, 0, 3)).reshape(
        NG, 128, 32 * 128)
    wut = np.ascontiguousarray(
        DQU.reshape(32, 128, NG, 128).transpose(2, 1, 0, 3)).reshape(
        NG, 128, 32 * 128)
    wdt = np.ascontiguousarray(
        DQD.reshape(NG, 128, 8, 512).transpose(0, 2, 1, 3))
    wo = np.ascontiguousarray(inputs["L1_attn_Wo"].T.astype(np.float32,
                                                            copy=False))
    bo = inputs["L1_attn_bo"].astype(np.float32, copy=False).reshape(1, D)
    in_maps = []
    for c in range(NCORES):
        b_, q_ = c // 4, c % 4
        ops = np.ascontiguousarray(np.stack(
            [dres[j]["op"].reshape(B, S, HD)[b_, TPC * q_:TPC * (q_ + 1), :].T
             for j in range(NCORES)]))
        in_maps.append({
            "ops": ops, "x2tm_in": cres[c]["x2tm"], "wo": wo, "bo": bo,
            "wgt": wgt, "wut": wut, "wdt": wdt,
        })
    return in_maps


def run_A(inputs):
    nc = build_A()
    in_maps = stage_A(inputs)
    res = run_bass_kernel_spmd(nc, in_maps, core_ids=list(range(NCORES)))
    return res.results



def kernel(**inputs):
    cores = list(range(NCORES))
    resA = run_bass_kernel_spmd(build_A(), stage_A(inputs), core_ids=cores)
    ares = [resA.results[c] for c in cores]
    resB = run_bass_kernel_spmd(build_B(), stage_B(inputs, ares),
                                core_ids=cores)
    bres = [resB.results[c] for c in cores]
    resC = run_bass_kernel_spmd(build_C(), stage_C(inputs, ares, bres),
                                core_ids=cores)
    cres = [resC.results[c] for c in cores]
    resD = run_bass_kernel_spmd(build_D(), stage_D(inputs, cres),
                                core_ids=cores)
    dres = [resD.results[c] for c in cores]
    resE = run_bass_kernel_spmd(build_E(), stage_E(inputs, ares, cres, dres),
                                core_ids=cores)
    out = np.concatenate(
        [np.asarray(resE.results[c]["out"], dtype=np.float32).reshape(TPC, D)
         for c in cores], 0)
    return out.reshape(B, S, D)


# version 02
# speedup vs baseline: 1.2971x; 1.2971x over previous
import math
import contextlib
import numpy as np
import concourse.bass as bass
import concourse.mybir as mybir
from concourse.bass_utils import run_bass_kernel_spmd

F32 = mybir.dt.float32
BF16 = mybir.dt.bfloat16
I32 = mybir.dt.int32
U8 = mybir.dt.uint8
AF = mybir.ActivationFunctionType
ALU = mybir.AluOpType
AX = mybir.AxisListType

NCORES = 8
D = 4096
S = 4096
B = 2
T = B * S
NH = 32
HD = 128
HPC = NH // NCORES
TPC = T // NCORES
CLIP = 8.0
LN_EPS = 1e-5
SC0 = 1.0 / math.sqrt(HD)
NEGBIG = -1.0e9

QL, KVL, ROPE, NOPE, QH, VH = 1536, 512, 64, 128, 192, 128
SC1 = 1.0 / math.sqrt(QH)
INTER = 11008
GRP = 128
NG = INTER // GRP
NGC = 11
NPC = NGC * GRP
NE = 3277
EI_NEG = -4.0


def _alibi_slopes():
    m = np.arange(1, NH + 1, dtype=np.float32) * (8.0 / NH)
    return (1.0 / np.power(2.0, m)).astype(np.float32)


class Sem:
    def __init__(self, handle):
        self.h = handle
        self.n = 0

    def inc(self, instr, k=1):
        self.n += k
        instr.then_inc(self.h, k)


def make_sems(ctx, nc, names):
    return {n: Sem(ctx.enter_context(nc.semaphore(n))) for n in names}


def edge(eng, sem, instr):
    sem.n += 1
    instr.then_inc(sem.h, 1)
    eng.wait_ge(sem.h, sem.n)


def _nchunks(total, w=512):
    out = []
    o = 0
    while o < total:
        cw = min(w, total - o)
        out.append((o, cw))
        o += cw
    return out



RACE_DETECT = [False]


def _bass():
    return bass.Bass("TRN2", target_bir_lowering=False, debug=False,
                     detect_race_conditions=RACE_DETECT[0])


def build_A(TC=16, MATS=("g", "u", "d"), KTO=None):
    nc = _bass()
    EC = 12
    TW = 512

    xfm = nc.dram_tensor("xfm", [D, T], F32, kind="ExternalInput").ap()
    wq = nc.dram_tensor("wq", [D, EC * 128], F32, kind="ExternalInput").ap()
    qwg = nc.dram_tensor("qwg", [D // 8, NPC], I32, kind="ExternalInput").ap()
    scg = nc.dram_tensor("scg", [32, NPC], F32, kind="ExternalInput").ap()
    qzg = nc.dram_tensor("qzg", [32, NPC // 8], I32, kind="ExternalInput").ap()
    gig = nc.dram_tensor("gig", [32, D], I32, kind="ExternalInput").ap()
    qwu = nc.dram_tensor("qwu", [D // 8, NPC], I32, kind="ExternalInput").ap()
    scu = nc.dram_tensor("scu", [32, NPC], F32, kind="ExternalInput").ap()
    qzu = nc.dram_tensor("qzu", [32, NPC // 8], I32, kind="ExternalInput").ap()
    qwd = nc.dram_tensor("qwd", [NPC // 8, D], I32, kind="ExternalInput").ap()
    scd = nc.dram_tensor("scd", [NGC, D], F32, kind="ExternalInput").ap()
    qzd = nc.dram_tensor("qzd", [NGC, D // 8], I32, kind="ExternalInput").ap()
    gid = nc.dram_tensor("gid", [NGC, NPC], I32, kind="ExternalInput").ap()
    gidv = nc.dram_tensor("gidv", [NGC, 1], F32, kind="ExternalInput").ap()
    shifts = nc.dram_tensor("shifts", [128, 1], I32, kind="ExternalInput").ap()

    qkvc = nc.dram_tensor("qkvc", [EC * 128, T], F32, kind="ExternalOutput").ap()
    stats = nc.dram_tensor("stats", [4, T], F32, kind="ExternalOutput").ap()
    dqg = nc.dram_tensor("dqg", [D, NPC], BF16, kind="ExternalOutput").ap()
    dqu = nc.dram_tensor("dqu", [D, NPC], BF16, kind="ExternalOutput").ap()
    dqd = nc.dram_tensor("dqd", [NPC, D], BF16, kind="ExternalOutput").ap()

    DQT = []
    KT = {"g": 32, "u": 32, "d": NGC}
    if KTO:
        KT.update(KTO)
    NCOLS = {"g": NPC, "u": NPC, "d": D}
    for mat in MATS:
        for kt in range(KT[mat]):
            for (c0, cw) in _nchunks(NCOLS[mat]):
                DQT.append((mat, kt, c0, cw))
    DQK = []
    KT_END = []
    cnt = 0
    for mat in MATS:
        nchk = len(_nchunks(NCOLS[mat]))
        for kt in range(KT[mat]):
            DQK.append((mat, kt))
            cnt += nchk
            KT_END.append(cnt)
    TILES_BEFORE = {}
    MATIDX = {}
    cnt = 0
    for mi, mat in enumerate(MATS):
        TILES_BEFORE[mat] = cnt
        MATIDX[mat] = mi
        cnt += KT[mat] * len(_nchunks(NCOLS[mat]))

    ctx = contextlib.ExitStack()
    with ctx:
        wsb = ctx.enter_context(nc.sbuf_tensor("wsb", [128, 32 * EC * 128], BF16))
        xsb = [ctx.enter_context(nc.sbuf_tensor(f"xsb{i}", [128, 16 * TW], BF16)) for i in range(3)]
        qsb = [ctx.enter_context(nc.sbuf_tensor(f"qsb{i}", [128, TW], F32)) for i in range(4)]
        sqsb = [ctx.enter_context(nc.sbuf_tensor(f"sqsb{i}", [128, TW], F32)) for i in range(2)]
        sel = ctx.enter_context(nc.sbuf_tensor("sel", [128, 16], F32))
        stsb = [ctx.enter_context(nc.sbuf_tensor(f"stsb{i}", [4, TW], F32)) for i in range(2)]
        qbc = [ctx.enter_context(nc.sbuf_tensor(f"qbc{i}", [128, 512], I32)) for i in range(2)]
        w4i = ctx.enter_context(nc.sbuf_tensor("w4i", [128, 512], I32))
        w4f = ctx.enter_context(nc.sbuf_tensor("w4f", [128, 512], F32))
        wt1 = ctx.enter_context(nc.sbuf_tensor("wt1", [128, 512], F32))
        wdq = [ctx.enter_context(nc.sbuf_tensor(f"wdq{i}", [128, 512], BF16)) for i in range(2)]
        shsb = ctx.enter_context(nc.sbuf_tensor("shsb", [128, 1], I32))
        scsb = ctx.enter_context(nc.sbuf_tensor("scsb", [32, D], BF16))
        zssb = ctx.enter_context(nc.sbuf_tensor("zssb", [32, D], BF16))
        qzsb = ctx.enter_context(nc.sbuf_tensor("qzsb", [32, 512], I32))
        z4sb = ctx.enter_context(nc.sbuf_tensor("z4sb", [32, 512], I32))
        gisb = [ctx.enter_context(nc.sbuf_tensor(f"gisb{i}", [32, 128], I32)) for i in range(2)]
        gtf = [ctx.enter_context(nc.sbuf_tensor(f"gtf{i}", [32, 128], BF16)) for i in range(2)]
        iog = ctx.enter_context(nc.sbuf_tensor("iog", [32, 1], F32))
        givsb = ctx.enter_context(nc.sbuf_tensor("givsb", [NGC, 1], F32))

        ps_qkv = [ctx.enter_context(nc.psum_tensor(f"psq{i}", [128, TW], F32)) for i in range(2)]
        ps_st = ctx.enter_context(nc.psum_tensor("psst", [4, TW], F32))
        ps_s = [ctx.enter_context(nc.psum_tensor(f"pss{i}", [128, 512], F32)) for i in range(2)]
        ps_z = [ctx.enter_context(nc.psum_tensor(f"psz{i}", [128, 512], F32)) for i in range(2)]

        s = make_sems(ctx, nc, [
            "ldw", "mmq", "evq", "sq", "smm", "evst",
            "mlds", "mldz", "zsp", "dgt",
            "dmm", "dvw", "dst", "ini", "dlds", "vch",
            "ldx0", "ldx1", "ldx2", "dld0", "dld1", "gld0", "gld1",
            "stq0", "stq1", "stq2", "stq3", "sts0", "sts1", "std0", "std1"])
        LDX = [s["ldx0"], s["ldx1"], s["ldx2"]]
        DLD = [s["dld0"], s["dld1"]]
        GLD = [s["gld0"], s["gld1"]]
        STQ = [s["stq0"], s["stq1"], s["stq2"], s["stq3"]]
        STS = [s["sts0"], s["sts1"]]
        STD = [s["std0"], s["std1"]]

        LDXC = []
        DLDC = []
        GLDC = []
        STQC = []
        STSC = []
        STDC = []

        blk = ctx.enter_context(nc.Block())

        @blk.gpsimd
        def _(g):
            s["ini"].inc(g.iota(iog[:], pattern=[[1, 1]], base=0,
                                channel_multiplier=1,
                                allow_small_or_imprecise_dtypes=True))
            for i in range(4):
                src = wq.rearrange("(c p) e -> p c e", p=128)[:, 8 * i:8 * (i + 1), :]
                s["ldw"].inc(g.dma_start(
                    out=wsb[:, (8 * i) * EC * 128:(8 * i + 8) * EC * 128],
                    in_=src), 16)
            s["mlds"].inc(g.dma_start(out=scsb[0:32, 0:NPC], in_=scg[:, :]), 16)
            for j in range(2 * TC):
                t, h = j // 2, j % 2
                if j >= 3:
                    g.wait_ge(s["mmq"].h, EC * ((j - 3) // 2 + 1))
                src = xfm.rearrange("(c p) t -> p c t", p=128)[
                    :, 16 * h:16 * (h + 1), t * TW:(t + 1) * TW]
                LDX[j % 3].inc(g.dma_start(out=xsb[j % 3][:], in_=src), 16)
                LDXC.append(LDX[j % 3].n)
            for di, (mat, kt, c0, cw) in enumerate(DQT):
                if kt == 0 and c0 == 0 and MATIDX[mat] > 0:
                    nb = TILES_BEFORE[mat]
                    g.wait_ge(s["dmm"].h, 2 * nb)
                    g.wait_ge(s["zsp"].h, MATIDX[mat])
                    ng = 32 if mat != "d" else NGC
                    sc_src = {"u": scu, "d": scd}[mat]
                    s["mlds"].inc(g.dma_start(out=scsb[0:ng, 0:NCOLS[mat]],
                                              in_=sc_src[:, :]), 16)
                qw = {"g": qwg, "u": qwu, "d": qwd}[mat]
                if di >= 2:
                    g.wait_ge(s["dvw"].h, di - 1)
                src = qw[16 * kt:16 * (kt + 1), c0:c0 + cw]
                src = src.rearrange("m (o n) -> m o n", o=1).broadcast_to([16, 8, cw])
                DLD[di % 2].inc(g.dma_start(out=qbc[di % 2][:, 0:cw], in_=src), 16)
                DLDC.append(DLD[di % 2].n)

        @blk.sync
        def _(sy):
            qz_src = {"g": qzg, "u": qzu, "d": qzd}
            sy.dma_start(out=shsb[:], in_=shifts[:, :]).then_inc(s["dlds"].h, 16)
            sy.dma_start(out=givsb[:], in_=gidv[:, :]).then_inc(s["dlds"].h, 16)
            s["dlds"].n += 32
            m0 = MATS[0]
            ng0 = 32 if m0 != "d" else NGC
            s["mldz"].inc(sy.dma_start(out=qzsb[0:ng0, 0:NCOLS[m0] // 8],
                                       in_=qz_src[m0][:, :]), 16)
            for t in range(TC):
                for e in range(EC):
                    i0 = t * EC + e
                    sy.wait_ge(s["evq"].h, i0 + 1)
                    STQ[i0 % 4].inc(
                        sy.dma_start(out=qkvc[e * 128:(e + 1) * 128, t * TW:(t + 1) * TW],
                                     in_=qsb[i0 % 4][:]), 16)
                    STQC.append(STQ[i0 % 4].n)
                sy.wait_ge(s["evst"].h, t + 1)
                STS[t % 2].inc(
                    sy.dma_start(out=stats[:, t * TW:(t + 1) * TW],
                                 in_=stsb[t % 2][:]), 16)
                STSC.append(STS[t % 2].n)
            nkt0 = 0
            for di, (mat, kt, c0, cw) in enumerate(DQT):
                if kt == 0 and c0 == 0 and MATIDX[mat] > 0:
                    sy.wait_ge(s["zsp"].h, MATIDX[mat])
                    ng = 32 if mat != "d" else NGC
                    s["mldz"].inc(sy.dma_start(out=qzsb[0:ng, 0:NCOLS[mat] // 8],
                                               in_=qz_src[mat][:, :]), 16)
                if c0 == 0:
                    if nkt0 >= 2:
                        sy.wait_ge(s["dgt"].h, nkt0 - 1)
                    if mat != "d":
                        src = gig[0:32, kt * 128:(kt + 1) * 128]
                        dst = gisb[nkt0 % 2][0:32, :]
                    else:
                        src = gid[0:NGC, kt * 128:(kt + 1) * 128]
                        dst = gisb[nkt0 % 2][0:NGC, :]
                    GLD[nkt0 % 2].inc(sy.dma_start(out=dst, in_=src), 16)
                    GLDC.append(GLD[nkt0 % 2].n)
                    nkt0 += 1
                dq = {"g": dqg, "u": dqu, "d": dqd}[mat]
                sy.wait_ge(s["dst"].h, di + 1)
                STD[di % 2].inc(
                    sy.dma_start(out=dq[128 * kt:128 * (kt + 1), c0:c0 + cw],
                                 in_=wdq[di % 2][:, 0:cw]), 16)
                STDC.append(STD[di % 2].n)
            for ss in (STQ + STS + STD):
                sy.wait_ge(ss.h, ss.n)

        @blk.scalar
        def _(sc):
            for t in range(TC):
                for e in range(8):
                    i0 = t * EC + e
                    sc.wait_ge(s["evq"].h, i0 + 1)
                    if t * 8 + e >= 2:
                        sc.wait_ge(s["smm"].h, t * 8 + e - 1)
                    s["sq"].inc(sc.activation(sqsb[e % 2][:], qsb[i0 % 4][:], AF.Square))

        @blk.vector
        def _(v):
            edge(v, s["vch"], v.memset(sel[:], 0.0))
            for j in range(4):
                v.memset(sel[:, 4 * j + j:4 * j + j + 1], 1.0)
            edge(v, s["vch"], v.engine_nop())
            for t in range(TC):
                for e in range(EC):
                    i0 = t * EC + e
                    v.wait_ge(s["mmq"].h, i0 + 1)
                    if i0 >= 4:
                        v.wait_ge(STQ[i0 % 4].h, STQC[i0 - 4])
                        tprev = (i0 - 4) // EC
                        eprev = (i0 - 4) % EC
                        if eprev < 8:
                            v.wait_ge(s["smm"].h, 8 * tprev + eprev + 1)
                    s["evq"].inc(v.tensor_scalar(qsb[i0 % 4][:], ps_qkv[e % 2][:],
                                                 -CLIP, CLIP, ALU.max, ALU.min))
                v.wait_ge(s["smm"].h, 8 * (t + 1))
                if t >= 2:
                    v.wait_ge(STS[t % 2].h, STSC[t - 2])
                s["evst"].inc(v.tensor_copy(stsb[t % 2][:], ps_st[:]))
            nkt = 0
            for di, (mat, kt, c0, cw) in enumerate(DQT):
                ng = 32 if mat != "d" else NGC
                ncols = NCOLS[mat]
                if c0 == 0 and kt == 0:
                    mi = MATIDX[mat]
                    v.wait_ge(s["mlds"].h, 16 * (mi + 1))
                    v.wait_ge(s["mldz"].h, 16 * (mi + 1))
                    for (zc0, zcw) in _nchunks(ncols):
                        for r in range(8):
                            ii = v.tensor_scalar(
                                z4sb[0:ng, 0:zcw].rearrange("g (n o) -> g n o", o=8)[:, :, r],
                                qzsb[0:ng, zc0 // 8:(zc0 + zcw) // 8], 4 * r, 15,
                                ALU.logical_shift_right, ALU.bitwise_and)
                        edge(v, s["vch"], ii)
                        edge(v, s["vch"], v.tensor_copy(zssb[0:ng, zc0:zc0 + zcw],
                                                        z4sb[0:ng, 0:zcw]))
                        edge(v, s["vch"], v.tensor_scalar(zssb[0:ng, zc0:zc0 + zcw],
                                        zssb[0:ng, zc0:zc0 + zcw], 1.0, None, ALU.add))
                        last = v.tensor_tensor(zssb[0:ng, zc0:zc0 + zcw],
                                               zssb[0:ng, zc0:zc0 + zcw],
                                               scsb[0:ng, zc0:zc0 + zcw], ALU.mult)
                    s["zsp"].inc(last)
                if c0 == 0:
                    nkt += 1
                    ki = nkt - 1
                    if nkt >= 3:
                        v.wait_ge(s["dmm"].h, 2 * KT_END[nkt - 3])
                    v.wait_ge(GLD[ki % 2].h, GLDC[ki])
                    if mat != "d":
                        v.wait_ge(s["ini"].h, 1)
                        s["dgt"].inc(v.tensor_scalar(gtf[ki % 2][0:32, :],
                                                     gisb[ki % 2][0:32, :],
                                                     iog[0:32, 0:1], None, ALU.is_equal))
                    else:
                        s["dgt"].inc(v.tensor_scalar(gtf[ki % 2][0:NGC, :],
                                                     gisb[ki % 2][0:NGC, :],
                                                     givsb[0:NGC, 0:1], None, ALU.is_equal))
                if di == 0:
                    v.wait_ge(s["dlds"].h, 32)
                v.wait_ge(DLD[di % 2].h, DLDC[di])
                ii = v.tensor_tensor(w4i[:, 0:cw], qbc[di % 2][:, 0:cw],
                                     shsb[:, 0:1].broadcast_to([128, cw]),
                                     ALU.logical_shift_right)
                s["dvw"].inc(ii)
                v.wait_ge(s["dvw"].h, s["dvw"].n)
                edge(v, s["vch"], v.tensor_scalar(w4i[:, 0:cw], w4i[:, 0:cw],
                                                  15, None, ALU.bitwise_and))
                edge(v, s["vch"], v.tensor_copy(w4f[:, 0:cw], w4i[:, 0:cw]))
                v.wait_ge(s["dmm"].h, 2 * (di + 1))
                if di >= 2:
                    v.wait_ge(STD[di % 2].h, STDC[di - 2])
                edge(v, s["vch"], v.tensor_tensor(wt1[:, 0:cw], w4f[:, 0:cw],
                                                  ps_s[di % 2][:, 0:cw], ALU.mult))
                s["dst"].inc(v.tensor_tensor(wdq[di % 2][:, 0:cw], wt1[:, 0:cw],
                                             ps_z[di % 2][:, 0:cw], ALU.subtract))

        @blk.tensor
        def _(te):
            te.wait_ge(s["ldw"].h, 64)
            for t in range(TC):
                for e in range(EC):
                    i0 = t * EC + e
                    if i0 >= 2:
                        te.wait_ge(s["evq"].h, i0 - 1)
                    for d in range(32):
                        if d == 0:
                            te.wait_ge(LDX[(2 * t) % 3].h, LDXC[2 * t])
                        if d == 16:
                            te.wait_ge(LDX[(2 * t + 1) % 3].h, LDXC[2 * t + 1])
                        xbuf = xsb[(2 * t + d // 16) % 3]
                        mm = te.matmul(ps_qkv[e % 2][:],
                                       wsb[:, (d * EC + e) * 128:(d * EC + e + 1) * 128],
                                       xbuf[:, (d % 16) * TW:(d % 16 + 1) * TW],
                                       start=(d == 0), stop=(d == 31))
                        if d == 31:
                            s["mmq"].inc(mm)
                    if 1 <= e <= 8:
                        es = e - 1
                        if es == 0 and t >= 1:
                            te.wait_ge(s["evst"].h, t)
                        te.wait_ge(s["evq"].h, t * EC + es + 1)
                        selr = sel[:, 0:4] if es < 4 else sel[:, 8:12]
                        te.matmul(ps_st[:], selr, qsb[(t * EC + es) % 4][:],
                                  start=(es == 0), stop=False)
                        te.wait_ge(s["sq"].h, t * 8 + es + 1)
                        sels = sel[:, 4:8] if es < 4 else sel[:, 12:16]
                        mm = te.matmul(ps_st[:], sels, sqsb[es % 2][:],
                                       start=False, stop=(es == 7))
                        s["smm"].inc(mm)
            nkt = 0
            for di, (mat, kt, c0, cw) in enumerate(DQT):
                ng = 32 if mat != "d" else NGC
                if c0 == 0:
                    nkt += 1
                ki = nkt - 1
                te.wait_ge(s["dgt"].h, nkt)
                te.wait_ge(s["zsp"].h, MATIDX[mat] + 1)
                if di >= 2:
                    te.wait_ge(s["dst"].h, di - 1)
                te.matmul(ps_s[di % 2][:, 0:cw], gtf[ki % 2][0:ng, :],
                          scsb[0:ng, c0:c0 + cw], start=True, stop=True)
                mm = te.matmul(ps_z[di % 2][:, 0:cw], gtf[ki % 2][0:ng, :],
                               zssb[0:ng, c0:c0 + cw], start=True, stop=True)
                s["dmm"].inc(mm, 2)

    return nc



class Sched:
    ENGS = ("g", "sy", "te", "v", "sc")

    def __init__(self):
        self.ops = {e: [] for e in self.ENGS}

    def w(self, eng, sem, val):
        if val > 0:
            self.ops[eng].append(lambda e, h=sem.h, v=val: e.wait_ge(h, v))

    def i(self, eng, fn, incs=()):
        vals = []
        for sem, k in incs:
            sem.n += k
            vals.append((sem.h, k))

        def run(e, fn=fn, vals=vals):
            ins = fn(e)
            for h, k in vals:
                ins.then_inc(h, k)
        self.ops[eng].append(run)
        return tuple(sem.n for sem, _ in incs)

    def edge(self, eng, fn, sem):
        self.i(eng, fn, [(sem, 1)])
        self.w(eng, sem, sem.n)

    def replay(self, eng, e):
        for f in self.ops[eng]:
            f(e)



def build_B(SD=S):
    nc = _bass()
    TB = B * SD
    NQT = SD // 128
    TCB = TB // 512

    qkvc = nc.dram_tensor("qkvc", [1536, TB], F32, kind="ExternalInput").ap()
    stall = nc.dram_tensor("stall", [32, TB], F32, kind="ExternalInput").ap()
    stsel = nc.dram_tensor("stsel", [32, 4], F32, kind="ExternalInput").ap()
    bsel = nc.dram_tensor("bsel", [4, 4 * 128], F32, kind="ExternalInput").ap()
    lnwb = nc.dram_tensor("lnwb", [128, 16], F32, kind="ExternalInput").ap()
    slb = nc.dram_tensor("slb", [128, 4], F32, kind="ExternalInput").ap()
    slc = nc.dram_tensor("slc", [128, 4], F32, kind="ExternalInput").ap()
    oo = nc.dram_tensor("oo", [8, SD, HD], F32, kind="ExternalOutput").ap()

    ctx = contextlib.ExitStack()
    with ctx:
        sb = lambda *a: ctx.enter_context(nc.sbuf_tensor(*a))
        pt = lambda *a: ctx.enter_context(nc.psum_tensor(*a))
        stall_sb = [sb(f"stalls{i}", [32, 512], F32) for i in range(2)]
        st4 = sb("st4", [4, 512], F32)
        lnv = sb("lnv", [128, 16], F32)
        lnt = sb("lnt", [128, 16], F32)
        mut = sb("mut", [128, 8], F32)
        vat = sb("vat", [128, 8], F32)
        lnrow = sb("lnrow", [4, TB], F32)
        bselsb = sb("bselsb", [4, 4 * 128], F32)
        stselsb = sb("stselsb", [32, 4], F32)
        bc = [sb(f"bc{i}", [128, SD], F32) for i in range(4)]
        qf = sb("qf", [128, SD], F32)
        tmp1 = sb("tmp1", [128, SD], F32)
        qln = sb("qln", [128, SD], BF16)
        kln = sb("kln", [128, SD], BF16)
        vbf = sb("vbf", [128, SD], BF16)
        vtm = sb("vtm", [128, SD], BF16)
        lnwsb = sb("lnwsb", [128, 16], F32)
        slbsb = sb("slbsb", [128, 4], F32)
        slcsb = sb("slcsb", [128, 4], F32)
        tio = sb("tio", [128, 512], F32)
        mk4 = sb("mk4", [128, 4 * 512], F32)
        ah = sb("ah", [128, 512], F32)
        dv4 = sb("dv4", [128, 4 * 512], F32)
        idf = sb("idf", [128, 128], F32)
        idb = sb("idb", [128, 128], BF16)
        psb = [sb(f"psb{i}", [128, 512], BF16) for i in range(2)]
        ptsb = [sb(f"ptsb{i}", [128, 512], BF16) for i in range(2)]
        osb = sb("osb", [128, HD], F32)
        osacc = sb("osacc", [128, HD], F32)
        mrun = sb("mrun", [128, 1], F32)
        mnew = sb("mnew", [128, 1], F32)
        mtile = sb("mtile", [128, 1], F32)
        psmax = sb("psmax", [128, 1], F32)
        offt = sb("offt", [128, 1], F32)
        biast = sb("biast", [128, 1], F32)
        corrs = sb("corrsB", [128, 4], F32)
        mdif = sb("mdif", [128, 1], F32)
        tsum = sb("tsum", [128, 1], F32)
        dnr = sb("dnr", [128, 1], F32)
        rcp = sb("rcp", [128, 4], F32)
        epst = sb("epst", [128, 1], F32)

        ps_s = [pt(f"pscs{i}", [128, 512], F32) for i in range(2)]
        ps_pt = [pt(f"pspt{i}", [128, 512], BF16) for i in range(2)]
        ps_o = [pt(f"pso{i}", [128, HD], F32) for i in range(2)]
        ps_bc = pt("psbc", [128, 512], F32)
        ps_x = pt("psxB", [128, 512], F32)
        ring = [ps_s[0], ps_s[1], ps_bc, ps_x]

        s = make_sems(ctx, nc, [
            "lds", "ld0", "ld1", "ldq", "ldv", "summ", "sue", "tt1", "lnve", "lnm",
            "ttb", "lnre", "bcm", "bce", "lnap", "vgp", "scm", "sadd",
            "smax", "sexp", "scorr", "sdn", "strp", "spte", "sres", "sav",
            "soev", "ost", "vtp", "vte", "vch", "sch", "gch", "idr"])

        S_ = Sched()

        for name, dst, src in (("stsel", stselsb, stsel), ("bsel", bselsb, bsel),
                               ("lnwb", lnwsb, lnwb), ("slb", slbsb, slb),
                               ("slc", slcsb, slc)):
            S_.i("sy", lambda e, d=dst, so=src: e.dma_start(out=d[:], in_=so[:, :]),
                 [(s["lds"], 16)])
        S_.i("g", lambda e: e.iota(idf[:], pattern=[[1, 128]], base=0,
                                   channel_multiplier=-1,
                                   allow_small_or_imprecise_dtypes=True),
             [(s["gch"], 1)])
        S_.i("g", lambda e: e.iota(tio[:], pattern=[[1, 512]], base=0,
                                   channel_multiplier=-1,
                                   allow_small_or_imprecise_dtypes=True),
             [(s["gch"], 1)])
        S_.w("v", s["gch"], 2)
        for vv in range(4):
            S_.edge("v", lambda e, vv=vv: e.tensor_scalar(
                mk4[:, 512 * vv:512 * (vv + 1)], tio[:], float(-128 * vv), None,
                ALU.add), s["vch"])
            S_.edge("v", lambda e, vv=vv: e.tensor_scalar(
                mk4[:, 512 * vv:512 * (vv + 1)], mk4[:, 512 * vv:512 * (vv + 1)],
                0.0, NEGBIG, ALU.is_gt, ALU.mult), s["vch"])
        S_.edge("v", lambda e: e.memset(epst[:], LN_EPS), s["vch"])
        S_.edge("v", lambda e: e.tensor_scalar(idf[:], idf[:], 0.0, None,
                                               ALU.is_equal), s["vch"])
        S_.i("v", lambda e: e.tensor_copy(idb[:], idf[:]), [(s["idr"], 1)])

        for t in range(TCB):
            S_.w("sy", s["summ"], t - 1)
            S_.i("sy", lambda e, t=t: e.dma_start(
                out=stall_sb[t % 2][:], in_=stall[:, 512 * t:512 * (t + 1)]),
                [(s["ld%d" % (t % 2)], 16)])
            S_.w("te", s["lds"], 80)
            S_.w("te", s["ld%d" % (t % 2)], s["ld%d" % (t % 2)].n)
            S_.w("te", s["sue"], t)
            S_.i("te", lambda e, t=t: e.matmul(ps_x[0:4, :], stselsb[:, :],
                                               stall_sb[t % 2][:],
                                               start=True, stop=True),
                 [(s["summ"], 1)])
            S_.w("v", s["summ"], t + 1)
            S_.w("v", s["tt1"], t)
            S_.i("v", lambda e: e.tensor_copy(st4[:], ps_x[0:4, :]), [(s["sue"], 1)])
            S_.w("te", s["sue"], t + 1)
            S_.w("te", s["idr"], 1)
            S_.w("te", s["lnve"], t)
            for j in range(4):
                S_.i("te", lambda e, j=j: e.matmul(
                    ps_s[0][:, 4 * j:4 * (j + 1)], st4[:, 128 * j:128 * (j + 1)],
                    idf[0:4, 0:4], is_transpose=True,
                    start=(j == 0), stop=(j == 3)),
                    [(s["tt1"], 1)] if j == 3 else ())
            S_.w("v", s["tt1"], t + 1)
            S_.w("v", s["ttb"], t)
            S_.i("v", lambda e: e.tensor_copy(lnv[:], ps_s[0][:, 0:16]), [(s["lnve"], 1)])
            S_.w("v", s["lnve"], s["lnve"].n)
            S_.edge("v", lambda e: e.tensor_scalar(mut[:], lnv[:, :].rearrange(
                "p (j s) -> p j s", s=2)[:, :, 0], 1.0 / D, None, ALU.mult), s["vch"])
            S_.edge("v", lambda e: e.tensor_scalar(vat[:], lnv[:, :].rearrange(
                "p (j s) -> p j s", s=2)[:, :, 1], 1.0 / D, None, ALU.mult), s["vch"])
            S_.edge("v", lambda e: e.tensor_tensor(
                lnv[:, 0:8], mut[:], mut[:], ALU.mult), s["vch"])
            S_.i("v", lambda e: e.tensor_tensor(
                vat[:], vat[:], lnv[:, 0:8], ALU.subtract), [(s["lnm"], 1)])
            S_.w("sc", s["lnm"], s["lnm"].n)
            S_.edge("sc", lambda e: e.activation(vat[:], vat[:], AF.Ln,
                                                 bias=epst[:, 0:1]), s["sch"])
            S_.i("sc", lambda e: e.activation(
                lnt[:, :].rearrange("p (j s) -> p j s", s=2)[:, :, 0], vat[:],
                AF.Exp, scale=-0.5), [(s["lnm"], 1)])
            S_.w("v", s["lnm"], s["lnm"].n)
            S_.i("v", lambda e: e.tensor_tensor(
                lnt[:, :].rearrange("p (j s) -> p j s", s=2)[:, :, 1], mut[:],
                lnt[:, :].rearrange("p (j s) -> p j s", s=2)[:, :, 0],
                ALU.mult), [(s["lnm"], 1)])
            S_.w("te", s["lnm"], s["lnm"].n)
            S_.w("te", s["lnre"], t)
            for j in range(4):
                S_.i("te", lambda e, j=j: e.matmul(
                    ps_bc[0:4, 128 * j:128 * (j + 1)], lnt[:, 4 * j:4 * (j + 1)],
                    idf[:, :], is_transpose=True, start=(j == 0), stop=(j == 3)),
                    [(s["ttb"], 1)] if j == 3 else ())
            S_.w("v", s["ttb"], t + 1)
            S_.i("v", lambda e, t=t: e.tensor_copy(lnrow[:, 512 * t:512 * (t + 1)],
                                                   ps_bc[0:4, :]), [(s["lnre"], 1)])

        nlnap = 0
        gT = [0]
        nvt = [0]
        nst = [0]
        for b in range(B):
            S_.w("te", s["sexp"], gT[0])
            S_.w("te", s["lnre"], TB // 512)
            for tch in range(SD // 512):
                gt = (SD * b + 512 * tch) // 512
                for s4 in range(4):
                    S_.w("te", s["lnre"], gt + 1)
                    S_.w("te", s["bce"], s["bcm"].n)
                    if b > 0:
                        S_.w("te", s["lnap"], nlnap)
                    S_.i("te", lambda e, gt=gt, s4=s4: e.matmul(
                        ps_bc[:], bselsb[:, 128 * s4:128 * (s4 + 1)],
                        lnrow[:, 512 * gt:512 * (gt + 1)], start=True, stop=True),
                        [(s["bcm"], 1)])
                    S_.w("v", s["bcm"], s["bcm"].n)
                    S_.i("v", lambda e, tch=tch, s4=s4: e.tensor_copy(
                        bc[s4][:, 512 * tch:512 * (tch + 1)], ps_bc[:]),
                        [(s["bce"], 1)])
            for h in range(HPC):
                bh = b * HPC + h
                for which in range(2):
                    r0 = 512 * which + 128 * h
                    S_.w("sy", s["lnap"], nlnap)
                    S_.w("sy", s["bce"], s["bce"].n)
                    S_.i("sy", lambda e, r0=r0, b=b: e.dma_start(
                        out=qf[:], in_=qkvc[r0:r0 + 128, SD * b:SD * (b + 1)]),
                        [(s["ldq"], 16)])
                    S_.w("v", s["ldq"], s["ldq"].n)
                    rb, mb = (bc[0], bc[1]) if which == 0 else (bc[2], bc[3])
                    S_.edge("v", lambda e, rb=rb: e.tensor_tensor(
                        tmp1[:], qf[:], rb[:], ALU.mult), s["vch"])
                    S_.i("v", lambda e, mb=mb: e.tensor_tensor(
                        tmp1[:], tmp1[:], mb[:], ALU.subtract), [(s["lnap"], 1)])
                    nlnap += 1
                    dst = qln if which == 0 else kln
                    wc = 4 * h + 2 * which
                    S_.w("v", s["lnap"], nlnap)
                    S_.w("v", s["scm"], s["scm"].n)
                    S_.i("v", lambda e, dst=dst, wc=wc: e.tensor_scalar(
                        dst[:], tmp1[:], lnwsb[:, wc:wc + 1],
                        lnwsb[:, wc + 1:wc + 2], ALU.mult, ALU.add),
                        [(s["lnap"], 1)])
                    nlnap += 1
                S_.w("g", s["vtp"], s["vtp"].n)
                S_.i("g", lambda e, h=h, b=b: e.dma_start(
                    out=vbf[:], in_=qkvc[1024 + 128 * h:1024 + 128 * (h + 1),
                                         SD * b:SD * (b + 1)]), [(s["ldv"], 16)])
                for gb in range(SD // 512):
                    S_.w("te", s["ldv"], s["ldv"].n)
                    S_.w("te", s["idr"], 1)
                    S_.w("te", s["spte"], gT[0])
                    S_.w("te", s["vte"], max(0, nvt[0] - 1))
                    pb = ps_pt[gb % 2]
                    for j in range(4):
                        S_.i("te", lambda e, gb=gb, j=j, pb=pb: e.matmul(
                            pb[:, 128 * j:128 * (j + 1)],
                            vbf[:, 512 * gb + 128 * j:512 * gb + 128 * (j + 1)],
                            idb[:], is_transpose=True,
                            start=(j == 0), stop=(j == 3)),
                            [(s["vtp"], 1)] if j == 3 else ())
                    S_.w("sc", s["vtp"], s["vtp"].n)
                    S_.w("sc", s["sav"], gT[0])
                    S_.i("sc", lambda e, gb=gb, pb=pb: e.copy(
                        vtm[:, 512 * gb:512 * (gb + 1)], pb[:]), [(s["vte"], 1)])
                    nvt[0] += 1
                S_.edge("v", lambda e, h=h: e.tensor_scalar(
                    ah[:], tio[:], slbsb[:, h:h + 1], None, ALU.mult), s["vch"])
                for vv in range(4):
                    ii = lambda e, vv=vv: e.tensor_tensor(
                        dv4[:, 512 * vv:512 * (vv + 1)], ah[:],
                        mk4[:, 512 * vv:512 * (vv + 1)], ALU.add)
                    if vv == 3:
                        S_.edge("v", ii, s["vch"])
                    else:
                        S_.i("v", ii)
                S_.i("v", lambda e: e.engine_nop(), [(s["lnap"], 1)])
                nlnap += 1
                tlist = [(qt, kt) for qt in range(NQT)
                         for kt in range(qt // 4 + 1)]
                ntile = len(tlist)
                gt0 = gT[0]
                for u in range(ntile + 2):
                    t = gt0 + u
                    if u < ntile:
                        qt, kt = tlist[u]
                        diag = (kt == qt // 4)
                        rb = ring[t % 4]
                        S_.w("te", s["sexp"], max(0, t - 3))
                        if u == 0:
                            S_.w("te", s["lnap"], nlnap)
                        if u < 4:
                            S_.w("te", s["bce"], s["bce"].n)
                        S_.i("te", lambda e, rb=rb, qt=qt, kt=kt: e.matmul(
                            rb[:], qln[:, 128 * qt:128 * (qt + 1)],
                            kln[:, 512 * kt:512 * (kt + 1)], start=True,
                            stop=True), [(s["scm"], 1)])
                        if kt == 0:
                            S_.edge("v", lambda e: e.memset(mrun[:], -1e30),
                                    s["vch"])
                            S_.edge("v", lambda e: e.memset(dnr[:], 0.0),
                                    s["vch"])
                        S_.w("v", s["scm"], t + 1)
                        if diag:
                            src = dv4[:, 512 * (qt % 4):512 * (qt % 4 + 1)]
                        else:
                            src = ah
                        S_.edge("v", lambda e, rb=rb, src=src: e.tensor_tensor(
                            rb[:], rb[:], src[:] if src is ah else src,
                            ALU.add), s["vch"])
                        S_.edge("v", lambda e, rb=rb: e.reduce_max(
                            psmax[:], rb[:], axis=AX.X), s["vch"])
                        cval = float(512 * kt - 128 * qt)
                        S_.edge("v", lambda e, h=h, cval=cval: e.tensor_scalar(
                            offt[:], slcsb[:, h:h + 1], cval, None, ALU.mult),
                            s["vch"])
                        S_.edge("v", lambda e: e.scalar_tensor_tensor(
                            mtile[:], psmax[:], SC0, offt[:], ALU.mult,
                            ALU.add), s["vch"])
                        S_.edge("v", lambda e: e.tensor_tensor(
                            mnew[:], mrun[:], mtile[:], ALU.max), s["vch"])
                        S_.edge("v", lambda e: e.tensor_tensor(
                            mdif[:], mrun[:], mnew[:], ALU.subtract), s["vch"])
                        S_.w("v", s["sexp"], t)
                        S_.edge("v", lambda e: e.tensor_tensor(
                            biast[:], offt[:], mnew[:], ALU.subtract), s["vch"])
                        S_.i("v", lambda e: e.tensor_copy(mrun[:], mnew[:]),
                             [(s["sadd"], 1)])
                        cc = t % 4
                        S_.w("sc", s["sadd"], t + 1)
                        S_.w("sc", s["sres"], max(0, t - 3))
                        S_.i("sc", lambda e, cc=cc: e.activation(
                            corrs[:, cc:cc + 1], mdif[:], AF.Exp),
                            [(s["scorr"], 1)])
                        S_.w("sc", s["strp"], max(0, t - 1))
                        S_.w("sc", s["sdn"], t)
                        S_.i("sc", lambda e, rb=rb, t=t: e.activation(
                            psb[t % 2][:], rb[:], AF.Exp, bias=biast[:, 0:1],
                            scale=SC0, accum_out=tsum[:, 0:1]),
                            [(s["sexp"], 1)])
                        S_.w("v", s["sexp"], t + 1)
                        S_.w("v", s["scorr"], t + 1)
                        S_.i("v", lambda e, cc=cc: e.scalar_tensor_tensor(
                            dnr[:], dnr[:], corrs[:, cc:cc + 1], tsum[:],
                            ALU.mult, ALU.add), [(s["sdn"], 1)])
                        if diag:
                            S_.w("v", s["sdn"], t + 1)
                            S_.edge("v", lambda e, qc4=qt % 4: e.reciprocal(
                                rcp[:, qc4:qc4 + 1], dnr[:]), s["vch"])
                    if 1 <= u <= ntile:
                        tm = gt0 + u - 1
                        S_.w("te", s["sexp"], tm + 1)
                        S_.w("te", s["spte"], max(0, tm - 1))
                        S_.w("te", s["idr"], 1)
                        if u - 1 < 2:
                            S_.w("te", s["vte"], nvt[0])
                        pb = ps_pt[tm % 2]
                        for j in range(4):
                            S_.i("te", lambda e, j=j, tm=tm, pb=pb: e.matmul(
                                pb[:, 128 * j:128 * (j + 1)],
                                psb[tm % 2][:, 128 * j:128 * (j + 1)], idb[:],
                                is_transpose=True, start=(j == 0),
                                stop=(j == 3)),
                                [(s["strp"], 1)] if j == 3 else ())
                        S_.w("sc", s["strp"], tm + 1)
                        S_.w("sc", s["sav"], max(0, tm - 1))
                        S_.i("sc", lambda e, tm=tm, pb=pb: e.copy(
                            ptsb[tm % 2][:], pb[:]), [(s["spte"], 1)])
                    if u >= 2:
                        ta = gt0 + u - 2
                        qta, kta = tlist[u - 2]
                        S_.w("te", s["spte"], ta + 1)
                        S_.w("te", s["sres"], max(0, ta - 1))
                        if u - 2 < 2:
                            S_.w("te", s["vte"], nvt[0])
                        for j in range(4):
                            S_.i("te", lambda e, ta=ta, kta=kta, j=j: e.matmul(
                                ps_o[ta % 2][:],
                                ptsb[ta % 2][:, 128 * j:128 * (j + 1)],
                                vtm[:, (4 * kta + j) * 128:(4 * kta + j + 1) * 128],
                                start=(j == 0), stop=(j == 3)),
                                [(s["sav"], 1)] if j == 3 else ())
                        S_.w("v", s["sav"], ta + 1)
                        cc2 = ta % 4
                        if kta == 0:
                            S_.w("v", s["soev"], s["soev"].n)
                            S_.i("v", lambda e, ta=ta: e.tensor_copy(
                                osacc[:], ps_o[ta % 2][:]), [(s["sres"], 1)])
                        else:
                            S_.i("v", lambda e, ta=ta, cc2=cc2:
                                 e.scalar_tensor_tensor(
                                osacc[:], osacc[:], corrs[:, cc2:cc2 + 1],
                                ps_o[ta % 2][:], ALU.mult, ALU.add),
                                [(s["sres"], 1)])
                        if kta == qta // 4:
                            S_.w("v", s["sres"], ta + 1)
                            S_.w("v", s["ost"], 16 * nst[0])
                            S_.i("v", lambda e, qc4=qta % 4: e.tensor_scalar(
                                osb[:], osacc[:], rcp[:, qc4:qc4 + 1], None,
                                ALU.mult), [(s["soev"], 1)])
                            S_.w("sy", s["soev"], s["soev"].n)
                            S_.i("sy", lambda e, bh=bh, qta=qta: e.dma_start(
                                out=oo[bh, 128 * qta:128 * (qta + 1), :],
                                in_=osb[:]), [(s["ost"], 16)])
                            nst[0] += 1
                gT[0] += ntile
        S_.w("sy", s["ost"], s["ost"].n)

        blk = ctx.enter_context(nc.Block())

        @blk.gpsimd
        def _(e):
            S_.replay("g", e)

        @blk.sync
        def _(e):
            S_.replay("sy", e)

        @blk.tensor
        def _(e):
            S_.replay("te", e)

        @blk.vector
        def _(e):
            S_.replay("v", e)

        @blk.scalar
        def _(e):
            S_.replay("sc", e)

    return nc



def build_E(TPE=TPC, NIC=NG, NIH=4):
    nc = _bass()
    NTT = TPE // 128
    NTH2 = TPE // 512
    base, rem = NIC // NIH, NIC % NIH
    lens = [base + (1 if i < rem else 0) for i in range(NIH)]
    ihr, st = [], 0
    for L in lens:
        ihr.append((st, st + L))
        st += L
    HMAX = max(lens)

    ops = nc.dram_tensor("ops", [NCORES, 128, TPE], F32, kind="ExternalInput").ap()
    x2tm = nc.dram_tensor("x2tm_in", [TPE, D], BF16, kind="ExternalInput").ap()
    wo = nc.dram_tensor("wo", [128, D], F32, kind="ExternalInput").ap()
    bo = nc.dram_tensor("bo", [1, D], F32, kind="ExternalInput").ap()
    wgt = nc.dram_tensor("wgt", [NIC, 128, 32 * 128], BF16, kind="ExternalInput").ap()
    wut = nc.dram_tensor("wut", [NIC, 128, 32 * 128], BF16, kind="ExternalInput").ap()
    wdt = nc.dram_tensor("wdt", [NIC, 8, 128, 512], BF16, kind="ExternalInput").ap()
    out = nc.dram_tensor("out", [TPE, D], F32, kind="ExternalOutput").ap()
    x3d = nc.dram_tensor("x3d", [NTT, 128, D], BF16).ap()
    ypart = nc.dram_tensor("ypart", [8, NTT, 128, 512], F32).ap()

    ctx = contextlib.ExitStack()
    with ctx:
        sb = lambda *a: ctx.enter_context(nc.sbuf_tensor(*a))
        pt = lambda *a: ctx.enter_context(nc.psum_tensor(*a))
        opb = [sb(f"opbE{i}", [128, 512], F32) for i in range(2)]
        osf = sb("osfE", [128, TPE], F32)
        osb = sb("osbE", [128, TPE], BF16)
        wotb = sb("wotbE", [128, D], BF16)
        bob = sb("bobE", [128, D], BF16)
        x2sl = sb("x2slE", [128, D], BF16)
        rsl = sb("rslE", [128, D], BF16)
        ybuf = sb("ybufE", [128, D // 2], F32)
        x3fm = sb("x3fmE", [128, 32 * TPE], BF16)
        hfm = sb("hfmE", [128, HMAX * TPE], BF16)
        wgsl = sb("wgslE", [128, 32 * 128], BF16)
        wusl = sb("wuslE", [128, 32 * 128], BF16)
        wdsl = [sb(f"wdslE{i}", [128, 512], BF16) for i in range(2)]
        ysl = [sb(f"yslE{i}", [128, 512], F32) for i in range(2)]
        ypv = [sb(f"ypvE{i}", [128, 512], F32) for i in range(2)]
        idf = sb("idfE", [128, 128], F32)
        epst = sb("epstE", [128, 1], F32)
        ssum = sb("ssumE", [128, 1], F32)
        sqs = sb("sqsE", [128, 1], F32)
        mu = sb("muE", [128, 1], F32)
        vv = sb("vvE", [128, 1], F32)
        m2 = sb("m2E", [128, 1], F32)
        rr = sb("rrE", [128, 1], F32)

        ps8 = [pt(f"psE{i}", [128, 512], F32) for i in range(8)]

        s = make_sems(ctx, nc, [
            "lop0", "lop1", "lwb", "lx2", "lwg", "lwd0", "lwd1", "lyp0",
            "lyp1", "lyg", "aop", "acast", "amm", "aev", "sqd", "sqr",
            "lnm", "x3p", "x3w", "trp", "tre", "f1m", "f1s", "f1h", "f2t",
            "f2e", "ypw0", "ypw1", "outw", "vch", "sch", "gch", "idr"])
        S_ = Sched()

        S_.i("g", lambda e: e.iota(idf[:], pattern=[[1, 128]], base=0,
                                   channel_multiplier=-1,
                                   allow_small_or_imprecise_dtypes=True),
             [(s["gch"], 1)])
        S_.i("g", lambda e: e.dma_start(out=wotb[:], in_=wo[:, :]),
             [(s["lwb"], 16)])
        S_.i("g", lambda e: e.dma_start(
            out=bob[:], in_=bo[0:1, :].broadcast_to([128, D])),
            [(s["lwb"], 16)])
        S_.w("v", s["gch"], 1)
        S_.edge("v", lambda e: e.tensor_scalar(idf[:], idf[:], 0.0, None,
                                               ALU.is_equal), s["vch"])
        S_.i("v", lambda e: e.memset(epst[:], LN_EPS), [(s["idr"], 1)])

        nop = 0
        for j in range(NCORES):
            for hh in range(TPE // 512):
                sl = nop % 2
                lsem = s["lop%d" % sl]
                if nop >= 2:
                    S_.w("sy", s["aop"], nop - 1)
                S_.i("sy", lambda e, j=j, hh=hh, sl=sl: e.dma_start(
                    out=opb[sl][:], in_=ops[j, :, 512 * hh:512 * (hh + 1)]),
                    [(lsem, 16)])
                nop += 1
                S_.w("v", lsem, 16 * ((nop + 1) // 2))
                S_.w("v", s["aop"], s["aop"].n)
                dsl = slice(512 * hh, 512 * (hh + 1))
                if j == 0:
                    S_.i("v", lambda e, sl=sl, dsl=dsl: e.tensor_copy(
                        osf[:, dsl], opb[sl][:]), [(s["aop"], 1)])
                else:
                    S_.i("v", lambda e, sl=sl, dsl=dsl: e.tensor_tensor(
                        osf[:, dsl], osf[:, dsl], opb[sl][:], ALU.add),
                        [(s["aop"], 1)])
        S_.w("v", s["aop"], s["aop"].n)
        S_.i("v", lambda e: e.tensor_copy(osb[:], osf[:]), [(s["acast"], 1)])

        nx2 = 0
        for tt in range(NTT):
            S_.w("sy", s["sqr"], max(0, tt))
            S_.i("sy", lambda e, tt=tt: e.dma_start(
                out=x2sl[:], in_=x2tm[128 * tt:128 * (tt + 1), :]),
                [(s["lx2"], 16)])
            nx2 += 1
            S_.w("v", s["x3w"], 16 * tt)
            for dc in range(8):
                bank = 4 + (dc % 2)
                idx = 8 * tt + dc
                S_.w("te", s["lwb"], 32)
                S_.w("te", s["acast"], 1)
                S_.w("te", s["aev"], max(0, idx - 1))
                S_.i("te", lambda e, tt=tt, dc=dc, bank=bank: e.matmul(
                    ps8[bank][:], osb[:, 128 * tt:128 * (tt + 1)],
                    wotb[:, 512 * dc:512 * (dc + 1)], start=True, stop=True),
                    [(s["amm"], 1)])
                S_.w("v", s["amm"], s["amm"].n)
                S_.w("v", s["lx2"], 16 * nx2)
                dsl = slice(512 * dc, 512 * (dc + 1))
                S_.edge("v", lambda e, bank=bank, dsl=dsl: e.tensor_tensor(
                    rsl[:, dsl], ps8[bank][:], x2sl[:, dsl], ALU.add), s["vch"])
                S_.i("v", lambda e, dsl=dsl: e.tensor_tensor(
                    rsl[:, dsl], rsl[:, dsl], bob[:, dsl], ALU.add),
                    [(s["aev"], 1)])
            S_.w("v", s["aev"], s["aev"].n)
            S_.edge("v", lambda e: e.reduce_sum(ssum[:], rsl[:], axis=AX.X),
                    s["vch"])
            S_.w("sc", s["aev"], s["aev"].n)
            S_.i("sc", lambda e: e.activation(x2sl[:], rsl[:], AF.Square),
                 [(s["sqd"], 1)])
            S_.w("v", s["sqd"], s["sqd"].n)
            S_.i("v", lambda e: e.reduce_sum(sqs[:], x2sl[:], axis=AX.X),
                 [(s["sqr"], 1)])
            S_.edge("v", lambda e: e.tensor_scalar(mu[:], ssum[:], 1.0 / D,
                                                   None, ALU.mult), s["vch"])
            S_.edge("v", lambda e: e.tensor_scalar(vv[:], sqs[:], 1.0 / D,
                                                   None, ALU.mult), s["vch"])
            S_.edge("v", lambda e: e.tensor_tensor(m2[:], mu[:], mu[:],
                                                   ALU.mult), s["vch"])
            S_.i("v", lambda e: e.tensor_tensor(vv[:], vv[:], m2[:],
                                                ALU.subtract), [(s["lnm"], 1)])
            S_.w("sc", s["lnm"], s["lnm"].n)
            S_.edge("sc", lambda e: e.activation(vv[:], vv[:], AF.Ln,
                                                 bias=epst[:, 0:1]), s["sch"])
            S_.i("sc", lambda e: e.activation(rr[:], vv[:], AF.Exp,
                                              scale=-0.5), [(s["lnm"], 1)])
            S_.w("v", s["lnm"], s["lnm"].n)
            for dg in range(8):
                dsl = slice(512 * dg, 512 * (dg + 1))
                S_.w("v", s["trp"], s["trp"].n)
                S_.edge("v", lambda e, dsl=dsl: e.tensor_scalar(
                    ysl[0][:], rsl[:, dsl], mu[:, 0:1], rr[:, 0:1],
                    ALU.subtract, ALU.mult), s["vch"])
                S_.i("v", lambda e, dsl=dsl: e.tensor_scalar(
                    rsl[:, dsl], rsl[:, dsl], mu[:, 0:1], rr[:, 0:1],
                    ALU.subtract, ALU.mult), [(s["x3p"], 1)])
                S_.w("te", s["x3p"], s["x3p"].n)
                S_.w("te", s["idr"], 1)
                S_.w("te", s["tre"], s["tre"].n)
                for jj in range(4):
                    S_.i("te", lambda e, jj=jj: e.matmul(
                        ps8[6][:, 128 * jj:128 * (jj + 1)],
                        ysl[0][:, 128 * jj:128 * (jj + 1)], idf[:],
                        is_transpose=True, start=(jj == 0), stop=(jj == 3)),
                        [(s["trp"], 1)] if jj == 3 else ())
                S_.w("sc", s["trp"], s["trp"].n)
                S_.i("sc", lambda e, dg=dg, tt=tt, TPE=TPE: e.copy(
                    x3fm[:, :].rearrange("p (c t) -> p c t", t=TPE)[
                        :, 4 * dg:4 * (dg + 1), 128 * tt:128 * (tt + 1)],
                    ps8[6][:, :].rearrange("p (c x) -> p c x", x=128)),
                    [(s["tre"], 1)])
            S_.w("sy", s["x3p"], s["x3p"].n)
            S_.i("sy", lambda e, tt=tt: e.dma_start(
                out=x3d[tt], in_=rsl[:]), [(s["x3w"], 16)])

        nwg = 0
        nwd = [0, 0]
        nyr = [0, 0]
        nyw = [0, 0]
        gpi = 2 * NTH2
        for p, (i0_, i1_) in enumerate(ihr):
            L = i1_ - i0_
            for jl, ich in enumerate(range(i0_, i1_)):
                S_.w("sy", s["f1m"], s["f1m"].n)
                S_.i("sy", lambda e, ich=ich: e.dma_start(
                    out=wgsl[:], in_=wgt[ich]), [(s["lwg"], 16)])
                S_.i("sy", lambda e, ich=ich: e.dma_start(
                    out=wusl[:], in_=wut[ich]), [(s["lwg"], 16)])
                nwg += 2
                for proj in range(2):
                    wsl = wgsl if proj == 0 else wusl
                    for th2 in range(NTH2):
                        bank = proj * NTH2 + th2
                        S_.w("te", s["lwg"], 16 * nwg)
                        if jl == 0 and proj == 0 and th2 == 0:
                            S_.w("te", s["tre"], s["tre"].n)
                            S_.w("te", s["f2e"], s["f2e"].n)
                        S_.w("te", s["f1h"], s["f1h"].n)
                        for dch in range(32):
                            S_.i("te", lambda e, wsl=wsl, dch=dch, th2=th2,
                                 bank=bank, TPE=TPE: e.matmul(
                                ps8[bank][:],
                                wsl[:, 128 * dch:128 * (dch + 1)],
                                x3fm[:, dch * TPE + 512 * th2:
                                     dch * TPE + 512 * (th2 + 1)],
                                start=(dch == 0), stop=(dch == 31)),
                                [(s["f1m"], 1)] if dch == 31 else ())
                for th2 in range(NTH2):
                    su = NTH2 * jl + th2
                    slq = su % 2
                    S_.w("sc", s["f1m"], s["f1m"].n - 2 * NTH2 + th2 + 1)
                    S_.w("sc", s["f1h"], s["f1h"].n)
                    if jl == 0 and th2 == 0:
                        S_.w("sc", s["trp"], s["trp"].n)
                        S_.w("sc", s["ypw0"], 16 * nyw[0])
                        S_.w("sc", s["ypw1"], 16 * nyw[1])
                    S_.i("sc", lambda e, th2=th2, slq=slq: e.activation(
                        ysl[slq][:], ps8[th2][:], AF.Silu), [(s["f1s"], 1)])
                    S_.w("v", s["f1s"], s["f1s"].n)
                    S_.w("v", s["f1m"], s["f1m"].n)
                    if p > 0 and jl == 0 and th2 == 0:
                        S_.w("v", s["f2t"], s["f2t"].n)
                    S_.i("v", lambda e, jl=jl, th2=th2, slq=slq,
                         TPE=TPE: e.tensor_tensor(
                        hfm[:, jl * TPE + 512 * th2:jl * TPE + 512 * (th2 + 1)],
                        ysl[slq][:], ps8[NTH2 + th2][:], ALU.mult),
                        [(s["f1h"], 1)])
            if p > 0:
                S_.w("g", s["ypw0"], 16 * nyw[0])
                S_.w("g", s["ypw1"], 16 * nyw[1])
            for dch in range(8):
                S_.w("te", s["f2e"], s["f2e"].n)
                if dch == 0:
                    S_.w("te", s["f1h"], s["f1h"].n)
                for jl in range(L):
                    li = dch * L + jl
                    slot = li % 2
                    S_.w("g", s["f2t"], max(0, s["f2t"].n - 1))
                    S_.i("g", lambda e, ich=i0_ + jl, dch=dch, slot=slot:
                         e.dma_start(out=wdsl[slot][:], in_=wdt[ich, dch]),
                         [(s["lwd%d" % slot], 16)])
                    nwd[slot] += 1
                    S_.w("te", s["lwd%d" % slot], 16 * nwd[slot])
                    for tt in range(NTT):
                        S_.i("te", lambda e, jl=jl, tt=tt, slot=slot, L=L,
                             TPE=TPE: e.matmul(
                            ps8[tt][:],
                            hfm[:, jl * TPE + 128 * tt:jl * TPE + 128 * (tt + 1)],
                            wdsl[slot][:], start=(jl == 0), stop=(jl == L - 1)),
                            [(s["f2t"], 1)] if tt == NTT - 1 else ())
                for tt in range(NTT):
                    sl2 = tt % 2
                    if p > 0:
                        slr = (dch * NTT + tt) % 2
                        S_.w("g", s["f2e"], max(0, s["f2e"].n - 1))
                        S_.i("g", lambda e, dch=dch, tt=tt, slr=slr:
                             e.dma_start(out=ypv[slr][:], in_=ypart[dch, tt]),
                             [(s["lyp%d" % slr], 16)])
                        nyr[slr] += 1
                    S_.w("v", s["f2t"], s["f2t"].n)
                    S_.w("v", s["ypw%d" % sl2], 16 * nyw[sl2])
                    if p == 0:
                        S_.i("v", lambda e, tt=tt, sl2=sl2: e.tensor_copy(
                            ysl[sl2][:], ps8[tt][:]), [(s["f2e"], 1)])
                    else:
                        S_.w("v", s["lyp%d" % slr], 16 * nyr[slr])
                        S_.i("v", lambda e, tt=tt, sl2=sl2, slr=slr:
                             e.tensor_tensor(ysl[sl2][:], ps8[tt][:],
                                             ypv[slr][:], ALU.add),
                             [(s["f2e"], 1)])
                    S_.w("sy", s["f2e"], s["f2e"].n)
                    S_.i("sy", lambda e, dch=dch, tt=tt, sl2=sl2: e.dma_start(
                        out=ypart[dch, tt], in_=ysl[sl2][:]),
                        [(s["ypw%d" % sl2], 16)])
                    nyw[sl2] += 1

        nyg = 0
        nout = 0
        for tt in range(NTT):
            S_.w("sy", s["sqr"], NTT + tt)
            S_.w("sy", s["x3w"], 16 * NTT)
            S_.i("sy", lambda e, tt=tt: e.dma_start(
                out=x2sl[:], in_=x3d[tt]), [(s["lx2"], 16)])
            nx2 += 1
            for hh in range(2):
                S_.w("sy", s["ypw0"], 16 * nyw[0])
                S_.w("sy", s["ypw1"], 16 * nyw[1])
                S_.w("sy", s["outw"], 16 * nout)
                S_.i("sy", lambda e, tt=tt, hh=hh: e.dma_start(
                    out=ybuf[:, :].rearrange("p (c x) -> p c x", x=512),
                    in_=ypart[4 * hh:4 * (hh + 1), tt].rearrange(
                        "d p x -> p d x")), [(s["lyg"], 16)])
                nyg += 1
                S_.w("v", s["lyg"], 16 * nyg)
                S_.w("v", s["lx2"], 16 * nx2)
                dsl = slice(2048 * hh, 2048 * (hh + 1))
                S_.i("v", lambda e, dsl=dsl: e.tensor_tensor(
                    rsl[:, dsl], ybuf[:], x2sl[:, dsl], ALU.add),
                    [(s["aev"], 1)])
                S_.w("sy", s["aev"], s["aev"].n)
            S_.w("v", s["aev"], s["aev"].n)
            S_.edge("v", lambda e: e.reduce_sum(ssum[:], rsl[:], axis=AX.X),
                    s["vch"])
            S_.w("sc", s["aev"], s["aev"].n)
            S_.i("sc", lambda e: e.activation(x2sl[:], rsl[:], AF.Square),
                 [(s["sqd"], 1)])
            S_.w("v", s["sqd"], s["sqd"].n)
            S_.i("v", lambda e: e.reduce_sum(sqs[:], x2sl[:], axis=AX.X),
                 [(s["sqr"], 1)])
            S_.edge("v", lambda e: e.tensor_scalar(mu[:], ssum[:], 1.0 / D,
                                                   None, ALU.mult), s["vch"])
            S_.edge("v", lambda e: e.tensor_scalar(vv[:], sqs[:], 1.0 / D,
                                                   None, ALU.mult), s["vch"])
            S_.edge("v", lambda e: e.tensor_tensor(m2[:], mu[:], mu[:],
                                                   ALU.mult), s["vch"])
            S_.i("v", lambda e: e.tensor_tensor(vv[:], vv[:], m2[:],
                                                ALU.subtract), [(s["lnm"], 1)])
            S_.w("sc", s["lnm"], s["lnm"].n)
            S_.edge("sc", lambda e: e.activation(vv[:], vv[:], AF.Ln,
                                                 bias=epst[:, 0:1]), s["sch"])
            S_.i("sc", lambda e: e.activation(rr[:], vv[:], AF.Exp,
                                              scale=-0.5), [(s["lnm"], 1)])
            S_.w("v", s["lnm"], s["lnm"].n)
            for hh in range(2):
                dsl = slice(2048 * hh, 2048 * (hh + 1))
                S_.w("v", s["outw"], 16 * nout)
                S_.i("v", lambda e, dsl=dsl: e.tensor_scalar(
                    ybuf[:], rsl[:, dsl], mu[:, 0:1], rr[:, 0:1],
                    ALU.subtract, ALU.mult), [(s["x3p"], 1)])
                S_.w("sy", s["x3p"], s["x3p"].n)
                S_.i("sy", lambda e, tt=tt, hh=hh: e.dma_start(
                    out=out[128 * tt:128 * (tt + 1),
                            2048 * hh:2048 * (hh + 1)], in_=ybuf[:]),
                    [(s["outw"], 16)])
                nout += 1
        S_.w("sy", s["outw"], 16 * nout)

        blk = ctx.enter_context(nc.Block())

        @blk.gpsimd
        def _(e):
            S_.replay("g", e)

        @blk.sync
        def _(e):
            S_.replay("sy", e)

        @blk.tensor
        def _(e):
            S_.replay("te", e)

        @blk.vector
        def _(e):
            S_.replay("v", e)

        @blk.scalar
        def _(e):
            S_.replay("sc", e)

    return nc



def _core_heads(c):
    return list(range(4 * c, 4 * c + 4))


def _qkv_cols(c):
    hs = _core_heads(c)
    cols = []
    for part in range(3):
        for h in hs:
            cols.extend(range(part * D + h * HD, part * D + (h + 1) * HD))
    return np.array(cols)


def _pad_cols(a, total, axis):
    pad = total - a.shape[axis]
    if pad == 0:
        return a
    widths = [(0, 0)] * a.ndim
    widths[axis] = (0, pad)
    return np.pad(a, widths, mode="edge")


def stage_A(inputs):
    x = np.ascontiguousarray(
        inputs["L0_attn_hidden_states"].reshape(T, D).T)
    wqkv_t = np.ascontiguousarray(inputs["L0_attn_Wqkv"].T)
    shifts = np.tile(np.arange(8, dtype=np.int32) * 4, 16).reshape(128, 1)
    in_maps = []
    gqw = inputs["L1_ffn_gate_qweight"]
    gsc = inputs["L1_ffn_gate_scales"]
    gqz = inputs["L1_ffn_gate_qzeros"]
    ggi = inputs["L1_ffn_gate_g_idx"]
    uqw = inputs["L1_ffn_up_qweight"]
    usc = inputs["L1_ffn_up_scales"]
    uqz = inputs["L1_ffn_up_qzeros"]
    ugi = inputs["L1_ffn_up_g_idx"]
    dqw = inputs["L1_ffn_down_qweight"]
    dsc = inputs["L1_ffn_down_scales"]
    dqz = inputs["L1_ffn_down_qzeros"]
    dgi = inputs["L1_ffn_down_g_idx"]
    for c in range(NCORES):
        cols = _qkv_cols(c)
        wq = np.ascontiguousarray(wqkv_t[:, cols])
        n0, n1 = core_inter_range(c)
        m = {
            "xfm": x, "wq": wq, "shifts": shifts,
            "qwg": _pad_cols(np.ascontiguousarray(gqw[:, n0:n1]), NPC, 1),
            "scg": _pad_cols(np.ascontiguousarray(gsc[:, n0:n1]), NPC, 1),
            "qzg": _pad_cols(np.ascontiguousarray(gqz[:, n0 // 8:n1 // 8]), NPC // 8, 1),
            "gig": np.ascontiguousarray(np.broadcast_to(ggi[None, :], (32, D))),
            "qwu": _pad_cols(np.ascontiguousarray(uqw[:, n0:n1]), NPC, 1),
            "scu": _pad_cols(np.ascontiguousarray(usc[:, n0:n1]), NPC, 1),
            "qzu": _pad_cols(np.ascontiguousarray(uqz[:, n0 // 8:n1 // 8]), NPC // 8, 1),
            "qwd": _pad_rows(np.ascontiguousarray(dqw[n0 // 8:n1 // 8, :]), NPC // 8),
            "scd": _pad_rows_groups(dsc, c),
            "qzd": _pad_rows_groups(dqz, c),
            "gid": _pad_cols(np.ascontiguousarray(dgi[None, n0:n1]), NPC, 1).repeat(NGC, 0).copy(),
            "gidv": group_ids_down(c),
        }
        in_maps.append(m)
    return in_maps


def core_inter_range(c):
    gstarts = []
    g0 = 0
    for i in range(NCORES):
        ng = 11 if i < 6 else 10
        gstarts.append((g0, g0 + ng))
        g0 += ng
    a, b = gstarts[c]
    return a * GRP, b * GRP


def group_ids_down(c):
    n0, _ = core_inter_range(c)
    return np.arange(n0 // GRP, n0 // GRP + NGC, dtype=np.float32).reshape(NGC, 1)


def _pad_rows(a, total):
    if a.shape[0] >= total:
        return a
    return np.pad(a, ((0, total - a.shape[0]), (0, 0)), mode="edge")


def _pad_rows_groups(a, c):
    n0, n1 = core_inter_range(c)
    g0, g1 = n0 // GRP, n1 // GRP
    sl = np.ascontiguousarray(a[g0:g1])
    if sl.shape[0] < NGC:
        sl = np.pad(sl, ((0, NGC - sl.shape[0]), (0, 0)), mode="edge")
    return sl



def build_C(TP=TPC, DBG=False, PH=5):
    nc = _bass()
    NT = TP // 128

    ofm = nc.dram_tensor("ofm", [D, TP], F32, kind="ExternalInput").ap()
    xtm = nc.dram_tensor("xtm", [TP, D], F32, kind="ExternalInput").ap()
    wot = nc.dram_tensor("wot", [D, D], F32, kind="ExternalInput").ap()
    kts = [nc.dram_tensor(f"k{i}t", [8, 128, 32, 512], U8, kind="ExternalInput").ap()
           for i in range(3)]
    s0b = nc.dram_tensor("s0b", [128, 4], F32, kind="ExternalInput").ap()
    eic = nc.dram_tensor("eic", [128, 32], F32, kind="ExternalInput").ap()
    wqct = nc.dram_tensor("wqct", [D, QL], F32, kind="ExternalInput").ap()
    bqc = nc.dram_tensor("bqc", [128, 12], F32, kind="ExternalInput").ap()
    wkvct = nc.dram_tensor("wkvct", [D, 576], F32, kind="ExternalInput").ap()
    bkvc = nc.dram_tensor("bkvc", [128, 5], F32, kind="ExternalInput").ap()

    x2tm = nc.dram_tensor("x2tm", [TP, D], BF16, kind="ExternalOutput").ap()
    qc = nc.dram_tensor("qc", [QL, TP], BF16, kind="ExternalOutput").ap()
    kvc = nc.dram_tensor("kvc", [576, TP], BF16, kind="ExternalOutput").ap()
    att = nc.dram_tensor("attb", [TP, D], BF16).ap()
    dbgs = [nc.dram_tensor(f"dbg{i}", [128, 32 * TP], BF16,
                           kind="ExternalOutput").ap() for i in range(2)] if DBG else None
    dbga = nc.dram_tensor("dbga", [128, 32 * TP], BF16,
                          kind="ExternalOutput").ap() if DBG else None
    dbgb = nc.dram_tensor("dbgb", [128, 32 * TP], F32,
                          kind="ExternalOutput").ap() if DBG else None
    dbgr = nc.dram_tensor("dbgr", [1, TP], F32,
                          kind="ExternalOutput").ap() if DBG else None
    dbgw = nc.dram_tensor("dbgw", [128, 32 * 512], BF16,
                          kind="ExternalOutput").ap() if DBG else None
    dbgsx = nc.dram_tensor("dbgsx", [128, 4], F32,
                           kind="ExternalOutput").ap() if DBG else None
    z2d = nc.dram_tensor("z2d", [TP, D], BF16).ap()
    x1d = nc.dram_tensor("x1d", [TP, D], BF16).ap()

    ctx = contextlib.ExitStack()
    with ctx:
        sb = lambda *a: ctx.enter_context(nc.sbuf_tensor(*a))
        pt = lambda *a: ctx.enter_context(nc.psum_tensor(*a))
        arA = sb("arA", [128, 32 * TP], BF16)
        arB = sb("arB", [128, 32 * TP], BF16)
        wk = sb("wk", [128, 32 * 512], BF16)
        ku = [sb("ku0", [128, 32 * 512], U8)]
        slab1 = sb("slab1", [128, D], BF16)
        slab2 = sb("slab2", [128, D], BF16)
        asb = [sb(f"asb{i}", [128, 512], BF16) for i in range(2)]
        rsx = sb("rsx", [1, TP], F32)
        nego = sb("nego", [1, 512], F32)
        onec = sb("onec", [128, 1], BF16)
        sc4 = sb("sc4", [128, 4], F32)
        eisb = sb("eisb", [128, 32], F32)
        sums = sb("sums", [128, 1], F32)
        sqs = sb("sqs", [128, 1], F32)
        mu = sb("mu", [128, 1], F32)
        var = sb("var", [128, 1], F32)
        rr = sb("rr", [128, 1], F32)
        epst = sb("epstC", [128, 1], F32)
        s2x = sb("s2x", [128, 4], F32)
        idf = sb("idfC", [128, 128], F32)
        idb = sb("idbC", [128, 128], BF16)
        bsb = sb("bsb", [128, 12], F32)
        bsb2 = sb("bsb2", [128, 5], F32)

        ps_a = [pt(f"psa{i}", [128, 512], F32) for i in range(2)]
        ps_r = pt("psr", [1, TP], F32)
        ps_t = pt("pstC", [128, 512], BF16)

        s = make_sems(ctx, nc, [
            "ldo", "ldw", "ldx", "lds", "ldk", "lda", "ldz", "ldx1",
            "mm", "ev", "sq", "sts", "sts0", "sts1", "trp", "tre", "rsm", "rse",
            "stx", "stq", "vch", "sch", "gch"])

        S_ = Sched()

        class StopSched(Exception):
            pass

        S_.i("g", lambda e: e.iota(idf[:], pattern=[[1, 128]], base=0,
                                   channel_multiplier=-1,
                                   allow_small_or_imprecise_dtypes=True),
             [(s["gch"], 1)])
        S_.w("v", s["gch"], 1)
        S_.edge("v", lambda e: e.tensor_scalar(idf[:], idf[:], 0.0, None,
                                               ALU.is_equal), s["vch"])
        S_.edge("v", lambda e: e.tensor_copy(idb[:], idf[:]), s["vch"])
        S_.edge("v", lambda e: e.memset(epst[:], LN_EPS), s["vch"])
        S_.edge("v", lambda e: e.memset(nego[:], -1.0), s["vch"])
        S_.edge("v", lambda e: e.memset(onec[:], 1.0), s["vch"])
        S_.i("sy", lambda e: e.dma_start(out=sc4[:], in_=s0b[:, :]), [(s["lds"], 16)])
        S_.i("sy", lambda e: e.dma_start(out=eisb[:], in_=eic[:, :]), [(s["lds"], 16)])
        S_.i("sy", lambda e: e.dma_start(out=bsb[:, 0:12], in_=bqc[:, :]),
             [(s["lds"], 16)])
        S_.i("sy", lambda e: e.dma_start(out=bsb2[:, 0:5], in_=bkvc[:, :]),
             [(s["lds"], 16)])
        S_.w("v", s["lds"], 64)
        S_.edge("v", lambda e: e.tensor_scalar(s2x[:, 0:2], sc4[:, 0:2], 2.0,
                                               None, ALU.mult), s["vch"])
        S_.edge("v", lambda e: e.tensor_copy(s2x[:, 2:3], sc4[:, 2:3]), s["vch"])
        S_.i("g", lambda e: e.dma_start(
            out=arA[:], in_=ofm.rearrange("(c p) t -> p c t", p=128)),
            [(s["ldo"], 16)])

        try:
            for ech in range(8):
                S_.w("g", s["mm"], ech * NT)
                S_.i("g", lambda e, ech=ech: e.dma_start(
                    out=wk[:], in_=wot.rearrange("(c p) e -> p c e", p=128)
                    [:, :, 512 * ech:512 * (ech + 1)]), [(s["ldw"], 16)])
                for tch in range(NT):
                    i0 = ech * NT + tch
                    S_.w("te", s["ldo"], 16)
                    S_.w("te", s["ldw"], 16 * (ech + 1))
                    S_.w("te", s["ev"], max(0, i0 - 1))
                    for dd in range(32):
                        S_.i("te", lambda e, dd=dd, tch=tch, i0=i0: e.matmul(
                            ps_a[i0 % 2][:],
                            arA[:, dd * TP + 128 * tch:dd * TP + 128 * (tch + 1)],
                            wk[:, 512 * dd:512 * (dd + 1)],
                            start=(dd == 0), stop=(dd == 31)),
                            [(s["mm"], 1)] if dd == 31 else ())
                    S_.w("v", s["mm"], s["mm"].n)
                    S_.w("v", s["sts%d" % (i0 % 2)].h if False else s["sts%d" % (i0 % 2)],
                         s["sts%d" % (i0 % 2)].n)
                    S_.i("v", lambda e, i0=i0: e.tensor_copy(asb[i0 % 2][:],
                                                             ps_a[i0 % 2][:]),
                         [(s["ev"], 1)])
                    S_.w("sy", s["ev"], s["ev"].n)
                    S_.i("sy", lambda e, ech=ech, tch=tch, i0=i0: e.dma_start(
                        out=att[128 * tch:128 * (tch + 1), 512 * ech:512 * (ech + 1)],
                        in_=asb[i0 % 2][:]), [(s["sts%d" % (i0 % 2)], 16)])

            NSTS0, NSTS1 = s["sts0"].n, s["sts1"].n
            prev_vx1 = 0
            for tch in range(NT):
                S_.w("sy", s["sts0"], NSTS0)
                S_.w("sy", s["sts1"], NSTS1)
                S_.w("sy", s["vch"], prev_vx1)
                S_.i("sy", lambda e, tch=tch: e.dma_start(
                    out=slab1[:], in_=att[128 * tch:128 * (tch + 1), :]),
                    [(s["lda"], 16)])
                S_.w("g", s["stx"], 16 * tch)
                S_.w("g", s["trp"], 8 * tch)
                S_.w("g", s["vch"], prev_vx1)
                S_.i("g", lambda e, tch=tch: e.dma_start(
                    out=slab2[:], in_=xtm[128 * tch:128 * (tch + 1), :]),
                    [(s["ldx"], 16)])
                S_.w("v", s["lda"], 16 * (tch + 1))
                S_.w("v", s["ldx"], 16 * (tch + 1))
                S_.edge("v", lambda e: e.tensor_tensor(slab1[:], slab1[:], slab2[:],
                                                       ALU.add), s["vch"])
                S_.edge("v", lambda e: e.reduce_sum(sums[:], slab1[:], axis=AX.X),
                        s["vch"])
                VCHQ = s["vch"].n
                S_.w("sc", s["vch"], VCHQ)
                S_.i("sc", lambda e: e.activation(slab2[:], slab1[:], AF.Square),
                     [(s["sq"], 1)])
                S_.w("v", s["sq"], s["sq"].n)
                S_.edge("v", lambda e: e.reduce_sum(sqs[:], slab2[:], axis=AX.X),
                        s["vch"])
                S_.edge("v", lambda e: e.tensor_scalar(mu[:], sums[:], 1.0 / D, None,
                                                       ALU.mult), s["vch"])
                S_.edge("v", lambda e: e.tensor_tensor(var[:], mu[:], mu[:],
                                                       ALU.mult), s["vch"])
                S_.edge("v", lambda e: e.scalar_tensor_tensor(
                    var[:], sqs[:], 1.0 / D, var[:], ALU.mult, ALU.subtract),
                    s["vch"])
                VCHV = s["vch"].n
                S_.w("sc", s["vch"], VCHV)
                S_.edge("sc", lambda e: e.activation(var[:], var[:], AF.Ln,
                                                     bias=epst[:, 0:1]), s["sch"])
                S_.i("sc", lambda e: e.activation(rr[:], var[:], AF.Exp, scale=-0.5),
                     [(s["sq"], 1)])
                S_.w("v", s["sq"], s["sq"].n)
                S_.edge("v", lambda e: e.tensor_scalar(slab2[:], slab1[:], mu[:, 0:1],
                                                       rr[:, 0:1], ALU.subtract,
                                                       ALU.mult), s["vch"])
                prev_vx1 = s["vch"].n
                S_.w("sy", s["vch"], s["vch"].n)
                S_.i("sy", lambda e, tch=tch: e.dma_start(
                    out=x1d[128 * tch:128 * (tch + 1), :], in_=slab2[:]),
                    [(s["stx"], 16)])
                for j4 in range(8):
                    S_.w("te", s["vch"], s["vch"].n)
                    S_.w("te", s["tre"], s["tre"].n)
                    for j in range(4):
                        S_.i("te", lambda e, j4=j4, j=j: e.matmul(
                            ps_t[:, 128 * j:128 * (j + 1)],
                            slab2[:, 512 * j4 + 128 * j:512 * j4 + 128 * (j + 1)],
                            idb[:], is_transpose=True, start=(j == 0), stop=(j == 3)),
                            [(s["trp"], 1)] if j == 3 else ())
                    S_.w("sc", s["trp"], s["trp"].n)
                    dst = arB[:, :].rearrange("p (c t) -> p c t", c=32)[
                        :, 4 * j4:4 * (j4 + 1), 128 * tch:128 * (tch + 1)]
                    S_.i("sc", lambda e, dst=dst: e.copy(
                        dst, ps_t[:].rearrange("p (c t) -> p c t", c=4)),
                        [(s["tre"], 1)])

            if DBG:
                S_.w("sy", s["tre"], s["tre"].n)
                S_.i("sy", lambda e: e.dma_start(out=dbga[:, :], in_=arB[:]),
                     [(s["stq"], 16)])
            def rowsums(src_ar, stg, only=0):
                S_.w("te", s["ev"], s["ev"].n)
                S_.w("te", s["tre"], s["tre"].n)
                if only == 1:
                    return
                for th in range(TP // 512):
                    S_.w("te", s["rse"], s["rse"].n)
                    for dd in range(32):
                        S_.i("te", lambda e, dd=dd, th=th: e.matmul(
                            ps_r[:, 512 * th:512 * (th + 1)],
                            onec[:],
                            src_ar[:, dd * TP + 512 * th:dd * TP + 512 * (th + 1)],
                            start=(dd == 0), stop=(dd == 31)),
                            [(s["rsm"], 1)] if dd == 31 else ())
                    if only == 2:
                        continue
                    S_.w("v", s["rsm"], s["rsm"].n)
                    S_.i("v", lambda e, th=th: e.tensor_copy(
                        rsx[:, 512 * th:512 * (th + 1)],
                        ps_r[:, 512 * th:512 * (th + 1)]), [(s["rse"], 1)])

            for stg in range(2):
                if PH < stg + 1:
                    break
                src_ar = arB if stg == 0 else arA
                dst_ar = arA if stg == 0 else arB
                kt = kts[stg]
                rowsums(src_ar if PH != 16 else arB, stg,
                        only=(1 if PH == 17 else 2 if PH == 18 else 0))
                RSE = s["rse"].n
                if PH in (15, 16, 17, 18) and stg == 1:
                    raise StopSched
                if DBG and stg == 0:
                    S_.w("sy", s["rse"], RSE)
                    S_.i("sy", lambda e: e.dma_start(out=dbgr[:, :], in_=rsx[:, 0:TP]),
                         [(s["stq"], 16)])
                    S_.w("sy", s["sq"], s["sq"].n + 1)
                    S_.i("sy", lambda e: e.dma_start(out=dbgw[:, :], in_=wk[:]),
                         [(s["stq"], 16)])
                    S_.i("sy", lambda e: e.dma_start(out=dbgsx[:, :], in_=sc4[:]),
                         [(s["stq"], 16)])
                for ng in range(8):
                    li = stg * 8 + ng
                    S_.w("g", s["mm"], s["mm"].n)
                    S_.w("g", s["sq"], s["sq"].n)
                    S_.i("g", lambda e, kt=kt, ng=ng: e.dma_start(
                        out=ku[0][:], in_=kt[ng, :, :, :]), [(s["ldk"], 16)])
                    S_.w("sc", s["ldk"], 16 * (li + 1))
                    S_.w("sc", s["mm"], s["mm"].n)
                    S_.i("sc", lambda e, ng=ng: e.activation(
                        wk[:], ku[0][:], AF.Copy, scale=2.0), [(s["sq"], 1)])
                    for nq in range(4):
                        nch = 4 * ng + nq
                        for th in range(TP // 512):
                            i0 = (stg * 8 + ng) * 4 * (TP // 512) + nq * (TP // 512) + th
                            S_.w("te", s["sq"], s["sq"].n)
                            S_.w("te", s["rse"], RSE)
                            S_.w("te", s["tre"], s["tre"].n)
                            S_.w("te", s["ev"], s["ev"].n)
                            PSI = (s["mm"].n) % 2
                            for dd in range(32):
                                S_.i("te", lambda e, dd=dd, nq=nq, th=th, PSI=PSI,
                                     src_ar=src_ar: e.matmul(
                                    ps_a[PSI][:],
                                    wk[:, 512 * dd + 128 * nq:512 * dd + 128 * (nq + 1)],
                                    src_ar[:, dd * TP + 512 * th:dd * TP + 512 * (th + 1)],
                                    start=(dd == 0), stop=False))
                            S_.i("te", lambda e, th=th, PSI=PSI: e.matmul(
                                ps_a[PSI][:], nego[0:1, 0:128],
                                rsx[0:1, 512 * th:512 * (th + 1)],
                                start=False, stop=True), [(s["mm"], 1)])
                            S_.w("sc", s["mm"], s["mm"].n)
                            S_.w("sc", s["ev"], s["ev"].n)
                            S_.i("sc", lambda e, PSI=PSI, stg=stg: e.activation(
                                asb[PSI][:], ps_a[PSI][:], AF.Relu,
                                scale=s2x[:, stg:stg + 1]), [(s["sq"], 1)])
                            S_.w("v", s["sq"], s["sq"].n)
                            dst = dst_ar[:, nch * TP + 512 * th:nch * TP + 512 * (th + 1)]
                            S_.i("v", lambda e, dst=dst, PSI=PSI, nch=nch: e.tensor_scalar(
                                dst, asb[PSI][:], eisb[:, nch:nch + 1], None,
                                ALU.mult), [(s["ev"], 1)])
                            if DBG:
                                S_.w("sy", s["ev"], s["ev"].n)
                                S_.i("sy", lambda e, stg=stg, nch=nch, th=th, dst=dst: e.dma_start(
                                    out=dbgs[stg][:, nch * TP + 512 * th:nch * TP + 512 * (th + 1)],
                                    in_=dst), [(s["stq"], 16)])
                                if stg == 0:
                                    S_.i("sy", lambda e, nch=nch, th=th, PSI=PSI: e.dma_start(
                                        out=dbgb[:, nch * TP + 512 * th:nch * TP + 512 * (th + 1)],
                                        in_=asb[PSI][:]), [(s["stq"], 16)])

            if PH < 3:
                raise StopSched
            rowsums(arB, 2)
            RSE = s["rse"].n
            for ech in range(8):
                li = 16 + ech
                S_.w("g", s["mm"], s["mm"].n)
                S_.w("g", s["sq"], s["sq"].n)
                S_.i("g", lambda e, ech=ech: e.dma_start(
                    out=ku[0][:], in_=kts[2][ech, :, :, :]), [(s["ldk"], 16)])
                S_.w("sc", s["ldk"], 16 * (li + 1))
                S_.w("sc", s["mm"], s["mm"].n)
                S_.i("sc", lambda e, ech=ech: e.activation(
                	wk[:], ku[0][:], AF.Copy, scale=2.0), [(s["sq"], 1)])
                for tch in range(NT):
                    i0 = ech * NT + tch
                    S_.w("te", s["sq"], s["sq"].n)
                    S_.w("te", s["rse"], RSE)
                    S_.w("te", s["tre"], s["tre"].n)
                    S_.w("te", s["ev"], s["ev"].n)
                    PSI = s["mm"].n % 2
                    for dd in range(32):
                        S_.i("te", lambda e, dd=dd, tch=tch, PSI=PSI: e.matmul(
                            ps_a[PSI][:],
                            arB[:, dd * TP + 128 * tch:dd * TP + 128 * (tch + 1)],
                            wk[:, 512 * dd:512 * (dd + 1)],
                            start=(dd == 0), stop=False))
                    S_.i("te", lambda e, tch=tch, PSI=PSI: e.matmul(
                        ps_a[PSI][:], rsx[0:1, 128 * tch:128 * (tch + 1)],
                        nego[0:1, 0:512], start=False, stop=True), [(s["mm"], 1)])
                    S_.w("v", s["mm"], s["mm"].n)
                    S_.w("v", s["sts"], s["sts"].n)
                    S_.i("v", lambda e, PSI=PSI: e.tensor_scalar(
                        asb[PSI][:], ps_a[PSI][:], s2x[:, 2:3], None, ALU.mult),
                        [(s["ev"], 1)])
                    S_.w("sy", s["ev"], s["ev"].n)
                    S_.i("sy", lambda e, ech=ech, tch=tch, PSI=PSI: e.dma_start(
                        out=z2d[128 * tch:128 * (tch + 1), 512 * ech:512 * (ech + 1)],
                        in_=asb[PSI][:]), [(s["sts"], 16)])

            if PH < 4:
                raise StopSched
            NSTS2 = s["sts"].n
            prev_vx2 = 0
            for tch in range(NT):
                S_.w("sy", s["sts"], NSTS2)
                S_.w("sy", s["vch"], prev_vx2)
                S_.i("sy", lambda e, tch=tch: e.dma_start(
                    out=slab1[:], in_=z2d[128 * tch:128 * (tch + 1), :]),
                    [(s["ldz"], 16)])
                S_.w("g", s["stx"], s["stx"].n)
                S_.w("g", s["trp"], s["trp"].n)
                S_.w("g", s["tre"], s["tre"].n)
                S_.i("g", lambda e, tch=tch: e.dma_start(
                    out=slab2[:], in_=x1d[128 * tch:128 * (tch + 1), :]),
                    [(s["ldx1"], 16)])
                S_.w("v", s["ldz"], s["ldz"].n)
                S_.w("v", s["ldx1"], s["ldx1"].n)
                S_.edge("v", lambda e: e.tensor_tensor(slab1[:], slab1[:], slab2[:],
                                                       ALU.add), s["vch"])
                S_.edge("v", lambda e: e.reduce_sum(sums[:], slab1[:], axis=AX.X),
                        s["vch"])
                S_.w("sc", s["vch"], s["vch"].n)
                S_.i("sc", lambda e: e.activation(slab2[:], slab1[:], AF.Square),
                     [(s["sq"], 1)])
                S_.w("v", s["sq"], s["sq"].n)
                S_.edge("v", lambda e: e.reduce_sum(sqs[:], slab2[:], axis=AX.X),
                        s["vch"])
                S_.edge("v", lambda e: e.tensor_scalar(mu[:], sums[:], 1.0 / D, None,
                                                       ALU.mult), s["vch"])
                S_.edge("v", lambda e: e.tensor_tensor(var[:], mu[:], mu[:],
                                                       ALU.mult), s["vch"])
                S_.edge("v", lambda e: e.scalar_tensor_tensor(
                    var[:], sqs[:], 1.0 / D, var[:], ALU.mult, ALU.subtract),
                    s["vch"])
                S_.w("sc", s["vch"], s["vch"].n)
                S_.edge("sc", lambda e: e.activation(var[:], var[:], AF.Ln,
                                                     bias=epst[:, 0:1]), s["sch"])
                S_.i("sc", lambda e: e.activation(rr[:], var[:], AF.Exp, scale=-0.5),
                     [(s["sq"], 1)])
                S_.w("v", s["sq"], s["sq"].n)
                S_.edge("v", lambda e: e.tensor_scalar(slab2[:], slab1[:], mu[:, 0:1],
                                                       rr[:, 0:1], ALU.subtract,
                                                       ALU.mult), s["vch"])
                prev_vx2 = s["vch"].n
                S_.w("sy", s["vch"], s["vch"].n)
                S_.i("sy", lambda e, tch=tch: e.dma_start(
                    out=x2tm[128 * tch:128 * (tch + 1), :], in_=slab2[:]),
                    [(s["stx"], 16)])
                for j4 in range(8):
                    S_.w("te", s["vch"], s["vch"].n)
                    S_.w("te", s["tre"], s["tre"].n)
                    for j in range(4):
                        S_.i("te", lambda e, j4=j4, j=j: e.matmul(
                            ps_t[:, 128 * j:128 * (j + 1)],
                            slab2[:, 512 * j4 + 128 * j:512 * j4 + 128 * (j + 1)],
                            idb[:], is_transpose=True, start=(j == 0), stop=(j == 3)),
                            [(s["trp"], 1)] if j == 3 else ())
                    S_.w("sc", s["trp"], s["trp"].n)
                    dst = arA[:, :].rearrange("p (c t) -> p c t", c=32)[
                        :, 4 * j4:4 * (j4 + 1), 128 * tch:128 * (tch + 1)]
                    S_.i("sc", lambda e, dst=dst: e.copy(
                        dst, ps_t[:].rearrange("p (c t) -> p c t", c=4)),
                        [(s["tre"], 1)])

            if PH < 5:
                raise StopSched
            for proj, wsrc, bias, nchk, outd in ((0, wqct, bsb, 12, qc),
                                                 (1, wkvct, bsb2, 5, kvc)):
                for ec in range(nchk):
                    mrows = 128 if not (proj == 1 and ec == 4) else 64
                    S_.w("g", s["mm"], s["mm"].n)
                    S_.i("g", lambda e, wsrc=wsrc, ec=ec, mrows=mrows: e.dma_start(
                        out=wk[:, 0:32 * mrows],
                        in_=wsrc.rearrange("(c p) q -> p c q", p=128)
                        [:, :, 128 * ec:128 * ec + mrows]), [(s["ldw"], 16)])
                    for th in range(TP // 512):
                        S_.w("te", s["ldw"], 16 * (8 + ec + 1 + (12 if proj else 0)))
                        S_.w("te", s["tre"], s["tre"].n)
                        S_.w("te", s["ev"], s["ev"].n)
                        PSI = s["mm"].n % 2
                        for dd in range(32):
                            S_.i("te", lambda e, dd=dd, th=th, PSI=PSI, mrows=mrows: e.matmul(
                                ps_a[PSI][0:mrows, :],
                                wk[:, mrows * dd:mrows * (dd + 1)],
                                arA[:, dd * TP + 512 * th:dd * TP + 512 * (th + 1)],
                                start=(dd == 0), stop=(dd == 31)),
                                [(s["mm"], 1)] if dd == 31 else ())
                        S_.w("v", s["mm"], s["mm"].n)
                        S_.w("v", s["sts"], s["sts"].n)
                        S_.i("v", lambda e, PSI=PSI, bias=bias, ec=ec, mrows=mrows: e.tensor_scalar(
                            asb[PSI][0:mrows, :], ps_a[PSI][0:mrows, :],
                            bias[0:mrows, ec:ec + 1], None, ALU.add), [(s["ev"], 1)])
                        S_.w("sy", s["ev"], s["ev"].n)
                        S_.i("sy", lambda e, outd=outd, ec=ec, th=th, PSI=PSI, mrows=mrows: e.dma_start(
                            out=outd[128 * ec:128 * ec + mrows, 512 * th:512 * (th + 1)],
                            in_=asb[PSI][0:mrows, :]), [(s["sts"], 16)])

        except StopSched:
            pass
        if DBG:
            S_.w("sy", s["ev"], s["ev"].n)
            S_.w("sy", s["tre"], s["tre"].n)
            S_.w("sy", s["mm"], s["mm"].n)
            S_.i("sy", lambda e: e.dma_start(out=dbga[:, :], in_=arA[:]),
                 [(s["stq"], 16)])
        S_.w("sy", s["sts"], s["sts"].n)
        S_.w("sy", s["stx"], s["stx"].n)
        S_.w("sy", s["stq"], s["stq"].n)

        blk = ctx.enter_context(nc.Block())

        @blk.gpsimd
        def _(e):
            S_.replay("g", e)

        @blk.sync
        def _(e):
            S_.replay("sy", e)

        @blk.tensor
        def _(e):
            S_.replay("te", e)

        @blk.vector
        def _(e):
            S_.replay("v", e)

        @blk.scalar
        def _(e):
            S_.replay("sc", e)

    return nc



def build_D(SD=S):
    nc = _bass()
    TB = B * SD
    NQT = SD // 128
    NKT = SD // 512
    NTH = SD // 512

    qcb = nc.dram_tensor("qcb", [QL, TB], BF16, kind="ExternalInput").ap()
    kvcb = nc.dram_tensor("kvcb", [576, TB], BF16, kind="ExternalInput").ap()
    wqupt = nc.dram_tensor("wqupt", [QL, 768], F32, kind="ExternalInput").ap()
    bqup = nc.dram_tensor("bqup", [128, 8], F32, kind="ExternalInput").ap()
    wkvupt = nc.dram_tensor("wkvupt", [KVL, 1024], F32, kind="ExternalInput").ap()
    bkvup = nc.dram_tensor("bkvup", [128, 8], F32, kind="ExternalInput").ap()
    op = nc.dram_tensor("op", [B, SD, HD], F32, kind="ExternalOutput").ap()

    ctx = contextlib.ExitStack()
    with ctx:
        sb = lambda *a: ctx.enter_context(nc.sbuf_tensor(*a))
        pt = lambda *a: ctx.enter_context(nc.psum_tensor(*a))
        qah = sb("qah", [128, 2 * SD], BF16)
        kvh = sb("kvh", [128, 2 * SD], BF16)
        krop = sb("krop", [64, SD], BF16)
        vtm = sb("vtmD", [128, SD], BF16)
        wq = sb("wqD", [128, 12 * 768], BF16)
        wkv = sb("wkvD", [128, 4 * 1024], BF16)
        bqs = sb("bqsD", [128, 8], F32)
        bkvs = sb("bkvsD", [128, 8], F32)
        qcs = sb("qcsD", [128, 12 * 512], BF16)
        idf = sb("idfD", [128, 128], F32)
        idb = sb("idbD", [128, 128], BF16)
        psb = [sb(f"psbD{i}", [128, 512], BF16) for i in range(2)]
        ptsb = [sb(f"ptsbD{i}", [128, 512], BF16) for i in range(2)]
        osum = sb("osumD", [128, SD], F32)
        oacc = sb("oaccD", [128, HD], F32)
        mrun = sb("mrunD", [128, 1], F32)
        mnew = sb("mnewD", [128, 1], F32)
        psmax = sb("psmaxD", [128, 1], F32)
        biast = sb("biastD", [128, 1], F32)
        corrs = sb("corrsD", [128, 4], F32)
        mdif = sb("mdifD", [128, 1], F32)
        tsum = sb("tsumD", [128, 1], F32)
        dnr = sb("dnrD", [128, 1], F32)
        rcp = sb("rcpD", [128, 4], F32)

        ps_p = [pt(f"pspD{i}", [128, 512], F32) for i in range(2)]
        ps_s = [pt(f"pssD{i}", [128, 512], F32) for i in range(2)]
        ps_pt = [pt(f"psptD{i}", [128, 512], BF16) for i in range(2)]
        ps_o = [pt(f"psoD{i}", [128, HD], F32) for i in range(2)]
        ring = [ps_s[0], ps_s[1], ps_p[0], ps_p[1]]

        s = make_sems(ctx, nc, [
            "ldw", "ldk", "ldb", "ldq", "pjm", "pev", "scm", "sadd", "scorr",
            "sexp", "sdn", "strp", "spte", "sres", "sav", "osu", "ost",
            "vtp", "vte", "vch", "sch", "idr", "gch"])

        S_ = Sched()

        S_.i("g", lambda e: e.iota(idf[:], pattern=[[1, 128]], base=0,
                                   channel_multiplier=-1,
                                   allow_small_or_imprecise_dtypes=True),
             [(s["gch"], 1)])
        S_.i("g", lambda e: e.dma_start(
            out=wq[:], in_=wqupt.rearrange("(c p) q -> p c q", p=128)),
            [(s["ldw"], 16)])
        S_.i("g", lambda e: e.dma_start(
            out=wkv[:], in_=wkvupt.rearrange("(c p) q -> p c q", p=128)),
            [(s["ldw"], 16)])
        S_.i("sy", lambda e: e.dma_start(out=bqs[:], in_=bqup[:, :]),
             [(s["ldb"], 16)])
        S_.i("sy", lambda e: e.dma_start(out=bkvs[:], in_=bkvup[:, :]),
             [(s["ldb"], 16)])
        S_.w("v", s["gch"], 1)
        S_.edge("v", lambda e: e.tensor_scalar(idf[:], idf[:], 0.0, None,
                                               ALU.is_equal), s["vch"])
        S_.i("v", lambda e: e.tensor_copy(idb[:], idf[:]), [(s["idr"], 1)])

        nload = 0
        gT = [0]
        nvt = [0]
        for b in range(B):
            if b > 0:
                S_.w("g", s["scm"], s["scm"].n)
                S_.w("v", s["ost"], 16 * b)
            S_.i("g", lambda e, b=b: e.dma_start(
                out=krop[:], in_=kvcb[KVL:576, SD * b:SD * (b + 1)]),
                [(s["ldk"], 16)])
            for h in range(HPC):
                S_.w("te", s["sexp"], gT[0])
                for which, nkch, wsb_, bias_, dst_ar in (
                        (0, 12, wq, bqs, qah), (1, 4, wkv, bkvs, kvh)):
                    for th in range(NTH):
                        S_.w("sy", s["pjm"], s["pjm"].n)
                        S_.w("sy", s["ldb"], 32)
                        if which == 0:
                            src = qcb.rearrange("(c p) t -> p c t", p=128)[
                                :, :, SD * b + 512 * th:SD * b + 512 * (th + 1)]
                            dwid = 12
                        else:
                            src = kvcb[0:KVL].rearrange("(c p) t -> p c t", p=128)[
                                :, :, SD * b + 512 * th:SD * b + 512 * (th + 1)]
                            dwid = 4
                        S_.i("sy", lambda e, src=src, dwid=dwid: e.dma_start(
                            out=qcs[:, 0:dwid * 512].rearrange(
                                "p (c x) -> p c x", x=512), in_=src),
                            [(s["ldq"], 16)])
                        nload += 1
                        for sub in range(2):
                            if which == 0:
                                mof = 192 * h + 128 * sub
                                mw = 128 if sub == 0 else 64
                            else:
                                mof = 256 * h + 128 * sub
                                mw = 128
                            ppi = s["pjm"].n % 2
                            S_.w("te", s["ldw"], 32)
                            S_.w("te", s["ldq"], 16 * nload)
                            S_.w("te", s["pev"], max(0, s["pjm"].n - 1))
                            if th == 0 and sub == 0:
                                S_.w("te", s["scm"], s["scm"].n)
                            for kc in range(nkch):
                                S_.i("te", lambda e, wsb_=wsb_, kc=kc, mof=mof,
                                     mw=mw, ppi=ppi, which=which,
                                     nkch=nkch: e.matmul(
                                    ps_p[ppi][0:mw, :],
                                    wsb_[:, kc * (768 if which == 0 else 1024) + mof:
                                         kc * (768 if which == 0 else 1024) + mof + mw],
                                    qcs[:, kc * 512:(kc + 1) * 512],
                                    start=(kc == 0), stop=(kc == nkch - 1)),
                                    [(s["pjm"], 1)] if kc == nkch - 1 else ())
                            bcol = 2 * h + sub
                            S_.w("v", s["pjm"], s["pjm"].n)
                            if th == 0 and sub == 0 and (b > 0 or h > 0):
                                pass
                            S_.i("v", lambda e, dst_ar=dst_ar, mw=mw, th=th,
                                 ppi=ppi, bias_=bias_, bcol=bcol, sub=sub,
                                 SD=SD: e.tensor_scalar(
                                dst_ar[0:mw, sub * SD + 512 * th:
                                       sub * SD + 512 * (th + 1)],
                                ps_p[ppi][0:mw, :], bias_[0:mw, bcol:bcol + 1],
                                None, ALU.add), [(s["pev"], 1)])
                for gb in range(SD // 512):
                    S_.w("te", s["pev"], s["pev"].n)
                    S_.w("te", s["idr"], 1)
                    S_.w("te", s["spte"], gT[0])
                    S_.w("te", s["vte"], max(0, nvt[0] - 1))
                    pb = ps_pt[gb % 2]
                    for j in range(4):
                        S_.i("te", lambda e, gb=gb, j=j, pb=pb: e.matmul(
                            pb[:, 128 * j:128 * (j + 1)],
                            kvh[:, SD + 512 * gb + 128 * j:SD + 512 * gb + 128 * (j + 1)],
                            idb[:], is_transpose=True,
                            start=(j == 0), stop=(j == 3)),
                            [(s["vtp"], 1)] if j == 3 else ())
                    S_.w("sc", s["vtp"], s["vtp"].n)
                    S_.w("sc", s["sav"], gT[0])
                    S_.i("sc", lambda e, gb=gb, pb=pb: e.copy(
                        vtm[:, 512 * gb:512 * (gb + 1)], pb[:]), [(s["vte"], 1)])
                    nvt[0] += 1
                ntile = NQT * NKT
                gt0 = gT[0]
                for u in range(ntile + 2):
                    t = gt0 + u
                    if u < ntile:
                        qt, kt = u // NKT, u % NKT
                        rb = ring[t % 4]
                        S_.w("te", s["ldk"], 16 * (b + 1))
                        S_.w("te", s["sexp"], max(0, t - 3))
                        if u < 4:
                            S_.w("te", s["pev"], s["pev"].n)
                        S_.i("te", lambda e, rb=rb, qt=qt, kt=kt, SD=SD: e.matmul(
                            rb[:], qah[:, 128 * qt:128 * (qt + 1)],
                            kvh[:, 512 * kt:512 * (kt + 1)],
                            start=True, stop=False))
                        S_.i("te", lambda e, rb=rb, qt=qt, kt=kt, SD=SD: e.matmul(
                            rb[:], qah[0:64, SD + 128 * qt:SD + 128 * (qt + 1)],
                            krop[0:64, 512 * kt:512 * (kt + 1)],
                            start=False, stop=True), [(s["scm"], 1)])
                        if kt == 0:
                            S_.edge("v", lambda e: e.memset(mrun[:], -1e30),
                                    s["vch"])
                            S_.edge("v", lambda e: e.memset(dnr[:], 0.0),
                                    s["vch"])
                        S_.w("v", s["scm"], t + 1)
                        S_.edge("v", lambda e, rb=rb: e.reduce_max(
                            psmax[:], rb[:], axis=AX.X), s["vch"])
                        S_.edge("v", lambda e: e.tensor_scalar(
                            mnew[:], psmax[:], SC1, None, ALU.mult), s["vch"])
                        S_.edge("v", lambda e: e.tensor_tensor(
                            mnew[:], mrun[:], mnew[:], ALU.max), s["vch"])
                        S_.edge("v", lambda e: e.tensor_tensor(
                            mdif[:], mrun[:], mnew[:], ALU.subtract), s["vch"])
                        S_.w("v", s["sexp"], t)
                        S_.edge("v", lambda e: e.tensor_scalar(
                            biast[:], mnew[:], -1.0, None, ALU.mult), s["vch"])
                        S_.i("v", lambda e: e.tensor_copy(mrun[:], mnew[:]),
                             [(s["sadd"], 1)])
                        cc = t % 4
                        S_.w("sc", s["sadd"], t + 1)
                        S_.w("sc", s["sres"], max(0, t - 3))
                        S_.i("sc", lambda e, cc=cc: e.activation(
                            corrs[:, cc:cc + 1], mdif[:], AF.Exp),
                            [(s["scorr"], 1)])
                        S_.w("sc", s["strp"], max(0, t - 1))
                        S_.w("sc", s["sdn"], t)
                        S_.i("sc", lambda e, rb=rb, t=t: e.activation(
                            psb[t % 2][:], rb[:], AF.Exp, bias=biast[:, 0:1],
                            scale=SC1, accum_out=tsum[:, 0:1]), [(s["sexp"], 1)])
                        S_.w("v", s["sexp"], t + 1)
                        S_.w("v", s["scorr"], t + 1)
                        S_.i("v", lambda e, cc=cc: e.scalar_tensor_tensor(
                            dnr[:], dnr[:], corrs[:, cc:cc + 1], tsum[:],
                            ALU.mult, ALU.add), [(s["sdn"], 1)])
                        if kt == NKT - 1:
                            S_.w("v", s["sdn"], t + 1)
                            S_.edge("v", lambda e, qc4=qt % 4: e.reciprocal(
                                rcp[:, qc4:qc4 + 1], dnr[:]), s["vch"])
                    if 1 <= u <= ntile:
                        tm = gt0 + u - 1
                        S_.w("te", s["sexp"], tm + 1)
                        S_.w("te", s["spte"], max(0, tm - 1))
                        S_.w("te", s["idr"], 1)
                        if u - 1 < 2:
                            S_.w("te", s["vte"], nvt[0])
                        pb = ps_pt[tm % 2]
                        for j in range(4):
                            S_.i("te", lambda e, j=j, tm=tm, pb=pb: e.matmul(
                                pb[:, 128 * j:128 * (j + 1)],
                                psb[tm % 2][:, 128 * j:128 * (j + 1)], idb[:],
                                is_transpose=True, start=(j == 0), stop=(j == 3)),
                                [(s["strp"], 1)] if j == 3 else ())
                        S_.w("sc", s["strp"], tm + 1)
                        S_.w("sc", s["sav"], max(0, tm - 1))
                        S_.i("sc", lambda e, tm=tm, pb=pb: e.copy(
                            ptsb[tm % 2][:], pb[:]), [(s["spte"], 1)])
                    if u >= 2:
                        ta = gt0 + u - 2
                        qta, kta = (u - 2) // NKT, (u - 2) % NKT
                        S_.w("te", s["spte"], ta + 1)
                        S_.w("te", s["sres"], max(0, ta - 1))
                        if u - 2 < 2:
                            S_.w("te", s["vte"], nvt[0])
                        for j in range(4):
                            S_.i("te", lambda e, ta=ta, kta=kta, j=j: e.matmul(
                                ps_o[ta % 2][:],
                                ptsb[ta % 2][:, 128 * j:128 * (j + 1)],
                                vtm[:, (4 * kta + j) * 128:(4 * kta + j + 1) * 128],
                                start=(j == 0), stop=(j == 3)),
                                [(s["sav"], 1)] if j == 3 else ())
                        S_.w("v", s["sav"], ta + 1)
                        cc2 = ta % 4
                        if kta == 0:
                            S_.w("v", s["osu"], s["osu"].n)
                            S_.i("v", lambda e, ta=ta: e.tensor_copy(
                                oacc[:], ps_o[ta % 2][:]), [(s["sres"], 1)])
                        else:
                            S_.i("v", lambda e, ta=ta, cc2=cc2:
                                 e.scalar_tensor_tensor(
                                oacc[:], oacc[:], corrs[:, cc2:cc2 + 1],
                                ps_o[ta % 2][:], ALU.mult, ALU.add),
                                [(s["sres"], 1)])
                        if kta == NKT - 1:
                            S_.w("v", s["sres"], ta + 1)
                            qc4 = qta % 4
                            if h == 0:
                                S_.i("v", lambda e, qta=qta, qc4=qc4:
                                     e.tensor_scalar(
                                    osum[:, 128 * qta:128 * (qta + 1)], oacc[:],
                                    rcp[:, qc4:qc4 + 1], None, ALU.mult),
                                    [(s["osu"], 1)])
                            else:
                                S_.i("v", lambda e, qta=qta, qc4=qc4:
                                     e.scalar_tensor_tensor(
                                    osum[:, 128 * qta:128 * (qta + 1)], oacc[:],
                                    rcp[:, qc4:qc4 + 1],
                                    osum[:, 128 * qta:128 * (qta + 1)],
                                    ALU.mult, ALU.add), [(s["osu"], 1)])
                gT[0] += ntile
            S_.w("sy", s["osu"], s["osu"].n)
            S_.i("sy", lambda e, b=b: e.dma_start(
                out=op[b].rearrange("(c p) v -> p c v", p=128),
                in_=osum[:, :].rearrange("p (c v) -> p c v", v=HD)),
                [(s["ost"], 16)])
        S_.w("sy", s["ost"], 16 * B)

        blk = ctx.enter_context(nc.Block())

        @blk.gpsimd
        def _(e):
            S_.replay("g", e)

        @blk.sync
        def _(e):
            S_.replay("sy", e)

        @blk.tensor
        def _(e):
            S_.replay("te", e)

        @blk.vector
        def _(e):
            S_.replay("v", e)

        @blk.scalar
        def _(e):
            S_.replay("sc", e)

    return nc


def stage_B(inputs, ares):
    slopes = _alibi_slopes()
    stall = np.concatenate([ares[j]["stats"] for j in range(NCORES)], axis=0)
    stsel = np.zeros((32, 4), np.float32)
    for j in range(NCORES):
        for si in range(4):
            stsel[4 * j + si, si] = 1.0
    bsel = np.zeros((4, 4 * 128), np.float32)
    for si in range(4):
        bsel[si, si * 128:(si + 1) * 128] = 1.0
    qw = inputs["L0_attn_q_ln_w"].astype(np.float32)
    qb = inputs["L0_attn_q_ln_b"].astype(np.float32)
    kw = inputs["L0_attn_k_ln_w"].astype(np.float32)
    kb = inputs["L0_attn_k_ln_b"].astype(np.float32)
    in_maps = []
    for c in range(NCORES):
        heads = _core_heads(c)
        lnwb = np.zeros((128, 16), np.float32)
        for hl, h in enumerate(heads):
            lnwb[:, 4 * hl + 0] = qw[128 * h:128 * (h + 1)]
            lnwb[:, 4 * hl + 1] = qb[128 * h:128 * (h + 1)]
            lnwb[:, 4 * hl + 2] = kw[128 * h:128 * (h + 1)]
            lnwb[:, 4 * hl + 3] = kb[128 * h:128 * (h + 1)]
        slb = np.broadcast_to(slopes[heads][None, :] / SC0, (128, 4)).copy()
        slc = np.broadcast_to(slopes[heads][None, :], (128, 4)).copy()
        in_maps.append({"qkvc": ares[c]["qkvc"], "stall": stall, "stsel": stsel,
                        "bsel": bsel, "lnwb": lnwb, "slb": slb, "slc": slc})
    return in_maps


def stage_C(inputs, ares, bres):
    o_full = np.empty((T, D), np.float32)
    for j in range(NCORES):
        for hl, h in enumerate(_core_heads(j)):
            for b in range(B):
                o_full[b * S:(b + 1) * S, h * HD:(h + 1) * HD] = \
                    bres[j]["oo"][b * HPC + hl]
    x = inputs["L0_attn_hidden_states"].reshape(T, D).astype(np.float32, copy=False)
    wot = np.ascontiguousarray(inputs["L0_attn_Wout"].T)
    kts = []
    for nm in ("L0_ffn_k0", "L0_ffn_k1", "L0_ffn_k2"):
        k = inputs[nm].view(np.uint8) if inputs[nm].dtype == np.bool_ else inputs[nm].astype(np.uint8)
        kts.append(np.ascontiguousarray(
            k.reshape(32, 128, 8, 512).transpose(2, 1, 0, 3)))
    s0b = np.broadcast_to(np.stack([
        np.float32(inputs["L0_ffn_s0"]), np.float32(inputs["L0_ffn_s1"]),
        np.float32(inputs["L0_ffn_s2"]), np.float32(0.0)])[None, :], (128, 4)).copy()
    eic = np.where(np.arange(D) >= NE, np.float32(EI_NEG),
                   np.float32(1.0)).reshape(32, 128).T.copy()
    wqct = np.ascontiguousarray(inputs["L1_attn_Wq_c"].T)
    bqc = inputs["L1_attn_bq_c"].astype(np.float32).reshape(12, 128).T.copy()
    wkvct = np.ascontiguousarray(inputs["L1_attn_Wkv_c"].T)
    bkvc = np.pad(inputs["L1_attn_bkv_c"].astype(np.float32),
                  (0, 64)).reshape(5, 128).T.copy()
    in_maps = []
    for c in range(NCORES):
        tsl = slice(TPC * c, TPC * (c + 1))
        in_maps.append({
            "ofm": np.ascontiguousarray(o_full[tsl].T),
            "xtm": np.ascontiguousarray(x[tsl]),
            "wot": wot, "k0t": kts[0], "k1t": kts[1], "k2t": kts[2],
            "s0b": s0b, "eic": eic, "wqct": wqct, "bqc": bqc,
            "wkvct": wkvct, "bkvc": bkvc,
        })
    return in_maps


def stage_D(inputs, cres):
    qcb = np.concatenate([cres[c]["qc"] for c in range(NCORES)], 1)
    kvcb = np.concatenate([cres[c]["kvc"] for c in range(NCORES)], 1)
    wqupT = inputs["L1_attn_Wq_up"].T.astype(np.float32, copy=False)
    bqu = inputs["L1_attn_bq_up"].astype(np.float32, copy=False)
    wkvupT = inputs["L1_attn_Wkv_up"].T.astype(np.float32, copy=False)
    bkvu = inputs["L1_attn_bkv_up"].astype(np.float32, copy=False)
    in_maps = []
    for c in range(NCORES):
        bqup = np.zeros((128, 8), np.float32)
        bkvup = np.zeros((128, 8), np.float32)
        for hh in range(HPC):
            gh = HPC * c + hh
            bqup[:, 2 * hh] = bqu[192 * gh:192 * gh + 128]
            bqup[0:64, 2 * hh + 1] = bqu[192 * gh + 128:192 * gh + 192]
            bkvup[:, 2 * hh] = bkvu[256 * gh:256 * gh + 128]
            bkvup[:, 2 * hh + 1] = bkvu[256 * gh + 128:256 * gh + 256]
        in_maps.append({
            "qcb": qcb, "kvcb": kvcb,
            "wqupt": np.ascontiguousarray(wqupT[:, 768 * c:768 * (c + 1)]),
            "bqup": bqup,
            "wkvupt": np.ascontiguousarray(wkvupT[:, 1024 * c:1024 * (c + 1)]),
            "bkvup": bkvup,
        })
    return in_maps


def stage_E(inputs, ares, cres, dres):
    dqg, dqu, dqd = [], [], []
    for j in range(NCORES):
        a, b = core_inter_range(j)
        w = b - a
        dqg.append(ares[j]["dqg"][:, :w])
        dqu.append(ares[j]["dqu"][:, :w])
        dqd.append(ares[j]["dqd"][:w, :])
    DQG = np.concatenate(dqg, 1)
    DQU = np.concatenate(dqu, 1)
    DQD = np.concatenate(dqd, 0)
    wgt = np.ascontiguousarray(
        DQG.reshape(32, 128, NG, 128).transpose(2, 1, 0, 3)).reshape(
        NG, 128, 32 * 128)
    wut = np.ascontiguousarray(
        DQU.reshape(32, 128, NG, 128).transpose(2, 1, 0, 3)).reshape(
        NG, 128, 32 * 128)
    wdt = np.ascontiguousarray(
        DQD.reshape(NG, 128, 8, 512).transpose(0, 2, 1, 3))
    wo = np.ascontiguousarray(inputs["L1_attn_Wo"].T.astype(np.float32,
                                                            copy=False))
    bo = inputs["L1_attn_bo"].astype(np.float32, copy=False).reshape(1, D)
    in_maps = []
    for c in range(NCORES):
        b_, q_ = c // 4, c % 4
        ops = np.ascontiguousarray(np.stack(
            [dres[j]["op"].reshape(B, S, HD)[b_, TPC * q_:TPC * (q_ + 1), :].T
             for j in range(NCORES)]))
        in_maps.append({
            "ops": ops, "x2tm_in": cres[c]["x2tm"], "wo": wo, "bo": bo,
            "wgt": wgt, "wut": wut, "wdt": wdt,
        })
    return in_maps


def run_A(inputs):
    nc = build_A()
    in_maps = stage_A(inputs)
    res = run_bass_kernel_spmd(nc, in_maps, core_ids=list(range(NCORES)))
    return res.results



def kernel(**inputs):
    cores = list(range(NCORES))
    resA = run_bass_kernel_spmd(build_A(), stage_A(inputs), core_ids=cores)
    ares = [resA.results[c] for c in cores]
    resB = run_bass_kernel_spmd(build_B(), stage_B(inputs, ares),
                                core_ids=cores)
    bres = [resB.results[c] for c in cores]
    resC = run_bass_kernel_spmd(build_C(), stage_C(inputs, ares, bres),
                                core_ids=cores)
    cres = [resC.results[c] for c in cores]
    resD = run_bass_kernel_spmd(build_D(), stage_D(inputs, cres),
                                core_ids=cores)
    dres = [resD.results[c] for c in cores]
    resE = run_bass_kernel_spmd(build_E(), stage_E(inputs, ares, cres, dres),
                                core_ids=cores)
    out = np.concatenate(
        [np.asarray(resE.results[c]["out"], dtype=np.float32).reshape(TPC, D)
         for c in cores], 0)
    return out.reshape(B, S, D)


# version 03
# speedup vs baseline: 1.3614x; 1.0495x over previous
import math
import contextlib
import numpy as np
import concourse.bass as bass
import concourse.mybir as mybir
from concourse.bass_utils import run_bass_kernel_spmd

F32 = mybir.dt.float32
BF16 = mybir.dt.bfloat16
I32 = mybir.dt.int32
U8 = mybir.dt.uint8
AF = mybir.ActivationFunctionType
ALU = mybir.AluOpType
AX = mybir.AxisListType

NCORES = 8
D = 4096
S = 4096
B = 2
T = B * S
NH = 32
HD = 128
HPC = NH // NCORES
TPC = T // NCORES
CLIP = 8.0
LN_EPS = 1e-5
SC0 = 1.0 / math.sqrt(HD)
NEGBIG = -1.0e9

QL, KVL, ROPE, NOPE, QH, VH = 1536, 512, 64, 128, 192, 128
SC1 = 1.0 / math.sqrt(QH)
INTER = 11008
GRP = 128
NG = INTER // GRP
NGC = 11
NPC = NGC * GRP
NE = 3277
EI_NEG = -4.0


def _alibi_slopes():
    m = np.arange(1, NH + 1, dtype=np.float32) * (8.0 / NH)
    return (1.0 / np.power(2.0, m)).astype(np.float32)


class Sem:
    def __init__(self, handle):
        self.h = handle
        self.n = 0

    def inc(self, instr, k=1):
        self.n += k
        instr.then_inc(self.h, k)


def make_sems(ctx, nc, names):
    return {n: Sem(ctx.enter_context(nc.semaphore(n))) for n in names}


def sb_alias(nc, backing, name, shape, dtype, off=0):
    h = nc.virtual_sbuf_tensor(name, shape, dtype)
    nc._tensor(h.name, h.shape, h.dtype, type="SB")
    nc.back_virtual_tensor_with(h, backing)
    mloc = nc.lookup_mloc(h)
    mloc.allocated = not nc.target_bir_lowering
    mloc.addr = nc.lookup_mloc(backing).addr + off
    return h


def edge(eng, sem, instr):
    sem.n += 1
    instr.then_inc(sem.h, 1)
    eng.wait_ge(sem.h, sem.n)


def _nchunks(total, w=512):
    out = []
    o = 0
    while o < total:
        cw = min(w, total - o)
        out.append((o, cw))
        o += cw
    return out



RACE_DETECT = [False]


def _bass():
    return bass.Bass("TRN2", target_bir_lowering=False, debug=False,
                     detect_race_conditions=RACE_DETECT[0])


def build_A(TC=16, MATS=("g", "u", "d"), KTO=None):
    nc = _bass()
    EC = 12
    TW = 512

    xfm = nc.dram_tensor("xfm", [D, T], F32, kind="ExternalInput").ap()
    wq = nc.dram_tensor("wq", [D, EC * 128], F32, kind="ExternalInput").ap()
    qwg = nc.dram_tensor("qwg", [D // 8, NPC], I32, kind="ExternalInput").ap()
    scg = nc.dram_tensor("scg", [32, NPC], F32, kind="ExternalInput").ap()
    qzg = nc.dram_tensor("qzg", [32, NPC // 8], I32, kind="ExternalInput").ap()
    gig = nc.dram_tensor("gig", [32, D], I32, kind="ExternalInput").ap()
    qwu = nc.dram_tensor("qwu", [D // 8, NPC], I32, kind="ExternalInput").ap()
    scu = nc.dram_tensor("scu", [32, NPC], F32, kind="ExternalInput").ap()
    qzu = nc.dram_tensor("qzu", [32, NPC // 8], I32, kind="ExternalInput").ap()
    qwd = nc.dram_tensor("qwd", [NPC // 8, D], I32, kind="ExternalInput").ap()
    scd = nc.dram_tensor("scd", [NGC, D], F32, kind="ExternalInput").ap()
    qzd = nc.dram_tensor("qzd", [NGC, D // 8], I32, kind="ExternalInput").ap()
    gid = nc.dram_tensor("gid", [NGC, NPC], I32, kind="ExternalInput").ap()
    gidv = nc.dram_tensor("gidv", [NGC, 1], F32, kind="ExternalInput").ap()
    shifts = nc.dram_tensor("shifts", [128, 1], I32, kind="ExternalInput").ap()

    qkvc = nc.dram_tensor("qkvc", [EC * 128, T], F32, kind="ExternalOutput").ap()
    stats = nc.dram_tensor("stats", [4, T], F32, kind="ExternalOutput").ap()
    dqg = nc.dram_tensor("dqg", [D, NPC], BF16, kind="ExternalOutput").ap()
    dqu = nc.dram_tensor("dqu", [D, NPC], BF16, kind="ExternalOutput").ap()
    dqd = nc.dram_tensor("dqd", [NPC, D], BF16, kind="ExternalOutput").ap()

    DQT = []
    KT = {"g": 32, "u": 32, "d": NGC}
    if KTO:
        KT.update(KTO)
    NCOLS = {"g": NPC, "u": NPC, "d": D}
    for mat in MATS:
        for kt in range(KT[mat]):
            for (c0, cw) in _nchunks(NCOLS[mat]):
                DQT.append((mat, kt, c0, cw))
    DQK = []
    KT_END = []
    cnt = 0
    for mat in MATS:
        nchk = len(_nchunks(NCOLS[mat]))
        for kt in range(KT[mat]):
            DQK.append((mat, kt))
            cnt += nchk
            KT_END.append(cnt)
    TILES_BEFORE = {}
    MATIDX = {}
    cnt = 0
    for mi, mat in enumerate(MATS):
        TILES_BEFORE[mat] = cnt
        MATIDX[mat] = mi
        cnt += KT[mat] * len(_nchunks(NCOLS[mat]))

    ctx = contextlib.ExitStack()
    with ctx:
        wsb = ctx.enter_context(nc.sbuf_tensor("wsb", [128, 32 * EC * 128], BF16))
        xsb = [ctx.enter_context(nc.sbuf_tensor(f"xsb{i}", [128, 16 * TW], BF16)) for i in range(3)]
        qsb = [ctx.enter_context(nc.sbuf_tensor(f"qsb{i}", [128, TW], F32)) for i in range(4)]
        sqsb = [ctx.enter_context(nc.sbuf_tensor(f"sqsb{i}", [128, TW], F32)) for i in range(2)]
        sel = ctx.enter_context(nc.sbuf_tensor("sel", [128, 16], F32))
        stsb = [ctx.enter_context(nc.sbuf_tensor(f"stsb{i}", [4, TW], F32)) for i in range(2)]
        qbc = [ctx.enter_context(nc.sbuf_tensor(f"qbc{i}", [128, 512], I32)) for i in range(2)]
        w4i = ctx.enter_context(nc.sbuf_tensor("w4i", [128, 512], I32))
        w4f = ctx.enter_context(nc.sbuf_tensor("w4f", [128, 512], F32))
        wt1 = ctx.enter_context(nc.sbuf_tensor("wt1", [128, 512], F32))
        wdq = [ctx.enter_context(nc.sbuf_tensor(f"wdq{i}", [128, 512], BF16)) for i in range(2)]
        shsb = ctx.enter_context(nc.sbuf_tensor("shsb", [128, 1], I32))
        scsb = ctx.enter_context(nc.sbuf_tensor("scsb", [32, D], BF16))
        zssb = ctx.enter_context(nc.sbuf_tensor("zssb", [32, D], BF16))
        qzsb = ctx.enter_context(nc.sbuf_tensor("qzsb", [32, 512], I32))
        z4sb = ctx.enter_context(nc.sbuf_tensor("z4sb", [32, 512], I32))
        gisb = [ctx.enter_context(nc.sbuf_tensor(f"gisb{i}", [32, 128], I32)) for i in range(2)]
        gtf = [ctx.enter_context(nc.sbuf_tensor(f"gtf{i}", [32, 128], BF16)) for i in range(2)]
        iog = ctx.enter_context(nc.sbuf_tensor("iog", [32, 1], F32))
        givsb = ctx.enter_context(nc.sbuf_tensor("givsb", [NGC, 1], F32))

        ps_qkv = [ctx.enter_context(nc.psum_tensor(f"psq{i}", [128, TW], F32)) for i in range(2)]
        ps_st = ctx.enter_context(nc.psum_tensor("psst", [4, TW], F32))
        ps_s = [ctx.enter_context(nc.psum_tensor(f"pss{i}", [128, 512], F32)) for i in range(2)]
        ps_z = [ctx.enter_context(nc.psum_tensor(f"psz{i}", [128, 512], F32)) for i in range(2)]

        s = make_sems(ctx, nc, [
            "ldw", "mmq", "evq", "sq", "smm", "evst",
            "mlds", "mldz", "zsp", "dgt",
            "dmm", "dvw", "dst", "ini", "dlds", "vch",
            "ldx0", "ldx1", "ldx2", "dld0", "dld1", "gld0", "gld1",
            "stq0", "stq1", "stq2", "stq3", "sts0", "sts1", "std0", "std1"])
        LDX = [s["ldx0"], s["ldx1"], s["ldx2"]]
        DLD = [s["dld0"], s["dld1"]]
        GLD = [s["gld0"], s["gld1"]]
        STQ = [s["stq0"], s["stq1"], s["stq2"], s["stq3"]]
        STS = [s["sts0"], s["sts1"]]
        STD = [s["std0"], s["std1"]]

        LDXC = []
        DLDC = []
        GLDC = []
        STQC = []
        STSC = []
        STDC = []

        blk = ctx.enter_context(nc.Block())

        @blk.gpsimd
        def _(g):
            s["ini"].inc(g.iota(iog[:], pattern=[[1, 1]], base=0,
                                channel_multiplier=1,
                                allow_small_or_imprecise_dtypes=True))
            for i in range(4):
                src = wq.rearrange("(c p) e -> p c e", p=128)[:, 8 * i:8 * (i + 1), :]
                s["ldw"].inc(g.dma_start(
                    out=wsb[:, (8 * i) * EC * 128:(8 * i + 8) * EC * 128],
                    in_=src), 16)
            s["mlds"].inc(g.dma_start(out=scsb[0:32, 0:NPC], in_=scg[:, :]), 16)
            for j in range(2 * TC):
                t, h = j // 2, j % 2
                if j >= 3:
                    g.wait_ge(s["mmq"].h, EC * ((j - 3) // 2 + 1))
                src = xfm.rearrange("(c p) t -> p c t", p=128)[
                    :, 16 * h:16 * (h + 1), t * TW:(t + 1) * TW]
                LDX[j % 3].inc(g.dma_start(out=xsb[j % 3][:], in_=src), 16)
                LDXC.append(LDX[j % 3].n)
            for di, (mat, kt, c0, cw) in enumerate(DQT):
                if kt == 0 and c0 == 0 and MATIDX[mat] > 0:
                    nb = TILES_BEFORE[mat]
                    g.wait_ge(s["dmm"].h, 2 * nb)
                    g.wait_ge(s["zsp"].h, MATIDX[mat])
                    ng = 32 if mat != "d" else NGC
                    sc_src = {"u": scu, "d": scd}[mat]
                    s["mlds"].inc(g.dma_start(out=scsb[0:ng, 0:NCOLS[mat]],
                                              in_=sc_src[:, :]), 16)
                qw = {"g": qwg, "u": qwu, "d": qwd}[mat]
                if di >= 2:
                    g.wait_ge(s["dvw"].h, di - 1)
                src = qw[16 * kt:16 * (kt + 1), c0:c0 + cw]
                src = src.rearrange("m (o n) -> m o n", o=1).broadcast_to([16, 8, cw])
                DLD[di % 2].inc(g.dma_start(out=qbc[di % 2][:, 0:cw], in_=src), 16)
                DLDC.append(DLD[di % 2].n)

        @blk.sync
        def _(sy):
            qz_src = {"g": qzg, "u": qzu, "d": qzd}
            sy.dma_start(out=shsb[:], in_=shifts[:, :]).then_inc(s["dlds"].h, 16)
            sy.dma_start(out=givsb[:], in_=gidv[:, :]).then_inc(s["dlds"].h, 16)
            s["dlds"].n += 32
            m0 = MATS[0]
            ng0 = 32 if m0 != "d" else NGC
            s["mldz"].inc(sy.dma_start(out=qzsb[0:ng0, 0:NCOLS[m0] // 8],
                                       in_=qz_src[m0][:, :]), 16)
            for t in range(TC):
                for e in range(EC):
                    i0 = t * EC + e
                    sy.wait_ge(s["evq"].h, i0 + 1)
                    STQ[i0 % 4].inc(
                        sy.dma_start(out=qkvc[e * 128:(e + 1) * 128, t * TW:(t + 1) * TW],
                                     in_=qsb[i0 % 4][:]), 16)
                    STQC.append(STQ[i0 % 4].n)
                sy.wait_ge(s["evst"].h, t + 1)
                STS[t % 2].inc(
                    sy.dma_start(out=stats[:, t * TW:(t + 1) * TW],
                                 in_=stsb[t % 2][:]), 16)
                STSC.append(STS[t % 2].n)
            nkt0 = 0
            for di, (mat, kt, c0, cw) in enumerate(DQT):
                if kt == 0 and c0 == 0 and MATIDX[mat] > 0:
                    sy.wait_ge(s["zsp"].h, MATIDX[mat])
                    ng = 32 if mat != "d" else NGC
                    s["mldz"].inc(sy.dma_start(out=qzsb[0:ng, 0:NCOLS[mat] // 8],
                                               in_=qz_src[mat][:, :]), 16)
                if c0 == 0:
                    if nkt0 >= 2:
                        sy.wait_ge(s["dgt"].h, nkt0 - 1)
                    if mat != "d":
                        src = gig[0:32, kt * 128:(kt + 1) * 128]
                        dst = gisb[nkt0 % 2][0:32, :]
                    else:
                        src = gid[0:NGC, kt * 128:(kt + 1) * 128]
                        dst = gisb[nkt0 % 2][0:NGC, :]
                    GLD[nkt0 % 2].inc(sy.dma_start(out=dst, in_=src), 16)
                    GLDC.append(GLD[nkt0 % 2].n)
                    nkt0 += 1
                dq = {"g": dqg, "u": dqu, "d": dqd}[mat]
                sy.wait_ge(s["dst"].h, di + 1)
                STD[di % 2].inc(
                    sy.dma_start(out=dq[128 * kt:128 * (kt + 1), c0:c0 + cw],
                                 in_=wdq[di % 2][:, 0:cw]), 16)
                STDC.append(STD[di % 2].n)
            for ss in (STQ + STS + STD):
                sy.wait_ge(ss.h, ss.n)

        @blk.scalar
        def _(sc):
            for t in range(TC):
                for e in range(8):
                    i0 = t * EC + e
                    sc.wait_ge(s["evq"].h, i0 + 1)
                    if t * 8 + e >= 2:
                        sc.wait_ge(s["smm"].h, t * 8 + e - 1)
                    s["sq"].inc(sc.activation(sqsb[e % 2][:], qsb[i0 % 4][:], AF.Square))

        @blk.vector
        def _(v):
            edge(v, s["vch"], v.memset(sel[:], 0.0))
            for j in range(4):
                v.memset(sel[:, 4 * j + j:4 * j + j + 1], 1.0)
            edge(v, s["vch"], v.engine_nop())
            for t in range(TC):
                for e in range(EC):
                    i0 = t * EC + e
                    v.wait_ge(s["mmq"].h, i0 + 1)
                    if i0 >= 4:
                        v.wait_ge(STQ[i0 % 4].h, STQC[i0 - 4])
                        tprev = (i0 - 4) // EC
                        eprev = (i0 - 4) % EC
                        if eprev < 8:
                            v.wait_ge(s["smm"].h, 8 * tprev + eprev + 1)
                    s["evq"].inc(v.tensor_scalar(qsb[i0 % 4][:], ps_qkv[e % 2][:],
                                                 -CLIP, CLIP, ALU.max, ALU.min))
                v.wait_ge(s["smm"].h, 8 * (t + 1))
                if t >= 2:
                    v.wait_ge(STS[t % 2].h, STSC[t - 2])
                s["evst"].inc(v.tensor_copy(stsb[t % 2][:], ps_st[:]))
            nkt = 0
            for di, (mat, kt, c0, cw) in enumerate(DQT):
                ng = 32 if mat != "d" else NGC
                ncols = NCOLS[mat]
                if c0 == 0 and kt == 0:
                    mi = MATIDX[mat]
                    v.wait_ge(s["mlds"].h, 16 * (mi + 1))
                    v.wait_ge(s["mldz"].h, 16 * (mi + 1))
                    for (zc0, zcw) in _nchunks(ncols):
                        for r in range(8):
                            ii = v.tensor_scalar(
                                z4sb[0:ng, 0:zcw].rearrange("g (n o) -> g n o", o=8)[:, :, r],
                                qzsb[0:ng, zc0 // 8:(zc0 + zcw) // 8], 4 * r, 15,
                                ALU.logical_shift_right, ALU.bitwise_and)
                        edge(v, s["vch"], ii)
                        edge(v, s["vch"], v.tensor_copy(zssb[0:ng, zc0:zc0 + zcw],
                                                        z4sb[0:ng, 0:zcw]))
                        edge(v, s["vch"], v.tensor_scalar(zssb[0:ng, zc0:zc0 + zcw],
                                        zssb[0:ng, zc0:zc0 + zcw], 1.0, None, ALU.add))
                        last = v.tensor_tensor(zssb[0:ng, zc0:zc0 + zcw],
                                               zssb[0:ng, zc0:zc0 + zcw],
                                               scsb[0:ng, zc0:zc0 + zcw], ALU.mult)
                    s["zsp"].inc(last)
                if c0 == 0:
                    nkt += 1
                    ki = nkt - 1
                    if nkt >= 3:
                        v.wait_ge(s["dmm"].h, 2 * KT_END[nkt - 3])
                    v.wait_ge(GLD[ki % 2].h, GLDC[ki])
                    if mat != "d":
                        v.wait_ge(s["ini"].h, 1)
                        s["dgt"].inc(v.tensor_scalar(gtf[ki % 2][0:32, :],
                                                     gisb[ki % 2][0:32, :],
                                                     iog[0:32, 0:1], None, ALU.is_equal))
                    else:
                        s["dgt"].inc(v.tensor_scalar(gtf[ki % 2][0:NGC, :],
                                                     gisb[ki % 2][0:NGC, :],
                                                     givsb[0:NGC, 0:1], None, ALU.is_equal))
                if di == 0:
                    v.wait_ge(s["dlds"].h, 32)
                v.wait_ge(DLD[di % 2].h, DLDC[di])
                ii = v.tensor_tensor(w4i[:, 0:cw], qbc[di % 2][:, 0:cw],
                                     shsb[:, 0:1].broadcast_to([128, cw]),
                                     ALU.logical_shift_right)
                s["dvw"].inc(ii)
                v.wait_ge(s["dvw"].h, s["dvw"].n)
                edge(v, s["vch"], v.tensor_scalar(w4i[:, 0:cw], w4i[:, 0:cw],
                                                  15, None, ALU.bitwise_and))
                edge(v, s["vch"], v.tensor_copy(w4f[:, 0:cw], w4i[:, 0:cw]))
                v.wait_ge(s["dmm"].h, 2 * (di + 1))
                if di >= 2:
                    v.wait_ge(STD[di % 2].h, STDC[di - 2])
                edge(v, s["vch"], v.tensor_tensor(wt1[:, 0:cw], w4f[:, 0:cw],
                                                  ps_s[di % 2][:, 0:cw], ALU.mult))
                s["dst"].inc(v.tensor_tensor(wdq[di % 2][:, 0:cw], wt1[:, 0:cw],
                                             ps_z[di % 2][:, 0:cw], ALU.subtract))

        @blk.tensor
        def _(te):
            te.wait_ge(s["ldw"].h, 64)
            for t in range(TC):
                for e in range(EC):
                    i0 = t * EC + e
                    if i0 >= 2:
                        te.wait_ge(s["evq"].h, i0 - 1)
                    for d in range(32):
                        if d == 0:
                            te.wait_ge(LDX[(2 * t) % 3].h, LDXC[2 * t])
                        if d == 16:
                            te.wait_ge(LDX[(2 * t + 1) % 3].h, LDXC[2 * t + 1])
                        xbuf = xsb[(2 * t + d // 16) % 3]
                        mm = te.matmul(ps_qkv[e % 2][:],
                                       wsb[:, (d * EC + e) * 128:(d * EC + e + 1) * 128],
                                       xbuf[:, (d % 16) * TW:(d % 16 + 1) * TW],
                                       start=(d == 0), stop=(d == 31))
                        if d == 31:
                            s["mmq"].inc(mm)
                    if 1 <= e <= 8:
                        es = e - 1
                        if es == 0 and t >= 1:
                            te.wait_ge(s["evst"].h, t)
                        te.wait_ge(s["evq"].h, t * EC + es + 1)
                        selr = sel[:, 0:4] if es < 4 else sel[:, 8:12]
                        te.matmul(ps_st[:], selr, qsb[(t * EC + es) % 4][:],
                                  start=(es == 0), stop=False)
                        te.wait_ge(s["sq"].h, t * 8 + es + 1)
                        sels = sel[:, 4:8] if es < 4 else sel[:, 12:16]
                        mm = te.matmul(ps_st[:], sels, sqsb[es % 2][:],
                                       start=False, stop=(es == 7))
                        s["smm"].inc(mm)
            nkt = 0
            for di, (mat, kt, c0, cw) in enumerate(DQT):
                ng = 32 if mat != "d" else NGC
                if c0 == 0:
                    nkt += 1
                ki = nkt - 1
                te.wait_ge(s["dgt"].h, nkt)
                te.wait_ge(s["zsp"].h, MATIDX[mat] + 1)
                if di >= 2:
                    te.wait_ge(s["dst"].h, di - 1)
                te.matmul(ps_s[di % 2][:, 0:cw], gtf[ki % 2][0:ng, :],
                          scsb[0:ng, c0:c0 + cw], start=True, stop=True)
                mm = te.matmul(ps_z[di % 2][:, 0:cw], gtf[ki % 2][0:ng, :],
                               zssb[0:ng, c0:c0 + cw], start=True, stop=True)
                s["dmm"].inc(mm, 2)

    return nc



class Sched:
    ENGS = ("g", "sy", "te", "v", "sc")

    def __init__(self):
        self.ops = {e: [] for e in self.ENGS}

    def w(self, eng, sem, val):
        if val > 0:
            self.ops[eng].append(lambda e, h=sem.h, v=val: e.wait_ge(h, v))

    def i(self, eng, fn, incs=()):
        vals = []
        for sem, k in incs:
            sem.n += k
            vals.append((sem.h, k))

        def run(e, fn=fn, vals=vals):
            ins = fn(e)
            for h, k in vals:
                ins.then_inc(h, k)
        self.ops[eng].append(run)
        return tuple(sem.n for sem, _ in incs)

    def edge(self, eng, fn, sem):
        self.i(eng, fn, [(sem, 1)])
        self.w(eng, sem, sem.n)

    def replay(self, eng, e):
        for f in self.ops[eng]:
            f(e)



def build_B(SD=S):
    nc = _bass()
    TB = B * SD
    NQT = SD // 128
    TCB = TB // 512

    qkvc = nc.dram_tensor("qkvc", [1536, TB], F32, kind="ExternalInput").ap()
    stall = nc.dram_tensor("stall", [32, TB], F32, kind="ExternalInput").ap()
    stsel = nc.dram_tensor("stsel", [32, 4], F32, kind="ExternalInput").ap()
    bsel = nc.dram_tensor("bsel", [4, 4 * 128], F32, kind="ExternalInput").ap()
    lnwb = nc.dram_tensor("lnwb", [128, 16], F32, kind="ExternalInput").ap()
    slb = nc.dram_tensor("slb", [128, 4], F32, kind="ExternalInput").ap()
    slc = nc.dram_tensor("slc", [128, 4], F32, kind="ExternalInput").ap()
    oo = nc.dram_tensor("oo", [8, SD, HD], F32, kind="ExternalOutput").ap()

    ctx = contextlib.ExitStack()
    with ctx:
        sb = lambda *a: ctx.enter_context(nc.sbuf_tensor(*a))
        pt = lambda *a: ctx.enter_context(nc.psum_tensor(*a))
        stall_sb = [sb(f"stalls{i}", [32, 512], F32) for i in range(2)]
        st4 = sb("st4", [4, 512], F32)
        lnv = sb("lnv", [128, 16], F32)
        lnt = sb("lnt", [128, 16], F32)
        mut = sb("mut", [128, 8], F32)
        vat = sb("vat", [128, 8], F32)
        lnrow = sb("lnrow", [4, TB], F32)
        bselsb = sb("bselsb", [4, 4 * 128], F32)
        stselsb = sb("stselsb", [32, 4], F32)
        bc = [sb(f"bc{i}", [128, SD], F32) for i in range(4)]
        qf = sb("qf", [128, SD], F32)
        tmp1 = sb("tmp1", [128, SD], F32)
        qln = sb("qln", [128, SD], BF16)
        kln = sb("kln", [128, SD], BF16)
        vbf = sb("vbf", [128, SD], BF16)
        vtm = sb("vtm", [128, SD], BF16)
        lnwsb = sb("lnwsb", [128, 16], F32)
        slbsb = sb("slbsb", [128, 4], F32)
        slcsb = sb("slcsb", [128, 4], F32)
        tio = sb("tio", [128, 512], F32)
        mk4 = sb("mk4", [128, 4 * 512], F32)
        ah = sb("ah", [128, 512], F32)
        dv4 = sb("dv4", [128, 4 * 512], F32)
        idf = sb("idf", [128, 128], F32)
        idb = sb("idb", [128, 128], BF16)
        psb = [sb(f"psb{i}", [128, 512], BF16) for i in range(2)]
        ptsb = [sb(f"ptsb{i}", [128, 512], BF16) for i in range(2)]
        osb = sb("osb", [128, HD], F32)
        osacc = sb("osacc", [128, HD], F32)
        mrun = sb("mrun", [128, 1], F32)
        mnew = sb("mnew", [128, 1], F32)
        mtile = sb("mtile", [128, 1], F32)
        psmax = sb("psmax", [128, 1], F32)
        offt = sb("offt", [128, 1], F32)
        biast = sb("biast", [128, 1], F32)
        corrs = sb("corrsB", [128, 4], F32)
        mdif = sb("mdif", [128, 1], F32)
        tsum = sb("tsum", [128, 1], F32)
        dnr = sb("dnr", [128, 1], F32)
        rcp = sb("rcp", [128, 4], F32)
        epst = sb("epst", [128, 1], F32)

        ps_s = [pt(f"pscs{i}", [128, 512], F32) for i in range(2)]
        ps_pt = [pt(f"pspt{i}", [128, 512], BF16) for i in range(2)]
        ps_o = [pt(f"pso{i}", [128, HD], F32) for i in range(2)]
        ps_bc = pt("psbc", [128, 512], F32)
        ps_x = pt("psxB", [128, 512], F32)
        ring = [ps_s[0], ps_s[1], ps_bc, ps_x]

        s = make_sems(ctx, nc, [
            "lds", "ld0", "ld1", "ldq", "ldv", "summ", "sue", "tt1", "lnve", "lnm",
            "ttb", "lnre", "bcm", "bce", "lnap", "vgp", "scm", "sadd",
            "smax", "sexp", "scorr", "sdn", "strp", "spte", "sres", "sav",
            "soev", "ost", "vtp", "vte", "vch", "sch", "gch", "idr"])

        S_ = Sched()

        for name, dst, src in (("stsel", stselsb, stsel), ("bsel", bselsb, bsel),
                               ("lnwb", lnwsb, lnwb), ("slb", slbsb, slb),
                               ("slc", slcsb, slc)):
            S_.i("sy", lambda e, d=dst, so=src: e.dma_start(out=d[:], in_=so[:, :]),
                 [(s["lds"], 16)])
        S_.i("g", lambda e: e.iota(idf[:], pattern=[[1, 128]], base=0,
                                   channel_multiplier=-1,
                                   allow_small_or_imprecise_dtypes=True),
             [(s["gch"], 1)])
        S_.i("g", lambda e: e.iota(tio[:], pattern=[[1, 512]], base=0,
                                   channel_multiplier=-1,
                                   allow_small_or_imprecise_dtypes=True),
             [(s["gch"], 1)])
        S_.w("v", s["gch"], 2)
        for vv in range(4):
            S_.edge("v", lambda e, vv=vv: e.tensor_scalar(
                mk4[:, 512 * vv:512 * (vv + 1)], tio[:], float(-128 * vv), None,
                ALU.add), s["vch"])
            S_.edge("v", lambda e, vv=vv: e.tensor_scalar(
                mk4[:, 512 * vv:512 * (vv + 1)], mk4[:, 512 * vv:512 * (vv + 1)],
                0.0, NEGBIG, ALU.is_gt, ALU.mult), s["vch"])
        S_.edge("v", lambda e: e.memset(epst[:], LN_EPS), s["vch"])
        S_.edge("v", lambda e: e.tensor_scalar(idf[:], idf[:], 0.0, None,
                                               ALU.is_equal), s["vch"])
        S_.i("v", lambda e: e.tensor_copy(idb[:], idf[:]), [(s["idr"], 1)])

        for t in range(TCB):
            S_.w("sy", s["summ"], t - 1)
            S_.i("sy", lambda e, t=t: e.dma_start(
                out=stall_sb[t % 2][:], in_=stall[:, 512 * t:512 * (t + 1)]),
                [(s["ld%d" % (t % 2)], 16)])
            S_.w("te", s["lds"], 80)
            S_.w("te", s["ld%d" % (t % 2)], s["ld%d" % (t % 2)].n)
            S_.w("te", s["sue"], t)
            S_.i("te", lambda e, t=t: e.matmul(ps_x[0:4, :], stselsb[:, :],
                                               stall_sb[t % 2][:],
                                               start=True, stop=True),
                 [(s["summ"], 1)])
            S_.w("v", s["summ"], t + 1)
            S_.w("v", s["tt1"], t)
            S_.i("v", lambda e: e.tensor_copy(st4[:], ps_x[0:4, :]), [(s["sue"], 1)])
            S_.w("te", s["sue"], t + 1)
            S_.w("te", s["idr"], 1)
            S_.w("te", s["lnve"], t)
            for j in range(4):
                S_.i("te", lambda e, j=j: e.matmul(
                    ps_s[0][:, 4 * j:4 * (j + 1)], st4[:, 128 * j:128 * (j + 1)],
                    idf[0:4, 0:4], is_transpose=True,
                    start=(j == 0), stop=(j == 3)),
                    [(s["tt1"], 1)] if j == 3 else ())
            S_.w("v", s["tt1"], t + 1)
            S_.w("v", s["ttb"], t)
            S_.i("v", lambda e: e.tensor_copy(lnv[:], ps_s[0][:, 0:16]), [(s["lnve"], 1)])
            S_.w("v", s["lnve"], s["lnve"].n)
            S_.edge("v", lambda e: e.tensor_scalar(mut[:], lnv[:, :].rearrange(
                "p (j s) -> p j s", s=2)[:, :, 0], 1.0 / D, None, ALU.mult), s["vch"])
            S_.edge("v", lambda e: e.tensor_scalar(vat[:], lnv[:, :].rearrange(
                "p (j s) -> p j s", s=2)[:, :, 1], 1.0 / D, None, ALU.mult), s["vch"])
            S_.edge("v", lambda e: e.tensor_tensor(
                lnv[:, 0:8], mut[:], mut[:], ALU.mult), s["vch"])
            S_.i("v", lambda e: e.tensor_tensor(
                vat[:], vat[:], lnv[:, 0:8], ALU.subtract), [(s["lnm"], 1)])
            S_.w("sc", s["lnm"], s["lnm"].n)
            S_.edge("sc", lambda e: e.activation(vat[:], vat[:], AF.Ln,
                                                 bias=epst[:, 0:1]), s["sch"])
            S_.i("sc", lambda e: e.activation(
                lnt[:, :].rearrange("p (j s) -> p j s", s=2)[:, :, 0], vat[:],
                AF.Exp, scale=-0.5), [(s["lnm"], 1)])
            S_.w("v", s["lnm"], s["lnm"].n)
            S_.i("v", lambda e: e.tensor_tensor(
                lnt[:, :].rearrange("p (j s) -> p j s", s=2)[:, :, 1], mut[:],
                lnt[:, :].rearrange("p (j s) -> p j s", s=2)[:, :, 0],
                ALU.mult), [(s["lnm"], 1)])
            S_.w("te", s["lnm"], s["lnm"].n)
            S_.w("te", s["lnre"], t)
            for j in range(4):
                S_.i("te", lambda e, j=j: e.matmul(
                    ps_bc[0:4, 128 * j:128 * (j + 1)], lnt[:, 4 * j:4 * (j + 1)],
                    idf[:, :], is_transpose=True, start=(j == 0), stop=(j == 3)),
                    [(s["ttb"], 1)] if j == 3 else ())
            S_.w("v", s["ttb"], t + 1)
            S_.i("v", lambda e, t=t: e.tensor_copy(lnrow[:, 512 * t:512 * (t + 1)],
                                                   ps_bc[0:4, :]), [(s["lnre"], 1)])

        nlnap = 0
        gT = [0]
        nvt = [0]
        nst = [0]
        for b in range(B):
            S_.w("te", s["sexp"], gT[0])
            S_.w("te", s["lnre"], TB // 512)
            for tch in range(SD // 512):
                gt = (SD * b + 512 * tch) // 512
                for s4 in range(4):
                    S_.w("te", s["lnre"], gt + 1)
                    S_.w("te", s["bce"], s["bcm"].n)
                    if b > 0:
                        S_.w("te", s["lnap"], nlnap)
                    S_.i("te", lambda e, gt=gt, s4=s4: e.matmul(
                        ps_bc[:], bselsb[:, 128 * s4:128 * (s4 + 1)],
                        lnrow[:, 512 * gt:512 * (gt + 1)], start=True, stop=True),
                        [(s["bcm"], 1)])
                    S_.w("v", s["bcm"], s["bcm"].n)
                    S_.i("v", lambda e, tch=tch, s4=s4: e.tensor_copy(
                        bc[s4][:, 512 * tch:512 * (tch + 1)], ps_bc[:]),
                        [(s["bce"], 1)])
            for h in range(HPC):
                bh = b * HPC + h
                for which in range(2):
                    r0 = 512 * which + 128 * h
                    S_.w("sy", s["lnap"], nlnap)
                    S_.w("sy", s["bce"], s["bce"].n)
                    S_.i("sy", lambda e, r0=r0, b=b: e.dma_start(
                        out=qf[:], in_=qkvc[r0:r0 + 128, SD * b:SD * (b + 1)]),
                        [(s["ldq"], 16)])
                    S_.w("v", s["ldq"], s["ldq"].n)
                    rb, mb = (bc[0], bc[1]) if which == 0 else (bc[2], bc[3])
                    S_.edge("v", lambda e, rb=rb: e.tensor_tensor(
                        tmp1[:], qf[:], rb[:], ALU.mult), s["vch"])
                    S_.i("v", lambda e, mb=mb: e.tensor_tensor(
                        tmp1[:], tmp1[:], mb[:], ALU.subtract), [(s["lnap"], 1)])
                    nlnap += 1
                    dst = qln if which == 0 else kln
                    wc = 4 * h + 2 * which
                    S_.w("v", s["lnap"], nlnap)
                    S_.w("v", s["scm"], s["scm"].n)
                    S_.i("v", lambda e, dst=dst, wc=wc: e.tensor_scalar(
                        dst[:], tmp1[:], lnwsb[:, wc:wc + 1],
                        lnwsb[:, wc + 1:wc + 2], ALU.mult, ALU.add),
                        [(s["lnap"], 1)])
                    nlnap += 1
                S_.w("g", s["vtp"], s["vtp"].n)
                S_.i("g", lambda e, h=h, b=b: e.dma_start(
                    out=vbf[:], in_=qkvc[1024 + 128 * h:1024 + 128 * (h + 1),
                                         SD * b:SD * (b + 1)]), [(s["ldv"], 16)])
                for gb in range(SD // 512):
                    S_.w("te", s["ldv"], s["ldv"].n)
                    S_.w("te", s["idr"], 1)
                    S_.w("te", s["spte"], gT[0])
                    S_.w("te", s["vte"], max(0, nvt[0] - 1))
                    pb = ps_pt[gb % 2]
                    for j in range(4):
                        S_.i("te", lambda e, gb=gb, j=j, pb=pb: e.matmul(
                            pb[:, 128 * j:128 * (j + 1)],
                            vbf[:, 512 * gb + 128 * j:512 * gb + 128 * (j + 1)],
                            idb[:], is_transpose=True,
                            start=(j == 0), stop=(j == 3)),
                            [(s["vtp"], 1)] if j == 3 else ())
                    S_.w("sc", s["vtp"], s["vtp"].n)
                    S_.w("sc", s["sav"], gT[0])
                    S_.i("sc", lambda e, gb=gb, pb=pb: e.copy(
                        vtm[:, 512 * gb:512 * (gb + 1)], pb[:]), [(s["vte"], 1)])
                    nvt[0] += 1
                S_.edge("v", lambda e, h=h: e.tensor_scalar(
                    ah[:], tio[:], slbsb[:, h:h + 1], None, ALU.mult), s["vch"])
                for vv in range(4):
                    ii = lambda e, vv=vv: e.tensor_tensor(
                        dv4[:, 512 * vv:512 * (vv + 1)], ah[:],
                        mk4[:, 512 * vv:512 * (vv + 1)], ALU.add)
                    if vv == 3:
                        S_.edge("v", ii, s["vch"])
                    else:
                        S_.i("v", ii)
                S_.i("v", lambda e: e.engine_nop(), [(s["lnap"], 1)])
                nlnap += 1
                tlist = [(qt, kt) for qt in range(NQT)
                         for kt in range(qt // 4 + 1)]
                ntile = len(tlist)
                gt0 = gT[0]
                for u in range(ntile + 2):
                    t = gt0 + u
                    if u < ntile:
                        qt, kt = tlist[u]
                        diag = (kt == qt // 4)
                        rb = ring[t % 4]
                        S_.w("te", s["sexp"], max(0, t - 3))
                        if u == 0:
                            S_.w("te", s["lnap"], nlnap)
                        if u < 4:
                            S_.w("te", s["bce"], s["bce"].n)
                        S_.i("te", lambda e, rb=rb, qt=qt, kt=kt: e.matmul(
                            rb[:], qln[:, 128 * qt:128 * (qt + 1)],
                            kln[:, 512 * kt:512 * (kt + 1)], start=True,
                            stop=True), [(s["scm"], 1)])
                        if kt == 0:
                            S_.edge("v", lambda e: e.memset(mrun[:], -1e30),
                                    s["vch"])
                            S_.edge("v", lambda e: e.memset(dnr[:], 0.0),
                                    s["vch"])
                        S_.w("v", s["scm"], t + 1)
                        if diag:
                            src = dv4[:, 512 * (qt % 4):512 * (qt % 4 + 1)]
                        else:
                            src = ah
                        S_.edge("v", lambda e, rb=rb, src=src: e.tensor_tensor(
                            rb[:], rb[:], src[:] if src is ah else src,
                            ALU.add), s["vch"])
                        S_.edge("v", lambda e, rb=rb: e.reduce_max(
                            psmax[:], rb[:], axis=AX.X), s["vch"])
                        cval = float(512 * kt - 128 * qt)
                        S_.edge("v", lambda e, h=h, cval=cval: e.tensor_scalar(
                            offt[:], slcsb[:, h:h + 1], cval, None, ALU.mult),
                            s["vch"])
                        S_.edge("v", lambda e: e.scalar_tensor_tensor(
                            mtile[:], psmax[:], SC0, offt[:], ALU.mult,
                            ALU.add), s["vch"])
                        S_.edge("v", lambda e: e.tensor_tensor(
                            mnew[:], mrun[:], mtile[:], ALU.max), s["vch"])
                        S_.edge("v", lambda e: e.tensor_tensor(
                            mdif[:], mrun[:], mnew[:], ALU.subtract), s["vch"])
                        S_.w("v", s["sexp"], t)
                        S_.edge("v", lambda e: e.tensor_tensor(
                            biast[:], offt[:], mnew[:], ALU.subtract), s["vch"])
                        S_.i("v", lambda e: e.tensor_copy(mrun[:], mnew[:]),
                             [(s["sadd"], 1)])
                        cc = t % 4
                        S_.w("sc", s["sadd"], t + 1)
                        S_.w("sc", s["sres"], max(0, t - 3))
                        S_.i("sc", lambda e, cc=cc: e.activation(
                            corrs[:, cc:cc + 1], mdif[:], AF.Exp),
                            [(s["scorr"], 1)])
                        S_.w("sc", s["strp"], max(0, t - 1))
                        S_.w("sc", s["sdn"], t)
                        S_.i("sc", lambda e, rb=rb, t=t: e.activation(
                            psb[t % 2][:], rb[:], AF.Exp, bias=biast[:, 0:1],
                            scale=SC0, accum_out=tsum[:, 0:1]),
                            [(s["sexp"], 1)])
                        S_.w("v", s["sexp"], t + 1)
                        S_.w("v", s["scorr"], t + 1)
                        S_.i("v", lambda e, cc=cc: e.scalar_tensor_tensor(
                            dnr[:], dnr[:], corrs[:, cc:cc + 1], tsum[:],
                            ALU.mult, ALU.add), [(s["sdn"], 1)])
                        if diag:
                            S_.w("v", s["sdn"], t + 1)
                            S_.edge("v", lambda e, qc4=qt % 4: e.reciprocal(
                                rcp[:, qc4:qc4 + 1], dnr[:]), s["vch"])
                    if 1 <= u <= ntile:
                        tm = gt0 + u - 1
                        S_.w("te", s["sexp"], tm + 1)
                        S_.w("te", s["spte"], max(0, tm - 1))
                        S_.w("te", s["idr"], 1)
                        if u - 1 < 2:
                            S_.w("te", s["vte"], nvt[0])
                        pb = ps_pt[tm % 2]
                        for j in range(4):
                            S_.i("te", lambda e, j=j, tm=tm, pb=pb: e.matmul(
                                pb[:, 128 * j:128 * (j + 1)],
                                psb[tm % 2][:, 128 * j:128 * (j + 1)], idb[:],
                                is_transpose=True, start=(j == 0),
                                stop=(j == 3)),
                                [(s["strp"], 1)] if j == 3 else ())
                        S_.w("sc", s["strp"], tm + 1)
                        S_.w("sc", s["sav"], max(0, tm - 1))
                        S_.i("sc", lambda e, tm=tm, pb=pb: e.copy(
                            ptsb[tm % 2][:], pb[:]), [(s["spte"], 1)])
                    if u >= 2:
                        ta = gt0 + u - 2
                        qta, kta = tlist[u - 2]
                        S_.w("te", s["spte"], ta + 1)
                        S_.w("te", s["sres"], max(0, ta - 1))
                        if u - 2 < 2:
                            S_.w("te", s["vte"], nvt[0])
                        for j in range(4):
                            S_.i("te", lambda e, ta=ta, kta=kta, j=j: e.matmul(
                                ps_o[ta % 2][:],
                                ptsb[ta % 2][:, 128 * j:128 * (j + 1)],
                                vtm[:, (4 * kta + j) * 128:(4 * kta + j + 1) * 128],
                                start=(j == 0), stop=(j == 3)),
                                [(s["sav"], 1)] if j == 3 else ())
                        S_.w("v", s["sav"], ta + 1)
                        cc2 = ta % 4
                        if kta == 0:
                            S_.w("v", s["soev"], s["soev"].n)
                            S_.i("v", lambda e, ta=ta: e.tensor_copy(
                                osacc[:], ps_o[ta % 2][:]), [(s["sres"], 1)])
                        else:
                            S_.i("v", lambda e, ta=ta, cc2=cc2:
                                 e.scalar_tensor_tensor(
                                osacc[:], osacc[:], corrs[:, cc2:cc2 + 1],
                                ps_o[ta % 2][:], ALU.mult, ALU.add),
                                [(s["sres"], 1)])
                        if kta == qta // 4:
                            S_.w("v", s["sres"], ta + 1)
                            S_.w("v", s["ost"], 16 * nst[0])
                            S_.i("v", lambda e, qc4=qta % 4: e.tensor_scalar(
                                osb[:], osacc[:], rcp[:, qc4:qc4 + 1], None,
                                ALU.mult), [(s["soev"], 1)])
                            S_.w("sy", s["soev"], s["soev"].n)
                            S_.i("sy", lambda e, bh=bh, qta=qta: e.dma_start(
                                out=oo[bh, 128 * qta:128 * (qta + 1), :],
                                in_=osb[:]), [(s["ost"], 16)])
                            nst[0] += 1
                gT[0] += ntile
        S_.w("sy", s["ost"], s["ost"].n)

        blk = ctx.enter_context(nc.Block())

        @blk.gpsimd
        def _(e):
            S_.replay("g", e)

        @blk.sync
        def _(e):
            S_.replay("sy", e)

        @blk.tensor
        def _(e):
            S_.replay("te", e)

        @blk.vector
        def _(e):
            S_.replay("v", e)

        @blk.scalar
        def _(e):
            S_.replay("sc", e)

    return nc



def build_E(TPE=TPC, NIC=NG, NIH=4):
    nc = _bass()
    NTT = TPE // 128
    NTH2 = TPE // 512
    base, rem = NIC // NIH, NIC % NIH
    lens = [base + (1 if i < rem else 0) for i in range(NIH)]
    ihr, st = [], 0
    for L in lens:
        ihr.append((st, st + L))
        st += L
    HMAX = max(lens)

    ops = nc.dram_tensor("ops", [NCORES, 128, TPE], F32, kind="ExternalInput").ap()
    x2tm = nc.dram_tensor("x2tm_in", [TPE, D], BF16, kind="ExternalInput").ap()
    wo = nc.dram_tensor("wo", [128, D], F32, kind="ExternalInput").ap()
    bo = nc.dram_tensor("bo", [1, D], F32, kind="ExternalInput").ap()
    wgt = nc.dram_tensor("wgt", [NIC, 128, 32 * 128], BF16, kind="ExternalInput").ap()
    wut = nc.dram_tensor("wut", [NIC, 128, 32 * 128], BF16, kind="ExternalInput").ap()
    wdt = nc.dram_tensor("wdt", [NIC, 8, 128, 512], BF16, kind="ExternalInput").ap()
    out = nc.dram_tensor("out", [TPE, D], F32, kind="ExternalOutput").ap()
    x3d = nc.dram_tensor("x3d", [NTT, 128, D], BF16).ap()
    ypart = nc.dram_tensor("ypart", [8, NTT, 128, 512], F32).ap()

    ctx = contextlib.ExitStack()
    with ctx:
        sb = lambda *a: ctx.enter_context(nc.sbuf_tensor(*a))
        pt = lambda *a: ctx.enter_context(nc.psum_tensor(*a))
        opb = [sb(f"opbE{i}", [128, 512], F32) for i in range(2)]
        osf = sb("osfE", [128, TPE], F32)
        osb = sb("osbE", [128, TPE], BF16)
        wotb = sb("wotbE", [128, D], BF16)
        bob = sb("bobE", [128, D], BF16)
        x2sl = sb("x2slE", [128, D], BF16)
        rsl = sb("rslE", [128, D], BF16)
        ybuf = sb("ybufE", [128, D // 2], F32)
        x3fm = sb("x3fmE", [128, 32 * TPE], BF16)
        hfm = sb("hfmE", [128, HMAX * TPE], BF16)
        wgsl = sb("wgslE", [128, 32 * 128], BF16)
        wusl = sb("wuslE", [128, 32 * 128], BF16)
        wgslB = sb_alias(nc, wotb, "wgslEB", [128, 32 * 128], BF16)
        wuslB = sb_alias(nc, x2sl, "wuslEB", [128, 32 * 128], BF16)
        wdsl = [sb(f"wdslE{i}", [128, 512], BF16) for i in range(4)]
        ysl = [sb(f"yslE{i}", [128, 512], F32) for i in range(2)]
        ypv = [sb(f"ypvE{i}", [128, 512], F32) for i in range(2)]
        idf = sb("idfE", [128, 128], F32)
        epst = sb("epstE", [128, 1], F32)
        ssum = sb("ssumE", [128, 1], F32)
        sqs = sb("sqsE", [128, 1], F32)
        mu = sb("muE", [128, 1], F32)
        vv = sb("vvE", [128, 1], F32)
        m2 = sb("m2E", [128, 1], F32)
        rr = sb("rrE", [128, 1], F32)

        ps8 = [pt(f"psE{i}", [128, 512], F32) for i in range(8)]

        s = make_sems(ctx, nc, [
            "lop0", "lop1", "lwb", "lx2", "lwg", "lwgb", "lwd0", "lwd1",
            "lwd2", "lwd3", "lyp0", "lyp1", "lyg", "aop", "acast", "amm",
            "aev", "sqd", "sqr",
            "lnm", "x3p", "x3w", "trp", "tre", "f1m", "f1s", "f1h", "f2t",
            "f2e", "ypw0", "ypw1", "outw", "vch", "sch", "gch", "idr"])
        S_ = Sched()

        S_.i("g", lambda e: e.iota(idf[:], pattern=[[1, 128]], base=0,
                                   channel_multiplier=-1,
                                   allow_small_or_imprecise_dtypes=True),
             [(s["gch"], 1)])
        S_.i("g", lambda e: e.dma_start(out=wotb[:], in_=wo[:, :]),
             [(s["lwb"], 16)])
        S_.i("g", lambda e: e.dma_start(
            out=bob[:], in_=bo[0:1, :].broadcast_to([128, D])),
            [(s["lwb"], 16)])
        S_.w("v", s["gch"], 1)
        S_.edge("v", lambda e: e.tensor_scalar(idf[:], idf[:], 0.0, None,
                                               ALU.is_equal), s["vch"])
        S_.i("v", lambda e: e.memset(epst[:], LN_EPS), [(s["idr"], 1)])

        nop = 0
        for j in range(NCORES):
            for hh in range(TPE // 512):
                sl = nop % 2
                lsem = s["lop%d" % sl]
                if nop >= 2:
                    S_.w("sy", s["aop"], nop - 1)
                S_.i("sy", lambda e, j=j, hh=hh, sl=sl: e.dma_start(
                    out=opb[sl][:], in_=ops[j, :, 512 * hh:512 * (hh + 1)]),
                    [(lsem, 16)])
                nop += 1
                S_.w("v", lsem, 16 * ((nop + 1) // 2))
                S_.w("v", s["aop"], s["aop"].n)
                dsl = slice(512 * hh, 512 * (hh + 1))
                if j == 0:
                    S_.i("v", lambda e, sl=sl, dsl=dsl: e.tensor_copy(
                        osf[:, dsl], opb[sl][:]), [(s["aop"], 1)])
                else:
                    S_.i("v", lambda e, sl=sl, dsl=dsl: e.tensor_tensor(
                        osf[:, dsl], osf[:, dsl], opb[sl][:], ALU.add),
                        [(s["aop"], 1)])
        S_.w("v", s["aop"], s["aop"].n)
        S_.i("v", lambda e: e.tensor_copy(osb[:], osf[:]), [(s["acast"], 1)])

        nx2 = 0
        for tt in range(NTT):
            S_.w("sy", s["sqr"], max(0, tt))
            S_.i("sy", lambda e, tt=tt: e.dma_start(
                out=x2sl[:], in_=x2tm[128 * tt:128 * (tt + 1), :]),
                [(s["lx2"], 16)])
            nx2 += 1
            S_.w("v", s["x3w"], 16 * tt)
            for dc in range(8):
                idx = 8 * tt + dc
                bank = (4, 5, 7)[idx % 3]
                S_.w("te", s["lwb"], 32)
                S_.w("te", s["acast"], 1)
                S_.w("te", s["aev"], max(0, idx - 2))
                S_.i("te", lambda e, tt=tt, dc=dc, bank=bank: e.matmul(
                    ps8[bank][:], osb[:, 128 * tt:128 * (tt + 1)],
                    wotb[:, 512 * dc:512 * (dc + 1)], start=True, stop=True),
                    [(s["amm"], 1)])
                S_.w("v", s["amm"], s["amm"].n)
                S_.w("v", s["lx2"], 16 * nx2)
                dsl = slice(512 * dc, 512 * (dc + 1))
                S_.edge("v", lambda e, bank=bank, dsl=dsl: e.tensor_tensor(
                    rsl[:, dsl], ps8[bank][:], x2sl[:, dsl], ALU.add), s["vch"])
                S_.i("v", lambda e, dsl=dsl: e.tensor_tensor(
                    rsl[:, dsl], rsl[:, dsl], bob[:, dsl], ALU.add),
                    [(s["aev"], 1)])
            S_.w("v", s["aev"], s["aev"].n)
            S_.edge("v", lambda e: e.reduce_sum(ssum[:], rsl[:], axis=AX.X),
                    s["vch"])
            S_.w("sc", s["aev"], s["aev"].n)
            S_.i("sc", lambda e: e.activation(x2sl[:], rsl[:], AF.Square),
                 [(s["sqd"], 1)])
            S_.w("v", s["sqd"], s["sqd"].n)
            S_.i("v", lambda e: e.reduce_sum(sqs[:], x2sl[:], axis=AX.X),
                 [(s["sqr"], 1)])
            S_.edge("v", lambda e: e.tensor_scalar(mu[:], ssum[:], 1.0 / D,
                                                   None, ALU.mult), s["vch"])
            S_.edge("v", lambda e: e.tensor_scalar(vv[:], sqs[:], 1.0 / D,
                                                   None, ALU.mult), s["vch"])
            S_.edge("v", lambda e: e.tensor_tensor(m2[:], mu[:], mu[:],
                                                   ALU.mult), s["vch"])
            S_.i("v", lambda e: e.tensor_tensor(vv[:], vv[:], m2[:],
                                                ALU.subtract), [(s["lnm"], 1)])
            S_.w("sc", s["lnm"], s["lnm"].n)
            S_.edge("sc", lambda e: e.activation(vv[:], vv[:], AF.Ln,
                                                 bias=epst[:, 0:1]), s["sch"])
            S_.i("sc", lambda e: e.activation(rr[:], vv[:], AF.Exp,
                                              scale=-0.5), [(s["lnm"], 1)])
            S_.w("v", s["lnm"], s["lnm"].n)
            for dg in range(8):
                dsl = slice(512 * dg, 512 * (dg + 1))
                S_.w("v", s["trp"], s["trp"].n)
                S_.edge("v", lambda e, dsl=dsl: e.tensor_scalar(
                    ysl[0][:], rsl[:, dsl], mu[:, 0:1], rr[:, 0:1],
                    ALU.subtract, ALU.mult), s["vch"])
                S_.i("v", lambda e, dsl=dsl: e.tensor_scalar(
                    rsl[:, dsl], rsl[:, dsl], mu[:, 0:1], rr[:, 0:1],
                    ALU.subtract, ALU.mult), [(s["x3p"], 1)])
                S_.w("te", s["x3p"], s["x3p"].n)
                S_.w("te", s["idr"], 1)
                S_.w("te", s["tre"], s["tre"].n)
                for jj in range(4):
                    S_.i("te", lambda e, jj=jj: e.matmul(
                        ps8[6][:, 128 * jj:128 * (jj + 1)],
                        ysl[0][:, 128 * jj:128 * (jj + 1)], idf[:],
                        is_transpose=True, start=(jj == 0), stop=(jj == 3)),
                        [(s["trp"], 1)] if jj == 3 else ())
                S_.w("sc", s["trp"], s["trp"].n)
                S_.i("sc", lambda e, dg=dg, tt=tt, TPE=TPE: e.copy(
                    x3fm[:, :].rearrange("p (c t) -> p c t", t=TPE)[
                        :, 4 * dg:4 * (dg + 1), 128 * tt:128 * (tt + 1)],
                    ps8[6][:, :].rearrange("p (c x) -> p c x", x=128)),
                    [(s["tre"], 1)])
            S_.w("sy", s["x3p"], s["x3p"].n)
            S_.i("sy", lambda e, tt=tt: e.dma_start(
                out=x3d[tt], in_=rsl[:]), [(s["x3w"], 16)])

        nwg = 0
        nwgb = 0
        nwd = [0, 0, 0, 0]
        nyr = [0, 0]
        nyw = [0, 0]
        gpi = 2 * NTH2
        for p, (i0_, i1_) in enumerate(ihr):
            L = i1_ - i0_
            for jl, ich in enumerate(range(i0_, i1_)):
                par = jl % 2
                wg_, wu_ = (wgsl, wusl) if par == 0 else (wgslB, wuslB)
                lsem = s["lwg"] if par == 0 else s["lwgb"]
                S_.w("sy", s["f1m"], max(0, s["f1m"].n - gpi))
                if p == 0 and jl == 1:
                    S_.w("sy", s["amm"], 8 * NTT)
                    S_.w("sy", s["sqr"], NTT)
                S_.i("sy", lambda e, ich=ich, wg_=wg_: e.dma_start(
                    out=wg_[:], in_=wgt[ich]), [(lsem, 16)])
                S_.i("sy", lambda e, ich=ich, wu_=wu_: e.dma_start(
                    out=wu_[:], in_=wut[ich]), [(lsem, 16)])
                if par == 0:
                    nwg += 2
                else:
                    nwgb += 2
                for proj in range(2):
                    wsl = wg_ if proj == 0 else wu_
                    for th2 in range(NTH2):
                        bank = proj * NTH2 + th2 + 4 * (jl % 2)
                        if par == 0:
                            S_.w("te", s["lwg"], 16 * nwg)
                        else:
                            S_.w("te", s["lwgb"], 16 * nwgb)
                        if jl < 2 and proj == 0 and th2 == 0:
                            S_.w("te", s["tre"], s["tre"].n)
                            S_.w("te", s["f2e"], s["f2e"].n)
                            S_.w("te", s["aev"], s["aev"].n)
                        S_.w("te", s["f1h"], max(0, s["f1h"].n - NTH2))
                        for dch in range(32):
                            S_.i("te", lambda e, wsl=wsl, dch=dch, th2=th2,
                                 bank=bank, TPE=TPE: e.matmul(
                                ps8[bank][:],
                                wsl[:, 128 * dch:128 * (dch + 1)],
                                x3fm[:, dch * TPE + 512 * th2:
                                     dch * TPE + 512 * (th2 + 1)],
                                start=(dch == 0), stop=(dch == 31)),
                                [(s["f1m"], 1)] if dch == 31 else ())
                for th2 in range(NTH2):
                    su = NTH2 * jl + th2
                    slq = su % 2
                    S_.w("sc", s["f1m"], s["f1m"].n - 2 * NTH2 + th2 + 1)
                    S_.w("sc", s["f1h"], s["f1h"].n)
                    if jl == 0 and th2 == 0:
                        S_.w("sc", s["trp"], s["trp"].n)
                        S_.w("sc", s["ypw0"], 16 * nyw[0])
                        S_.w("sc", s["ypw1"], 16 * nyw[1])
                    gb_ = th2 + 4 * (jl % 2)
                    S_.i("sc", lambda e, gb_=gb_, slq=slq: e.activation(
                        ysl[slq][:], ps8[gb_][:], AF.Silu), [(s["f1s"], 1)])
                    S_.w("v", s["f1s"], s["f1s"].n)
                    S_.w("v", s["f1m"], s["f1m"].n)
                    if p > 0 and jl == 0 and th2 == 0:
                        S_.w("v", s["f2t"], s["f2t"].n)
                    ub_ = NTH2 + th2 + 4 * (jl % 2)
                    S_.i("v", lambda e, jl=jl, th2=th2, slq=slq, ub_=ub_,
                         TPE=TPE: e.tensor_tensor(
                        hfm[:, jl * TPE + 512 * th2:jl * TPE + 512 * (th2 + 1)],
                        ysl[slq][:], ps8[ub_][:], ALU.mult),
                        [(s["f1h"], 1)])
            if p > 0:
                S_.w("g", s["ypw0"], 16 * nyw[0])
                S_.w("g", s["ypw1"], 16 * nyw[1])
            for dch in range(8):
                e0 = s["f2e"].n
                if dch == 0:
                    S_.w("te", s["f1h"], s["f1h"].n)
                for jl in range(L):
                    li = dch * L + jl
                    slot = li % 4
                    S_.w("g", s["f2t"], max(0, s["f2t"].n - 3))
                    S_.i("g", lambda e, ich=i0_ + jl, dch=dch, slot=slot:
                         e.dma_start(out=wdsl[slot][:], in_=wdt[ich, dch]),
                         [(s["lwd%d" % slot], 16)])
                    nwd[slot] += 1
                    S_.w("te", s["lwd%d" % slot], 16 * nwd[slot])
                    for tt in range(NTT):
                        if jl == 0:
                            S_.w("te", s["f2e"], max(0, e0 - NTT + tt + 1))
                        S_.i("te", lambda e, jl=jl, tt=tt, slot=slot, L=L,
                             TPE=TPE: e.matmul(
                            ps8[tt][:],
                            hfm[:, jl * TPE + 128 * tt:jl * TPE + 128 * (tt + 1)],
                            wdsl[slot][:], start=(jl == 0), stop=(jl == L - 1)),
                            [(s["f2t"], 1)] if tt == NTT - 1 else ())
                for tt in range(NTT):
                    sl2 = tt % 2
                    if p > 0:
                        slr = (dch * NTT + tt) % 2
                        S_.w("g", s["f2e"], max(0, s["f2e"].n - 1))
                        S_.i("g", lambda e, dch=dch, tt=tt, slr=slr:
                             e.dma_start(out=ypv[slr][:], in_=ypart[dch, tt]),
                             [(s["lyp%d" % slr], 16)])
                        nyr[slr] += 1
                    S_.w("v", s["f2t"], s["f2t"].n)
                    S_.w("v", s["ypw%d" % sl2], 16 * nyw[sl2])
                    if p == 0:
                        S_.i("v", lambda e, tt=tt, sl2=sl2: e.tensor_copy(
                            ysl[sl2][:], ps8[tt][:]), [(s["f2e"], 1)])
                    else:
                        S_.w("v", s["lyp%d" % slr], 16 * nyr[slr])
                        S_.i("v", lambda e, tt=tt, sl2=sl2, slr=slr:
                             e.tensor_tensor(ysl[sl2][:], ps8[tt][:],
                                             ypv[slr][:], ALU.add),
                             [(s["f2e"], 1)])
                    S_.w("sy", s["f2e"], s["f2e"].n)
                    S_.i("sy", lambda e, dch=dch, tt=tt, sl2=sl2: e.dma_start(
                        out=ypart[dch, tt], in_=ysl[sl2][:]),
                        [(s["ypw%d" % sl2], 16)])
                    nyw[sl2] += 1

        nyg = 0
        nout = 0
        for tt in range(NTT):
            S_.w("sy", s["f1m"], s["f1m"].n)
            S_.w("sy", s["sqr"], NTT + tt)
            S_.w("sy", s["x3w"], 16 * NTT)
            S_.i("sy", lambda e, tt=tt: e.dma_start(
                out=x2sl[:], in_=x3d[tt]), [(s["lx2"], 16)])
            nx2 += 1
            for hh in range(2):
                S_.w("sy", s["ypw0"], 16 * nyw[0])
                S_.w("sy", s["ypw1"], 16 * nyw[1])
                S_.w("sy", s["outw"], 16 * nout)
                S_.i("sy", lambda e, tt=tt, hh=hh: e.dma_start(
                    out=ybuf[:, :].rearrange("p (c x) -> p c x", x=512),
                    in_=ypart[4 * hh:4 * (hh + 1), tt].rearrange(
                        "d p x -> p d x")), [(s["lyg"], 16)])
                nyg += 1
                S_.w("v", s["lyg"], 16 * nyg)
                S_.w("v", s["lx2"], 16 * nx2)
                dsl = slice(2048 * hh, 2048 * (hh + 1))
                S_.i("v", lambda e, dsl=dsl: e.tensor_tensor(
                    rsl[:, dsl], ybuf[:], x2sl[:, dsl], ALU.add),
                    [(s["aev"], 1)])
                S_.w("sy", s["aev"], s["aev"].n)
            S_.w("v", s["aev"], s["aev"].n)
            S_.edge("v", lambda e: e.reduce_sum(ssum[:], rsl[:], axis=AX.X),
                    s["vch"])
            S_.w("sc", s["aev"], s["aev"].n)
            S_.i("sc", lambda e: e.activation(x2sl[:], rsl[:], AF.Square),
                 [(s["sqd"], 1)])
            S_.w("v", s["sqd"], s["sqd"].n)
            S_.i("v", lambda e: e.reduce_sum(sqs[:], x2sl[:], axis=AX.X),
                 [(s["sqr"], 1)])
            S_.edge("v", lambda e: e.tensor_scalar(mu[:], ssum[:], 1.0 / D,
                                                   None, ALU.mult), s["vch"])
            S_.edge("v", lambda e: e.tensor_scalar(vv[:], sqs[:], 1.0 / D,
                                                   None, ALU.mult), s["vch"])
            S_.edge("v", lambda e: e.tensor_tensor(m2[:], mu[:], mu[:],
                                                   ALU.mult), s["vch"])
            S_.i("v", lambda e: e.tensor_tensor(vv[:], vv[:], m2[:],
                                                ALU.subtract), [(s["lnm"], 1)])
            S_.w("sc", s["lnm"], s["lnm"].n)
            S_.edge("sc", lambda e: e.activation(vv[:], vv[:], AF.Ln,
                                                 bias=epst[:, 0:1]), s["sch"])
            S_.i("sc", lambda e: e.activation(rr[:], vv[:], AF.Exp,
                                              scale=-0.5), [(s["lnm"], 1)])
            S_.w("v", s["lnm"], s["lnm"].n)
            for hh in range(2):
                dsl = slice(2048 * hh, 2048 * (hh + 1))
                S_.w("v", s["outw"], 16 * nout)
                S_.i("v", lambda e, dsl=dsl: e.tensor_scalar(
                    ybuf[:], rsl[:, dsl], mu[:, 0:1], rr[:, 0:1],
                    ALU.subtract, ALU.mult), [(s["x3p"], 1)])
                S_.w("sy", s["x3p"], s["x3p"].n)
                S_.i("sy", lambda e, tt=tt, hh=hh: e.dma_start(
                    out=out[128 * tt:128 * (tt + 1),
                            2048 * hh:2048 * (hh + 1)], in_=ybuf[:]),
                    [(s["outw"], 16)])
                nout += 1
        S_.w("sy", s["outw"], 16 * nout)

        blk = ctx.enter_context(nc.Block())

        @blk.gpsimd
        def _(e):
            S_.replay("g", e)

        @blk.sync
        def _(e):
            S_.replay("sy", e)

        @blk.tensor
        def _(e):
            S_.replay("te", e)

        @blk.vector
        def _(e):
            S_.replay("v", e)

        @blk.scalar
        def _(e):
            S_.replay("sc", e)

    return nc



def _core_heads(c):
    return list(range(4 * c, 4 * c + 4))


def _qkv_cols(c):
    hs = _core_heads(c)
    cols = []
    for part in range(3):
        for h in hs:
            cols.extend(range(part * D + h * HD, part * D + (h + 1) * HD))
    return np.array(cols)


def _pad_cols(a, total, axis):
    pad = total - a.shape[axis]
    if pad == 0:
        return a
    widths = [(0, 0)] * a.ndim
    widths[axis] = (0, pad)
    return np.pad(a, widths, mode="edge")


def stage_A(inputs):
    x = np.ascontiguousarray(
        inputs["L0_attn_hidden_states"].reshape(T, D).T)
    wqkv_t = np.ascontiguousarray(inputs["L0_attn_Wqkv"].T)
    shifts = np.tile(np.arange(8, dtype=np.int32) * 4, 16).reshape(128, 1)
    in_maps = []
    gqw = inputs["L1_ffn_gate_qweight"]
    gsc = inputs["L1_ffn_gate_scales"]
    gqz = inputs["L1_ffn_gate_qzeros"]
    ggi = inputs["L1_ffn_gate_g_idx"]
    uqw = inputs["L1_ffn_up_qweight"]
    usc = inputs["L1_ffn_up_scales"]
    uqz = inputs["L1_ffn_up_qzeros"]
    ugi = inputs["L1_ffn_up_g_idx"]
    dqw = inputs["L1_ffn_down_qweight"]
    dsc = inputs["L1_ffn_down_scales"]
    dqz = inputs["L1_ffn_down_qzeros"]
    dgi = inputs["L1_ffn_down_g_idx"]
    for c in range(NCORES):
        cols = _qkv_cols(c)
        wq = np.ascontiguousarray(wqkv_t[:, cols])
        n0, n1 = core_inter_range(c)
        m = {
            "xfm": x, "wq": wq, "shifts": shifts,
            "qwg": _pad_cols(np.ascontiguousarray(gqw[:, n0:n1]), NPC, 1),
            "scg": _pad_cols(np.ascontiguousarray(gsc[:, n0:n1]), NPC, 1),
            "qzg": _pad_cols(np.ascontiguousarray(gqz[:, n0 // 8:n1 // 8]), NPC // 8, 1),
            "gig": np.ascontiguousarray(np.broadcast_to(ggi[None, :], (32, D))),
            "qwu": _pad_cols(np.ascontiguousarray(uqw[:, n0:n1]), NPC, 1),
            "scu": _pad_cols(np.ascontiguousarray(usc[:, n0:n1]), NPC, 1),
            "qzu": _pad_cols(np.ascontiguousarray(uqz[:, n0 // 8:n1 // 8]), NPC // 8, 1),
            "qwd": _pad_rows(np.ascontiguousarray(dqw[n0 // 8:n1 // 8, :]), NPC // 8),
            "scd": _pad_rows_groups(dsc, c),
            "qzd": _pad_rows_groups(dqz, c),
            "gid": _pad_cols(np.ascontiguousarray(dgi[None, n0:n1]), NPC, 1).repeat(NGC, 0).copy(),
            "gidv": group_ids_down(c),
        }
        in_maps.append(m)
    return in_maps


def core_inter_range(c):
    gstarts = []
    g0 = 0
    for i in range(NCORES):
        ng = 11 if i < 6 else 10
        gstarts.append((g0, g0 + ng))
        g0 += ng
    a, b = gstarts[c]
    return a * GRP, b * GRP


def group_ids_down(c):
    n0, _ = core_inter_range(c)
    return np.arange(n0 // GRP, n0 // GRP + NGC, dtype=np.float32).reshape(NGC, 1)


def _pad_rows(a, total):
    if a.shape[0] >= total:
        return a
    return np.pad(a, ((0, total - a.shape[0]), (0, 0)), mode="edge")


def _pad_rows_groups(a, c):
    n0, n1 = core_inter_range(c)
    g0, g1 = n0 // GRP, n1 // GRP
    sl = np.ascontiguousarray(a[g0:g1])
    if sl.shape[0] < NGC:
        sl = np.pad(sl, ((0, NGC - sl.shape[0]), (0, 0)), mode="edge")
    return sl



def build_C(TP=TPC, DBG=False, PH=5):
    nc = _bass()
    NT = TP // 128

    ofm = nc.dram_tensor("ofm", [D, TP], F32, kind="ExternalInput").ap()
    xtm = nc.dram_tensor("xtm", [TP, D], F32, kind="ExternalInput").ap()
    wot = nc.dram_tensor("wot", [D, D], F32, kind="ExternalInput").ap()
    kts = [nc.dram_tensor(f"k{i}t", [8, 128, 32, 512], U8, kind="ExternalInput").ap()
           for i in range(3)]
    s0b = nc.dram_tensor("s0b", [128, 4], F32, kind="ExternalInput").ap()
    eic = nc.dram_tensor("eic", [128, 32], F32, kind="ExternalInput").ap()
    wqct = nc.dram_tensor("wqct", [D, QL], F32, kind="ExternalInput").ap()
    bqc = nc.dram_tensor("bqc", [128, 12], F32, kind="ExternalInput").ap()
    wkvct = nc.dram_tensor("wkvct", [D, 576], F32, kind="ExternalInput").ap()
    bkvc = nc.dram_tensor("bkvc", [128, 5], F32, kind="ExternalInput").ap()

    x2tm = nc.dram_tensor("x2tm", [TP, D], BF16, kind="ExternalOutput").ap()
    qc = nc.dram_tensor("qc", [QL, TP], BF16, kind="ExternalOutput").ap()
    kvc = nc.dram_tensor("kvc", [576, TP], BF16, kind="ExternalOutput").ap()
    att = nc.dram_tensor("attb", [TP, D], BF16).ap()
    dbgs = [nc.dram_tensor(f"dbg{i}", [128, 32 * TP], BF16,
                           kind="ExternalOutput").ap() for i in range(2)] if DBG else None
    dbga = nc.dram_tensor("dbga", [128, 32 * TP], BF16,
                          kind="ExternalOutput").ap() if DBG else None
    dbgb = nc.dram_tensor("dbgb", [128, 32 * TP], F32,
                          kind="ExternalOutput").ap() if DBG else None
    dbgr = nc.dram_tensor("dbgr", [1, TP], F32,
                          kind="ExternalOutput").ap() if DBG else None
    dbgw = nc.dram_tensor("dbgw", [128, 32 * 512], BF16,
                          kind="ExternalOutput").ap() if DBG else None
    dbgsx = nc.dram_tensor("dbgsx", [128, 4], F32,
                           kind="ExternalOutput").ap() if DBG else None
    z2d = nc.dram_tensor("z2d", [TP, D], BF16).ap()
    x1d = nc.dram_tensor("x1d", [TP, D], BF16).ap()

    ctx = contextlib.ExitStack()
    with ctx:
        sb = lambda *a: ctx.enter_context(nc.sbuf_tensor(*a))
        pt = lambda *a: ctx.enter_context(nc.psum_tensor(*a))
        arA = sb("arA", [128, 32 * TP], BF16)
        arB = sb("arB", [128, 32 * TP], BF16)
        wk = sb("wk", [128, 32 * 512], BF16)
        ku = [sb("ku0", [128, 32 * 512], U8)]
        slab1 = sb("slab1", [128, D], BF16)
        slab2 = sb("slab2", [128, D], BF16)
        asb = [sb(f"asb{i}", [128, 512], BF16) for i in range(2)]
        rsx = sb("rsx", [1, TP], F32)
        nego = sb("nego", [1, 512], F32)
        onec = sb("onec", [128, 1], BF16)
        sc4 = sb("sc4", [128, 4], F32)
        eisb = sb("eisb", [128, 32], F32)
        sums = sb("sums", [128, 1], F32)
        sqs = sb("sqs", [128, 1], F32)
        mu = sb("mu", [128, 1], F32)
        var = sb("var", [128, 1], F32)
        rr = sb("rr", [128, 1], F32)
        epst = sb("epstC", [128, 1], F32)
        s2x = sb("s2x", [128, 4], F32)
        idf = sb("idfC", [128, 128], F32)
        idb = sb("idbC", [128, 128], BF16)
        bsb = sb("bsb", [128, 12], F32)
        bsb2 = sb("bsb2", [128, 5], F32)

        ps_a = [pt(f"psa{i}", [128, 512], F32) for i in range(2)]
        ps_r = pt("psr", [1, TP], F32)
        ps_t = pt("pstC", [128, 512], BF16)

        s = make_sems(ctx, nc, [
            "ldo", "ldw", "ldx", "lds", "ldk", "lda", "ldz", "ldx1",
            "mm", "ev", "sq", "sts", "sts0", "sts1", "trp", "tre", "rsm", "rse",
            "stx", "stq", "vch", "sch", "gch"])

        S_ = Sched()

        class StopSched(Exception):
            pass

        S_.i("g", lambda e: e.iota(idf[:], pattern=[[1, 128]], base=0,
                                   channel_multiplier=-1,
                                   allow_small_or_imprecise_dtypes=True),
             [(s["gch"], 1)])
        S_.w("v", s["gch"], 1)
        S_.edge("v", lambda e: e.tensor_scalar(idf[:], idf[:], 0.0, None,
                                               ALU.is_equal), s["vch"])
        S_.edge("v", lambda e: e.tensor_copy(idb[:], idf[:]), s["vch"])
        S_.edge("v", lambda e: e.memset(epst[:], LN_EPS), s["vch"])
        S_.edge("v", lambda e: e.memset(nego[:], -1.0), s["vch"])
        S_.edge("v", lambda e: e.memset(onec[:], 1.0), s["vch"])
        S_.i("sy", lambda e: e.dma_start(out=sc4[:], in_=s0b[:, :]), [(s["lds"], 16)])
        S_.i("sy", lambda e: e.dma_start(out=eisb[:], in_=eic[:, :]), [(s["lds"], 16)])
        S_.i("sy", lambda e: e.dma_start(out=bsb[:, 0:12], in_=bqc[:, :]),
             [(s["lds"], 16)])
        S_.i("sy", lambda e: e.dma_start(out=bsb2[:, 0:5], in_=bkvc[:, :]),
             [(s["lds"], 16)])
        S_.w("v", s["lds"], 64)
        S_.edge("v", lambda e: e.tensor_scalar(s2x[:, 0:2], sc4[:, 0:2], 2.0,
                                               None, ALU.mult), s["vch"])
        S_.edge("v", lambda e: e.tensor_copy(s2x[:, 2:3], sc4[:, 2:3]), s["vch"])
        S_.i("g", lambda e: e.dma_start(
            out=arA[:], in_=ofm.rearrange("(c p) t -> p c t", p=128)),
            [(s["ldo"], 16)])

        try:
            for ech in range(8):
                S_.w("g", s["mm"], ech * NT)
                S_.i("g", lambda e, ech=ech: e.dma_start(
                    out=wk[:], in_=wot.rearrange("(c p) e -> p c e", p=128)
                    [:, :, 512 * ech:512 * (ech + 1)]), [(s["ldw"], 16)])
                for tch in range(NT):
                    i0 = ech * NT + tch
                    S_.w("te", s["ldo"], 16)
                    S_.w("te", s["ldw"], 16 * (ech + 1))
                    S_.w("te", s["ev"], max(0, i0 - 1))
                    for dd in range(32):
                        S_.i("te", lambda e, dd=dd, tch=tch, i0=i0: e.matmul(
                            ps_a[i0 % 2][:],
                            arA[:, dd * TP + 128 * tch:dd * TP + 128 * (tch + 1)],
                            wk[:, 512 * dd:512 * (dd + 1)],
                            start=(dd == 0), stop=(dd == 31)),
                            [(s["mm"], 1)] if dd == 31 else ())
                    S_.w("v", s["mm"], s["mm"].n)
                    S_.w("v", s["sts%d" % (i0 % 2)].h if False else s["sts%d" % (i0 % 2)],
                         s["sts%d" % (i0 % 2)].n)
                    S_.i("v", lambda e, i0=i0: e.tensor_copy(asb[i0 % 2][:],
                                                             ps_a[i0 % 2][:]),
                         [(s["ev"], 1)])
                    S_.w("sy", s["ev"], s["ev"].n)
                    S_.i("sy", lambda e, ech=ech, tch=tch, i0=i0: e.dma_start(
                        out=att[128 * tch:128 * (tch + 1), 512 * ech:512 * (ech + 1)],
                        in_=asb[i0 % 2][:]), [(s["sts%d" % (i0 % 2)], 16)])

            NSTS0, NSTS1 = s["sts0"].n, s["sts1"].n
            prev_vx1 = 0
            for tch in range(NT):
                S_.w("sy", s["sts0"], NSTS0)
                S_.w("sy", s["sts1"], NSTS1)
                S_.w("sy", s["vch"], prev_vx1)
                S_.i("sy", lambda e, tch=tch: e.dma_start(
                    out=slab1[:], in_=att[128 * tch:128 * (tch + 1), :]),
                    [(s["lda"], 16)])
                S_.w("g", s["stx"], 16 * tch)
                S_.w("g", s["trp"], 8 * tch)
                S_.w("g", s["vch"], prev_vx1)
                S_.i("g", lambda e, tch=tch: e.dma_start(
                    out=slab2[:], in_=xtm[128 * tch:128 * (tch + 1), :]),
                    [(s["ldx"], 16)])
                S_.w("v", s["lda"], 16 * (tch + 1))
                S_.w("v", s["ldx"], 16 * (tch + 1))
                S_.edge("v", lambda e: e.tensor_tensor(slab1[:], slab1[:], slab2[:],
                                                       ALU.add), s["vch"])
                S_.edge("v", lambda e: e.reduce_sum(sums[:], slab1[:], axis=AX.X),
                        s["vch"])
                VCHQ = s["vch"].n
                S_.w("sc", s["vch"], VCHQ)
                S_.i("sc", lambda e: e.activation(slab2[:], slab1[:], AF.Square),
                     [(s["sq"], 1)])
                S_.w("v", s["sq"], s["sq"].n)
                S_.edge("v", lambda e: e.reduce_sum(sqs[:], slab2[:], axis=AX.X),
                        s["vch"])
                S_.edge("v", lambda e: e.tensor_scalar(mu[:], sums[:], 1.0 / D, None,
                                                       ALU.mult), s["vch"])
                S_.edge("v", lambda e: e.tensor_tensor(var[:], mu[:], mu[:],
                                                       ALU.mult), s["vch"])
                S_.edge("v", lambda e: e.scalar_tensor_tensor(
                    var[:], sqs[:], 1.0 / D, var[:], ALU.mult, ALU.subtract),
                    s["vch"])
                VCHV = s["vch"].n
                S_.w("sc", s["vch"], VCHV)
                S_.edge("sc", lambda e: e.activation(var[:], var[:], AF.Ln,
                                                     bias=epst[:, 0:1]), s["sch"])
                S_.i("sc", lambda e: e.activation(rr[:], var[:], AF.Exp, scale=-0.5),
                     [(s["sq"], 1)])
                S_.w("v", s["sq"], s["sq"].n)
                S_.edge("v", lambda e: e.tensor_scalar(slab2[:], slab1[:], mu[:, 0:1],
                                                       rr[:, 0:1], ALU.subtract,
                                                       ALU.mult), s["vch"])
                prev_vx1 = s["vch"].n
                S_.w("sy", s["vch"], s["vch"].n)
                S_.i("sy", lambda e, tch=tch: e.dma_start(
                    out=x1d[128 * tch:128 * (tch + 1), :], in_=slab2[:]),
                    [(s["stx"], 16)])
                for j4 in range(8):
                    S_.w("te", s["vch"], s["vch"].n)
                    S_.w("te", s["tre"], s["tre"].n)
                    for j in range(4):
                        S_.i("te", lambda e, j4=j4, j=j: e.matmul(
                            ps_t[:, 128 * j:128 * (j + 1)],
                            slab2[:, 512 * j4 + 128 * j:512 * j4 + 128 * (j + 1)],
                            idb[:], is_transpose=True, start=(j == 0), stop=(j == 3)),
                            [(s["trp"], 1)] if j == 3 else ())
                    S_.w("sc", s["trp"], s["trp"].n)
                    dst = arB[:, :].rearrange("p (c t) -> p c t", c=32)[
                        :, 4 * j4:4 * (j4 + 1), 128 * tch:128 * (tch + 1)]
                    S_.i("sc", lambda e, dst=dst: e.copy(
                        dst, ps_t[:].rearrange("p (c t) -> p c t", c=4)),
                        [(s["tre"], 1)])

            if DBG:
                S_.w("sy", s["tre"], s["tre"].n)
                S_.i("sy", lambda e: e.dma_start(out=dbga[:, :], in_=arB[:]),
                     [(s["stq"], 16)])
            def rowsums(src_ar, stg, only=0):
                S_.w("te", s["ev"], s["ev"].n)
                S_.w("te", s["tre"], s["tre"].n)
                if only == 1:
                    return
                for th in range(TP // 512):
                    S_.w("te", s["rse"], s["rse"].n)
                    for dd in range(32):
                        S_.i("te", lambda e, dd=dd, th=th: e.matmul(
                            ps_r[:, 512 * th:512 * (th + 1)],
                            onec[:],
                            src_ar[:, dd * TP + 512 * th:dd * TP + 512 * (th + 1)],
                            start=(dd == 0), stop=(dd == 31)),
                            [(s["rsm"], 1)] if dd == 31 else ())
                    if only == 2:
                        continue
                    S_.w("v", s["rsm"], s["rsm"].n)
                    S_.i("v", lambda e, th=th: e.tensor_copy(
                        rsx[:, 512 * th:512 * (th + 1)],
                        ps_r[:, 512 * th:512 * (th + 1)]), [(s["rse"], 1)])

            for stg in range(2):
                if PH < stg + 1:
                    break
                src_ar = arB if stg == 0 else arA
                dst_ar = arA if stg == 0 else arB
                kt = kts[stg]
                rowsums(src_ar if PH != 16 else arB, stg,
                        only=(1 if PH == 17 else 2 if PH == 18 else 0))
                RSE = s["rse"].n
                if PH in (15, 16, 17, 18) and stg == 1:
                    raise StopSched
                if DBG and stg == 0:
                    S_.w("sy", s["rse"], RSE)
                    S_.i("sy", lambda e: e.dma_start(out=dbgr[:, :], in_=rsx[:, 0:TP]),
                         [(s["stq"], 16)])
                    S_.w("sy", s["sq"], s["sq"].n + 1)
                    S_.i("sy", lambda e: e.dma_start(out=dbgw[:, :], in_=wk[:]),
                         [(s["stq"], 16)])
                    S_.i("sy", lambda e: e.dma_start(out=dbgsx[:, :], in_=sc4[:]),
                         [(s["stq"], 16)])
                for ng in range(8):
                    li = stg * 8 + ng
                    S_.w("g", s["mm"], s["mm"].n)
                    S_.w("g", s["sq"], s["sq"].n)
                    S_.i("g", lambda e, kt=kt, ng=ng: e.dma_start(
                        out=ku[0][:], in_=kt[ng, :, :, :]), [(s["ldk"], 16)])
                    S_.w("sc", s["ldk"], 16 * (li + 1))
                    S_.w("sc", s["mm"], s["mm"].n)
                    S_.i("sc", lambda e, ng=ng: e.activation(
                        wk[:], ku[0][:], AF.Copy, scale=2.0), [(s["sq"], 1)])
                    for nq in range(4):
                        nch = 4 * ng + nq
                        for th in range(TP // 512):
                            i0 = (stg * 8 + ng) * 4 * (TP // 512) + nq * (TP // 512) + th
                            S_.w("te", s["sq"], s["sq"].n)
                            S_.w("te", s["rse"], RSE)
                            S_.w("te", s["tre"], s["tre"].n)
                            S_.w("te", s["ev"], s["ev"].n)
                            PSI = (s["mm"].n) % 2
                            for dd in range(32):
                                S_.i("te", lambda e, dd=dd, nq=nq, th=th, PSI=PSI,
                                     src_ar=src_ar: e.matmul(
                                    ps_a[PSI][:],
                                    wk[:, 512 * dd + 128 * nq:512 * dd + 128 * (nq + 1)],
                                    src_ar[:, dd * TP + 512 * th:dd * TP + 512 * (th + 1)],
                                    start=(dd == 0), stop=False))
                            S_.i("te", lambda e, th=th, PSI=PSI: e.matmul(
                                ps_a[PSI][:], nego[0:1, 0:128],
                                rsx[0:1, 512 * th:512 * (th + 1)],
                                start=False, stop=True), [(s["mm"], 1)])
                            S_.w("sc", s["mm"], s["mm"].n)
                            S_.w("sc", s["ev"], s["ev"].n)
                            S_.i("sc", lambda e, PSI=PSI, stg=stg: e.activation(
                                asb[PSI][:], ps_a[PSI][:], AF.Relu,
                                scale=s2x[:, stg:stg + 1]), [(s["sq"], 1)])
                            S_.w("v", s["sq"], s["sq"].n)
                            dst = dst_ar[:, nch * TP + 512 * th:nch * TP + 512 * (th + 1)]
                            S_.i("v", lambda e, dst=dst, PSI=PSI, nch=nch: e.tensor_scalar(
                                dst, asb[PSI][:], eisb[:, nch:nch + 1], None,
                                ALU.mult), [(s["ev"], 1)])
                            if DBG:
                                S_.w("sy", s["ev"], s["ev"].n)
                                S_.i("sy", lambda e, stg=stg, nch=nch, th=th, dst=dst: e.dma_start(
                                    out=dbgs[stg][:, nch * TP + 512 * th:nch * TP + 512 * (th + 1)],
                                    in_=dst), [(s["stq"], 16)])
                                if stg == 0:
                                    S_.i("sy", lambda e, nch=nch, th=th, PSI=PSI: e.dma_start(
                                        out=dbgb[:, nch * TP + 512 * th:nch * TP + 512 * (th + 1)],
                                        in_=asb[PSI][:]), [(s["stq"], 16)])

            if PH < 3:
                raise StopSched
            rowsums(arB, 2)
            RSE = s["rse"].n
            for ech in range(8):
                li = 16 + ech
                S_.w("g", s["mm"], s["mm"].n)
                S_.w("g", s["sq"], s["sq"].n)
                S_.i("g", lambda e, ech=ech: e.dma_start(
                    out=ku[0][:], in_=kts[2][ech, :, :, :]), [(s["ldk"], 16)])
                S_.w("sc", s["ldk"], 16 * (li + 1))
                S_.w("sc", s["mm"], s["mm"].n)
                S_.i("sc", lambda e, ech=ech: e.activation(
                	wk[:], ku[0][:], AF.Copy, scale=2.0), [(s["sq"], 1)])
                for tch in range(NT):
                    i0 = ech * NT + tch
                    S_.w("te", s["sq"], s["sq"].n)
                    S_.w("te", s["rse"], RSE)
                    S_.w("te", s["tre"], s["tre"].n)
                    S_.w("te", s["ev"], s["ev"].n)
                    PSI = s["mm"].n % 2
                    for dd in range(32):
                        S_.i("te", lambda e, dd=dd, tch=tch, PSI=PSI: e.matmul(
                            ps_a[PSI][:],
                            arB[:, dd * TP + 128 * tch:dd * TP + 128 * (tch + 1)],
                            wk[:, 512 * dd:512 * (dd + 1)],
                            start=(dd == 0), stop=False))
                    S_.i("te", lambda e, tch=tch, PSI=PSI: e.matmul(
                        ps_a[PSI][:], rsx[0:1, 128 * tch:128 * (tch + 1)],
                        nego[0:1, 0:512], start=False, stop=True), [(s["mm"], 1)])
                    S_.w("v", s["mm"], s["mm"].n)
                    S_.w("v", s["sts"], s["sts"].n)
                    S_.i("v", lambda e, PSI=PSI: e.tensor_scalar(
                        asb[PSI][:], ps_a[PSI][:], s2x[:, 2:3], None, ALU.mult),
                        [(s["ev"], 1)])
                    S_.w("sy", s["ev"], s["ev"].n)
                    S_.i("sy", lambda e, ech=ech, tch=tch, PSI=PSI: e.dma_start(
                        out=z2d[128 * tch:128 * (tch + 1), 512 * ech:512 * (ech + 1)],
                        in_=asb[PSI][:]), [(s["sts"], 16)])

            if PH < 4:
                raise StopSched
            NSTS2 = s["sts"].n
            prev_vx2 = 0
            for tch in range(NT):
                S_.w("sy", s["sts"], NSTS2)
                S_.w("sy", s["vch"], prev_vx2)
                S_.i("sy", lambda e, tch=tch: e.dma_start(
                    out=slab1[:], in_=z2d[128 * tch:128 * (tch + 1), :]),
                    [(s["ldz"], 16)])
                S_.w("g", s["stx"], s["stx"].n)
                S_.w("g", s["trp"], s["trp"].n)
                S_.w("g", s["tre"], s["tre"].n)
                S_.i("g", lambda e, tch=tch: e.dma_start(
                    out=slab2[:], in_=x1d[128 * tch:128 * (tch + 1), :]),
                    [(s["ldx1"], 16)])
                S_.w("v", s["ldz"], s["ldz"].n)
                S_.w("v", s["ldx1"], s["ldx1"].n)
                S_.edge("v", lambda e: e.tensor_tensor(slab1[:], slab1[:], slab2[:],
                                                       ALU.add), s["vch"])
                S_.edge("v", lambda e: e.reduce_sum(sums[:], slab1[:], axis=AX.X),
                        s["vch"])
                S_.w("sc", s["vch"], s["vch"].n)
                S_.i("sc", lambda e: e.activation(slab2[:], slab1[:], AF.Square),
                     [(s["sq"], 1)])
                S_.w("v", s["sq"], s["sq"].n)
                S_.edge("v", lambda e: e.reduce_sum(sqs[:], slab2[:], axis=AX.X),
                        s["vch"])
                S_.edge("v", lambda e: e.tensor_scalar(mu[:], sums[:], 1.0 / D, None,
                                                       ALU.mult), s["vch"])
                S_.edge("v", lambda e: e.tensor_tensor(var[:], mu[:], mu[:],
                                                       ALU.mult), s["vch"])
                S_.edge("v", lambda e: e.scalar_tensor_tensor(
                    var[:], sqs[:], 1.0 / D, var[:], ALU.mult, ALU.subtract),
                    s["vch"])
                S_.w("sc", s["vch"], s["vch"].n)
                S_.edge("sc", lambda e: e.activation(var[:], var[:], AF.Ln,
                                                     bias=epst[:, 0:1]), s["sch"])
                S_.i("sc", lambda e: e.activation(rr[:], var[:], AF.Exp, scale=-0.5),
                     [(s["sq"], 1)])
                S_.w("v", s["sq"], s["sq"].n)
                S_.edge("v", lambda e: e.tensor_scalar(slab2[:], slab1[:], mu[:, 0:1],
                                                       rr[:, 0:1], ALU.subtract,
                                                       ALU.mult), s["vch"])
                prev_vx2 = s["vch"].n
                S_.w("sy", s["vch"], s["vch"].n)
                S_.i("sy", lambda e, tch=tch: e.dma_start(
                    out=x2tm[128 * tch:128 * (tch + 1), :], in_=slab2[:]),
                    [(s["stx"], 16)])
                for j4 in range(8):
                    S_.w("te", s["vch"], s["vch"].n)
                    S_.w("te", s["tre"], s["tre"].n)
                    for j in range(4):
                        S_.i("te", lambda e, j4=j4, j=j: e.matmul(
                            ps_t[:, 128 * j:128 * (j + 1)],
                            slab2[:, 512 * j4 + 128 * j:512 * j4 + 128 * (j + 1)],
                            idb[:], is_transpose=True, start=(j == 0), stop=(j == 3)),
                            [(s["trp"], 1)] if j == 3 else ())
                    S_.w("sc", s["trp"], s["trp"].n)
                    dst = arA[:, :].rearrange("p (c t) -> p c t", c=32)[
                        :, 4 * j4:4 * (j4 + 1), 128 * tch:128 * (tch + 1)]
                    S_.i("sc", lambda e, dst=dst: e.copy(
                        dst, ps_t[:].rearrange("p (c t) -> p c t", c=4)),
                        [(s["tre"], 1)])

            if PH < 5:
                raise StopSched
            for proj, wsrc, bias, nchk, outd in ((0, wqct, bsb, 12, qc),
                                                 (1, wkvct, bsb2, 5, kvc)):
                for ec in range(nchk):
                    mrows = 128 if not (proj == 1 and ec == 4) else 64
                    S_.w("g", s["mm"], s["mm"].n)
                    S_.i("g", lambda e, wsrc=wsrc, ec=ec, mrows=mrows: e.dma_start(
                        out=wk[:, 0:32 * mrows],
                        in_=wsrc.rearrange("(c p) q -> p c q", p=128)
                        [:, :, 128 * ec:128 * ec + mrows]), [(s["ldw"], 16)])
                    for th in range(TP // 512):
                        S_.w("te", s["ldw"], 16 * (8 + ec + 1 + (12 if proj else 0)))
                        S_.w("te", s["tre"], s["tre"].n)
                        S_.w("te", s["ev"], s["ev"].n)
                        PSI = s["mm"].n % 2
                        for dd in range(32):
                            S_.i("te", lambda e, dd=dd, th=th, PSI=PSI, mrows=mrows: e.matmul(
                                ps_a[PSI][0:mrows, :],
                                wk[:, mrows * dd:mrows * (dd + 1)],
                                arA[:, dd * TP + 512 * th:dd * TP + 512 * (th + 1)],
                                start=(dd == 0), stop=(dd == 31)),
                                [(s["mm"], 1)] if dd == 31 else ())
                        S_.w("v", s["mm"], s["mm"].n)
                        S_.w("v", s["sts"], s["sts"].n)
                        S_.i("v", lambda e, PSI=PSI, bias=bias, ec=ec, mrows=mrows: e.tensor_scalar(
                            asb[PSI][0:mrows, :], ps_a[PSI][0:mrows, :],
                            bias[0:mrows, ec:ec + 1], None, ALU.add), [(s["ev"], 1)])
                        S_.w("sy", s["ev"], s["ev"].n)
                        S_.i("sy", lambda e, outd=outd, ec=ec, th=th, PSI=PSI, mrows=mrows: e.dma_start(
                            out=outd[128 * ec:128 * ec + mrows, 512 * th:512 * (th + 1)],
                            in_=asb[PSI][0:mrows, :]), [(s["sts"], 16)])

        except StopSched:
            pass
        if DBG:
            S_.w("sy", s["ev"], s["ev"].n)
            S_.w("sy", s["tre"], s["tre"].n)
            S_.w("sy", s["mm"], s["mm"].n)
            S_.i("sy", lambda e: e.dma_start(out=dbga[:, :], in_=arA[:]),
                 [(s["stq"], 16)])
        S_.w("sy", s["sts"], s["sts"].n)
        S_.w("sy", s["stx"], s["stx"].n)
        S_.w("sy", s["stq"], s["stq"].n)

        blk = ctx.enter_context(nc.Block())

        @blk.gpsimd
        def _(e):
            S_.replay("g", e)

        @blk.sync
        def _(e):
            S_.replay("sy", e)

        @blk.tensor
        def _(e):
            S_.replay("te", e)

        @blk.vector
        def _(e):
            S_.replay("v", e)

        @blk.scalar
        def _(e):
            S_.replay("sc", e)

    return nc



def build_D(SD=S):
    nc = _bass()
    TB = B * SD
    NQT = SD // 128
    NKT = SD // 512
    NTH = SD // 512

    qcb = nc.dram_tensor("qcb", [QL, TB], BF16, kind="ExternalInput").ap()
    kvcb = nc.dram_tensor("kvcb", [576, TB], BF16, kind="ExternalInput").ap()
    wqupt = nc.dram_tensor("wqupt", [QL, 768], F32, kind="ExternalInput").ap()
    bqup = nc.dram_tensor("bqup", [128, 8], F32, kind="ExternalInput").ap()
    wkvupt = nc.dram_tensor("wkvupt", [KVL, 1024], F32, kind="ExternalInput").ap()
    bkvup = nc.dram_tensor("bkvup", [128, 8], F32, kind="ExternalInput").ap()
    op = nc.dram_tensor("op", [B, SD, HD], F32, kind="ExternalOutput").ap()

    ctx = contextlib.ExitStack()
    with ctx:
        sb = lambda *a: ctx.enter_context(nc.sbuf_tensor(*a))
        pt = lambda *a: ctx.enter_context(nc.psum_tensor(*a))
        qah = sb("qah", [128, 2 * SD], BF16)
        kvh = sb("kvh", [128, 2 * SD], BF16)
        krop = sb("krop", [64, SD], BF16)
        vtm = sb("vtmD", [128, SD], BF16)
        wq = sb("wqD", [128, 12 * 768], BF16)
        wkv = sb("wkvD", [128, 4 * 1024], BF16)
        bqs = sb("bqsD", [128, 8], F32)
        bkvs = sb("bkvsD", [128, 8], F32)
        qcs = sb("qcsD", [128, 12 * 512], BF16)
        idf = sb("idfD", [128, 128], F32)
        idb = sb("idbD", [128, 128], BF16)
        psb = [sb(f"psbD{i}", [128, 512], BF16) for i in range(2)]
        ptsb = [sb(f"ptsbD{i}", [128, 512], BF16) for i in range(2)]
        osum = sb("osumD", [128, SD], F32)
        oacc = sb("oaccD", [128, HD], F32)
        mrun = sb("mrunD", [128, 1], F32)
        mnew = sb("mnewD", [128, 1], F32)
        psmax = sb("psmaxD", [128, 1], F32)
        biast = sb("biastD", [128, 1], F32)
        corrs = sb("corrsD", [128, 4], F32)
        mdif = sb("mdifD", [128, 1], F32)
        tsum = sb("tsumD", [128, 1], F32)
        dnr = sb("dnrD", [128, 1], F32)
        rcp = sb("rcpD", [128, 4], F32)

        ps_p = [pt(f"pspD{i}", [128, 512], F32) for i in range(2)]
        ps_s = [pt(f"pssD{i}", [128, 512], F32) for i in range(2)]
        ps_pt = [pt(f"psptD{i}", [128, 512], BF16) for i in range(2)]
        ps_o = [pt(f"psoD{i}", [128, HD], F32) for i in range(2)]
        ring = [ps_s[0], ps_s[1], ps_p[0], ps_p[1]]

        s = make_sems(ctx, nc, [
            "ldw", "ldk", "ldb", "ldq", "pjm", "pev", "scm", "sadd", "scorr",
            "sexp", "sdn", "strp", "spte", "sres", "sav", "osu", "ost",
            "vtp", "vte", "vch", "sch", "idr", "gch"])

        S_ = Sched()

        S_.i("g", lambda e: e.iota(idf[:], pattern=[[1, 128]], base=0,
                                   channel_multiplier=-1,
                                   allow_small_or_imprecise_dtypes=True),
             [(s["gch"], 1)])
        S_.i("g", lambda e: e.dma_start(
            out=wq[:], in_=wqupt.rearrange("(c p) q -> p c q", p=128)),
            [(s["ldw"], 16)])
        S_.i("g", lambda e: e.dma_start(
            out=wkv[:], in_=wkvupt.rearrange("(c p) q -> p c q", p=128)),
            [(s["ldw"], 16)])
        S_.i("sy", lambda e: e.dma_start(out=bqs[:], in_=bqup[:, :]),
             [(s["ldb"], 16)])
        S_.i("sy", lambda e: e.dma_start(out=bkvs[:], in_=bkvup[:, :]),
             [(s["ldb"], 16)])
        S_.w("v", s["gch"], 1)
        S_.edge("v", lambda e: e.tensor_scalar(idf[:], idf[:], 0.0, None,
                                               ALU.is_equal), s["vch"])
        S_.i("v", lambda e: e.tensor_copy(idb[:], idf[:]), [(s["idr"], 1)])

        nload = 0
        gT = [0]
        nvt = [0]
        for b in range(B):
            if b > 0:
                S_.w("g", s["scm"], s["scm"].n)
                S_.w("v", s["ost"], 16 * b)
            S_.i("g", lambda e, b=b: e.dma_start(
                out=krop[:], in_=kvcb[KVL:576, SD * b:SD * (b + 1)]),
                [(s["ldk"], 16)])
            for h in range(HPC):
                S_.w("te", s["sexp"], gT[0])
                for which, nkch, wsb_, bias_, dst_ar in (
                        (0, 12, wq, bqs, qah), (1, 4, wkv, bkvs, kvh)):
                    for th in range(NTH):
                        S_.w("sy", s["pjm"], s["pjm"].n)
                        S_.w("sy", s["ldb"], 32)
                        if which == 0:
                            src = qcb.rearrange("(c p) t -> p c t", p=128)[
                                :, :, SD * b + 512 * th:SD * b + 512 * (th + 1)]
                            dwid = 12
                        else:
                            src = kvcb[0:KVL].rearrange("(c p) t -> p c t", p=128)[
                                :, :, SD * b + 512 * th:SD * b + 512 * (th + 1)]
                            dwid = 4
                        S_.i("sy", lambda e, src=src, dwid=dwid: e.dma_start(
                            out=qcs[:, 0:dwid * 512].rearrange(
                                "p (c x) -> p c x", x=512), in_=src),
                            [(s["ldq"], 16)])
                        nload += 1
                        for sub in range(2):
                            if which == 0:
                                mof = 192 * h + 128 * sub
                                mw = 128 if sub == 0 else 64
                            else:
                                mof = 256 * h + 128 * sub
                                mw = 128
                            ppi = s["pjm"].n % 2
                            S_.w("te", s["ldw"], 32)
                            S_.w("te", s["ldq"], 16 * nload)
                            S_.w("te", s["pev"], max(0, s["pjm"].n - 1))
                            if th == 0 and sub == 0:
                                S_.w("te", s["scm"], s["scm"].n)
                            for kc in range(nkch):
                                S_.i("te", lambda e, wsb_=wsb_, kc=kc, mof=mof,
                                     mw=mw, ppi=ppi, which=which,
                                     nkch=nkch: e.matmul(
                                    ps_p[ppi][0:mw, :],
                                    wsb_[:, kc * (768 if which == 0 else 1024) + mof:
                                         kc * (768 if which == 0 else 1024) + mof + mw],
                                    qcs[:, kc * 512:(kc + 1) * 512],
                                    start=(kc == 0), stop=(kc == nkch - 1)),
                                    [(s["pjm"], 1)] if kc == nkch - 1 else ())
                            bcol = 2 * h + sub
                            S_.w("v", s["pjm"], s["pjm"].n)
                            if th == 0 and sub == 0 and (b > 0 or h > 0):
                                pass
                            S_.i("v", lambda e, dst_ar=dst_ar, mw=mw, th=th,
                                 ppi=ppi, bias_=bias_, bcol=bcol, sub=sub,
                                 SD=SD: e.tensor_scalar(
                                dst_ar[0:mw, sub * SD + 512 * th:
                                       sub * SD + 512 * (th + 1)],
                                ps_p[ppi][0:mw, :], bias_[0:mw, bcol:bcol + 1],
                                None, ALU.add), [(s["pev"], 1)])
                for gb in range(SD // 512):
                    S_.w("te", s["pev"], s["pev"].n)
                    S_.w("te", s["idr"], 1)
                    S_.w("te", s["spte"], gT[0])
                    S_.w("te", s["vte"], max(0, nvt[0] - 1))
                    pb = ps_pt[gb % 2]
                    for j in range(4):
                        S_.i("te", lambda e, gb=gb, j=j, pb=pb: e.matmul(
                            pb[:, 128 * j:128 * (j + 1)],
                            kvh[:, SD + 512 * gb + 128 * j:SD + 512 * gb + 128 * (j + 1)],
                            idb[:], is_transpose=True,
                            start=(j == 0), stop=(j == 3)),
                            [(s["vtp"], 1)] if j == 3 else ())
                    S_.w("sc", s["vtp"], s["vtp"].n)
                    S_.w("sc", s["sav"], gT[0])
                    S_.i("sc", lambda e, gb=gb, pb=pb: e.copy(
                        vtm[:, 512 * gb:512 * (gb + 1)], pb[:]), [(s["vte"], 1)])
                    nvt[0] += 1
                ntile = NQT * NKT
                gt0 = gT[0]
                for u in range(ntile + 2):
                    t = gt0 + u
                    if u < ntile:
                        qt, kt = u // NKT, u % NKT
                        rb = ring[t % 4]
                        S_.w("te", s["ldk"], 16 * (b + 1))
                        S_.w("te", s["sexp"], max(0, t - 3))
                        if u < 4:
                            S_.w("te", s["pev"], s["pev"].n)
                        S_.i("te", lambda e, rb=rb, qt=qt, kt=kt, SD=SD: e.matmul(
                            rb[:], qah[:, 128 * qt:128 * (qt + 1)],
                            kvh[:, 512 * kt:512 * (kt + 1)],
                            start=True, stop=False))
                        S_.i("te", lambda e, rb=rb, qt=qt, kt=kt, SD=SD: e.matmul(
                            rb[:], qah[0:64, SD + 128 * qt:SD + 128 * (qt + 1)],
                            krop[0:64, 512 * kt:512 * (kt + 1)],
                            start=False, stop=True), [(s["scm"], 1)])
                        if kt == 0:
                            S_.edge("v", lambda e: e.memset(mrun[:], -1e30),
                                    s["vch"])
                            S_.edge("v", lambda e: e.memset(dnr[:], 0.0),
                                    s["vch"])
                        S_.w("v", s["scm"], t + 1)
                        S_.edge("v", lambda e, rb=rb: e.reduce_max(
                            psmax[:], rb[:], axis=AX.X), s["vch"])
                        S_.edge("v", lambda e: e.tensor_scalar(
                            mnew[:], psmax[:], SC1, None, ALU.mult), s["vch"])
                        S_.edge("v", lambda e: e.tensor_tensor(
                            mnew[:], mrun[:], mnew[:], ALU.max), s["vch"])
                        S_.edge("v", lambda e: e.tensor_tensor(
                            mdif[:], mrun[:], mnew[:], ALU.subtract), s["vch"])
                        S_.w("v", s["sexp"], t)
                        S_.edge("v", lambda e: e.tensor_scalar(
                            biast[:], mnew[:], -1.0, None, ALU.mult), s["vch"])
                        S_.i("v", lambda e: e.tensor_copy(mrun[:], mnew[:]),
                             [(s["sadd"], 1)])
                        cc = t % 4
                        S_.w("sc", s["sadd"], t + 1)
                        S_.w("sc", s["sres"], max(0, t - 3))
                        S_.i("sc", lambda e, cc=cc: e.activation(
                            corrs[:, cc:cc + 1], mdif[:], AF.Exp),
                            [(s["scorr"], 1)])
                        S_.w("sc", s["strp"], max(0, t - 1))
                        S_.w("sc", s["sdn"], t)
                        S_.i("sc", lambda e, rb=rb, t=t: e.activation(
                            psb[t % 2][:], rb[:], AF.Exp, bias=biast[:, 0:1],
                            scale=SC1, accum_out=tsum[:, 0:1]), [(s["sexp"], 1)])
                        S_.w("v", s["sexp"], t + 1)
                        S_.w("v", s["scorr"], t + 1)
                        S_.i("v", lambda e, cc=cc: e.scalar_tensor_tensor(
                            dnr[:], dnr[:], corrs[:, cc:cc + 1], tsum[:],
                            ALU.mult, ALU.add), [(s["sdn"], 1)])
                        if kt == NKT - 1:
                            S_.w("v", s["sdn"], t + 1)
                            S_.edge("v", lambda e, qc4=qt % 4: e.reciprocal(
                                rcp[:, qc4:qc4 + 1], dnr[:]), s["vch"])
                    if 1 <= u <= ntile:
                        tm = gt0 + u - 1
                        S_.w("te", s["sexp"], tm + 1)
                        S_.w("te", s["spte"], max(0, tm - 1))
                        S_.w("te", s["idr"], 1)
                        if u - 1 < 2:
                            S_.w("te", s["vte"], nvt[0])
                        pb = ps_pt[tm % 2]
                        for j in range(4):
                            S_.i("te", lambda e, j=j, tm=tm, pb=pb: e.matmul(
                                pb[:, 128 * j:128 * (j + 1)],
                                psb[tm % 2][:, 128 * j:128 * (j + 1)], idb[:],
                                is_transpose=True, start=(j == 0), stop=(j == 3)),
                                [(s["strp"], 1)] if j == 3 else ())
                        S_.w("sc", s["strp"], tm + 1)
                        S_.w("sc", s["sav"], max(0, tm - 1))
                        S_.i("sc", lambda e, tm=tm, pb=pb: e.copy(
                            ptsb[tm % 2][:], pb[:]), [(s["spte"], 1)])
                    if u >= 2:
                        ta = gt0 + u - 2
                        qta, kta = (u - 2) // NKT, (u - 2) % NKT
                        S_.w("te", s["spte"], ta + 1)
                        S_.w("te", s["sres"], max(0, ta - 1))
                        if u - 2 < 2:
                            S_.w("te", s["vte"], nvt[0])
                        for j in range(4):
                            S_.i("te", lambda e, ta=ta, kta=kta, j=j: e.matmul(
                                ps_o[ta % 2][:],
                                ptsb[ta % 2][:, 128 * j:128 * (j + 1)],
                                vtm[:, (4 * kta + j) * 128:(4 * kta + j + 1) * 128],
                                start=(j == 0), stop=(j == 3)),
                                [(s["sav"], 1)] if j == 3 else ())
                        S_.w("v", s["sav"], ta + 1)
                        cc2 = ta % 4
                        if kta == 0:
                            S_.w("v", s["osu"], s["osu"].n)
                            S_.i("v", lambda e, ta=ta: e.tensor_copy(
                                oacc[:], ps_o[ta % 2][:]), [(s["sres"], 1)])
                        else:
                            S_.i("v", lambda e, ta=ta, cc2=cc2:
                                 e.scalar_tensor_tensor(
                                oacc[:], oacc[:], corrs[:, cc2:cc2 + 1],
                                ps_o[ta % 2][:], ALU.mult, ALU.add),
                                [(s["sres"], 1)])
                        if kta == NKT - 1:
                            S_.w("v", s["sres"], ta + 1)
                            qc4 = qta % 4
                            if h == 0:
                                S_.i("v", lambda e, qta=qta, qc4=qc4:
                                     e.tensor_scalar(
                                    osum[:, 128 * qta:128 * (qta + 1)], oacc[:],
                                    rcp[:, qc4:qc4 + 1], None, ALU.mult),
                                    [(s["osu"], 1)])
                            else:
                                S_.i("v", lambda e, qta=qta, qc4=qc4:
                                     e.scalar_tensor_tensor(
                                    osum[:, 128 * qta:128 * (qta + 1)], oacc[:],
                                    rcp[:, qc4:qc4 + 1],
                                    osum[:, 128 * qta:128 * (qta + 1)],
                                    ALU.mult, ALU.add), [(s["osu"], 1)])
                gT[0] += ntile
            S_.w("sy", s["osu"], s["osu"].n)
            S_.i("sy", lambda e, b=b: e.dma_start(
                out=op[b].rearrange("(c p) v -> p c v", p=128),
                in_=osum[:, :].rearrange("p (c v) -> p c v", v=HD)),
                [(s["ost"], 16)])
        S_.w("sy", s["ost"], 16 * B)

        blk = ctx.enter_context(nc.Block())

        @blk.gpsimd
        def _(e):
            S_.replay("g", e)

        @blk.sync
        def _(e):
            S_.replay("sy", e)

        @blk.tensor
        def _(e):
            S_.replay("te", e)

        @blk.vector
        def _(e):
            S_.replay("v", e)

        @blk.scalar
        def _(e):
            S_.replay("sc", e)

    return nc


def stage_B(inputs, ares):
    slopes = _alibi_slopes()
    stall = np.concatenate([ares[j]["stats"] for j in range(NCORES)], axis=0)
    stsel = np.zeros((32, 4), np.float32)
    for j in range(NCORES):
        for si in range(4):
            stsel[4 * j + si, si] = 1.0
    bsel = np.zeros((4, 4 * 128), np.float32)
    for si in range(4):
        bsel[si, si * 128:(si + 1) * 128] = 1.0
    qw = inputs["L0_attn_q_ln_w"].astype(np.float32)
    qb = inputs["L0_attn_q_ln_b"].astype(np.float32)
    kw = inputs["L0_attn_k_ln_w"].astype(np.float32)
    kb = inputs["L0_attn_k_ln_b"].astype(np.float32)
    in_maps = []
    for c in range(NCORES):
        heads = _core_heads(c)
        lnwb = np.zeros((128, 16), np.float32)
        for hl, h in enumerate(heads):
            lnwb[:, 4 * hl + 0] = qw[128 * h:128 * (h + 1)]
            lnwb[:, 4 * hl + 1] = qb[128 * h:128 * (h + 1)]
            lnwb[:, 4 * hl + 2] = kw[128 * h:128 * (h + 1)]
            lnwb[:, 4 * hl + 3] = kb[128 * h:128 * (h + 1)]
        slb = np.broadcast_to(slopes[heads][None, :] / SC0, (128, 4)).copy()
        slc = np.broadcast_to(slopes[heads][None, :], (128, 4)).copy()
        in_maps.append({"qkvc": ares[c]["qkvc"], "stall": stall, "stsel": stsel,
                        "bsel": bsel, "lnwb": lnwb, "slb": slb, "slc": slc})
    return in_maps


def stage_C(inputs, ares, bres):
    o_full = np.empty((T, D), np.float32)
    for j in range(NCORES):
        for hl, h in enumerate(_core_heads(j)):
            for b in range(B):
                o_full[b * S:(b + 1) * S, h * HD:(h + 1) * HD] = \
                    bres[j]["oo"][b * HPC + hl]
    x = inputs["L0_attn_hidden_states"].reshape(T, D).astype(np.float32, copy=False)
    wot = np.ascontiguousarray(inputs["L0_attn_Wout"].T)
    kts = []
    for nm in ("L0_ffn_k0", "L0_ffn_k1", "L0_ffn_k2"):
        k = inputs[nm].view(np.uint8) if inputs[nm].dtype == np.bool_ else inputs[nm].astype(np.uint8)
        kts.append(np.ascontiguousarray(
            k.reshape(32, 128, 8, 512).transpose(2, 1, 0, 3)))
    s0b = np.broadcast_to(np.stack([
        np.float32(inputs["L0_ffn_s0"]), np.float32(inputs["L0_ffn_s1"]),
        np.float32(inputs["L0_ffn_s2"]), np.float32(0.0)])[None, :], (128, 4)).copy()
    eic = np.where(np.arange(D) >= NE, np.float32(EI_NEG),
                   np.float32(1.0)).reshape(32, 128).T.copy()
    wqct = np.ascontiguousarray(inputs["L1_attn_Wq_c"].T)
    bqc = inputs["L1_attn_bq_c"].astype(np.float32).reshape(12, 128).T.copy()
    wkvct = np.ascontiguousarray(inputs["L1_attn_Wkv_c"].T)
    bkvc = np.pad(inputs["L1_attn_bkv_c"].astype(np.float32),
                  (0, 64)).reshape(5, 128).T.copy()
    in_maps = []
    for c in range(NCORES):
        tsl = slice(TPC * c, TPC * (c + 1))
        in_maps.append({
            "ofm": np.ascontiguousarray(o_full[tsl].T),
            "xtm": np.ascontiguousarray(x[tsl]),
            "wot": wot, "k0t": kts[0], "k1t": kts[1], "k2t": kts[2],
            "s0b": s0b, "eic": eic, "wqct": wqct, "bqc": bqc,
            "wkvct": wkvct, "bkvc": bkvc,
        })
    return in_maps


def stage_D(inputs, cres):
    qcb = np.concatenate([cres[c]["qc"] for c in range(NCORES)], 1)
    kvcb = np.concatenate([cres[c]["kvc"] for c in range(NCORES)], 1)
    wqupT = inputs["L1_attn_Wq_up"].T.astype(np.float32, copy=False)
    bqu = inputs["L1_attn_bq_up"].astype(np.float32, copy=False)
    wkvupT = inputs["L1_attn_Wkv_up"].T.astype(np.float32, copy=False)
    bkvu = inputs["L1_attn_bkv_up"].astype(np.float32, copy=False)
    in_maps = []
    for c in range(NCORES):
        bqup = np.zeros((128, 8), np.float32)
        bkvup = np.zeros((128, 8), np.float32)
        for hh in range(HPC):
            gh = HPC * c + hh
            bqup[:, 2 * hh] = bqu[192 * gh:192 * gh + 128]
            bqup[0:64, 2 * hh + 1] = bqu[192 * gh + 128:192 * gh + 192]
            bkvup[:, 2 * hh] = bkvu[256 * gh:256 * gh + 128]
            bkvup[:, 2 * hh + 1] = bkvu[256 * gh + 128:256 * gh + 256]
        in_maps.append({
            "qcb": qcb, "kvcb": kvcb,
            "wqupt": np.ascontiguousarray(wqupT[:, 768 * c:768 * (c + 1)]),
            "bqup": bqup,
            "wkvupt": np.ascontiguousarray(wkvupT[:, 1024 * c:1024 * (c + 1)]),
            "bkvup": bkvup,
        })
    return in_maps


def stage_E(inputs, ares, cres, dres):
    dqg, dqu, dqd = [], [], []
    for j in range(NCORES):
        a, b = core_inter_range(j)
        w = b - a
        dqg.append(ares[j]["dqg"][:, :w])
        dqu.append(ares[j]["dqu"][:, :w])
        dqd.append(ares[j]["dqd"][:w, :])
    DQG = np.concatenate(dqg, 1)
    DQU = np.concatenate(dqu, 1)
    DQD = np.concatenate(dqd, 0)
    wgt = np.ascontiguousarray(
        DQG.reshape(32, 128, NG, 128).transpose(2, 1, 0, 3)).reshape(
        NG, 128, 32 * 128)
    wut = np.ascontiguousarray(
        DQU.reshape(32, 128, NG, 128).transpose(2, 1, 0, 3)).reshape(
        NG, 128, 32 * 128)
    wdt = np.ascontiguousarray(
        DQD.reshape(NG, 128, 8, 512).transpose(0, 2, 1, 3))
    wo = np.ascontiguousarray(inputs["L1_attn_Wo"].T.astype(np.float32,
                                                            copy=False))
    bo = inputs["L1_attn_bo"].astype(np.float32, copy=False).reshape(1, D)
    in_maps = []
    for c in range(NCORES):
        b_, q_ = c // 4, c % 4
        ops = np.ascontiguousarray(np.stack(
            [dres[j]["op"].reshape(B, S, HD)[b_, TPC * q_:TPC * (q_ + 1), :].T
             for j in range(NCORES)]))
        in_maps.append({
            "ops": ops, "x2tm_in": cres[c]["x2tm"], "wo": wo, "bo": bo,
            "wgt": wgt, "wut": wut, "wdt": wdt,
        })
    return in_maps


def run_A(inputs):
    nc = build_A()
    in_maps = stage_A(inputs)
    res = run_bass_kernel_spmd(nc, in_maps, core_ids=list(range(NCORES)))
    return res.results



def kernel(**inputs):
    cores = list(range(NCORES))
    resA = run_bass_kernel_spmd(build_A(), stage_A(inputs), core_ids=cores)
    ares = [resA.results[c] for c in cores]
    resB = run_bass_kernel_spmd(build_B(), stage_B(inputs, ares),
                                core_ids=cores)
    bres = [resB.results[c] for c in cores]
    resC = run_bass_kernel_spmd(build_C(), stage_C(inputs, ares, bres),
                                core_ids=cores)
    cres = [resC.results[c] for c in cores]
    resD = run_bass_kernel_spmd(build_D(), stage_D(inputs, cres),
                                core_ids=cores)
    dres = [resD.results[c] for c in cores]
    resE = run_bass_kernel_spmd(build_E(), stage_E(inputs, ares, cres, dres),
                                core_ids=cores)
    out = np.concatenate(
        [np.asarray(resE.results[c]["out"], dtype=np.float32).reshape(TPC, D)
         for c in cores], 0)
    return out.reshape(B, S, D)
